# version 05
# baseline (speedup 1.0000x reference)
import numpy as np
from contextlib import ExitStack

import concourse.bass as bass
import concourse.mybir as mybir
from concourse.bass_utils import run_bass_kernel_spmd

F32 = mybir.dt.float32
BF16 = mybir.dt.bfloat16
AF = mybir.ActivationFunctionType
ALU = mybir.AluOpType
AX = mybir.AxisListType

T = 4096
D = 1024
NCH = 8
NT5 = 8
NT1 = 32
DFF = 4096
NHEAD = 16
HDIM = 64

ENGINES = ("tensor", "vector", "scalar", "sync", "gpsimd")
SEM_ROLLOVER = 48000


class Step:
    __slots__ = ("engine", "fn", "deps", "amount", "needs_inc", "sem", "val", "tag")

    def __init__(self, engine, fn, deps, amount, tag):
        self.engine = engine
        self.fn = fn
        self.deps = deps
        self.amount = amount
        self.needs_inc = False
        self.sem = None
        self.val = None
        self.tag = tag


class Res:
    __slots__ = ("writer", "readers", "name")

    def __init__(self, name=""):
        self.writer = None
        self.readers = []
        self.name = name


class Sched:
    def __init__(self, nc, es):
        self.nc = nc
        self.es = es
        self.steps = {e: [] for e in ENGINES}
        self.pending = {e: [] for e in ENGINES}

    def step(self, engine, fn, reads=(), writes=(), deps=(), dma=False, tag=""):
        d = list(deps) + self.pending[engine]
        self.pending[engine] = []
        for r in reads:
            if r.writer is not None:
                d.append(r.writer)
        for w in writes:
            d.extend(w.readers)
            if w.writer is not None:
                d.append(w.writer)
        st = Step(engine, fn, [t for t in d if t is not None], 16 if dma else 1, tag)
        if dma:
            st.needs_inc = True
        self.steps[engine].append(st)
        for t in st.deps:
            if not (t.engine == engine == "tensor"):
                t.needs_inc = True
        for r in reads:
            r.readers.append(st)
        for w in writes:
            w.writer = st
            w.readers = []
        return st

    def barrier(self):
        last = {e: (self.steps[e][-1] if self.steps[e] else None) for e in ENGINES}
        for e in ENGINES:
            self.pending[e] = [last[o] for o in ENGINES if o != e and last[o] is not None]

    def finalize(self, block):
        sems = {}
        for e in ENGINES:
            cnt, si = 0, 0
            sem = self.es.enter_context(self.nc.semaphore(f"S_{e}_{si}"))
            for st in self.steps[e]:
                if st.needs_inc:
                    if cnt + st.amount > SEM_ROLLOVER:
                        si += 1
                        sem = self.es.enter_context(self.nc.semaphore(f"S_{e}_{si}"))
                        cnt = 0
                    cnt += st.amount
                    st.sem = sem
                    st.val = cnt

        def mk_body(e):
            def body(eng):
                seen = {}
                for st in self.steps[e]:
                    for t in st.deps:
                        if t.engine == e == "tensor":
                            continue
                        assert t.sem is not None, f"dep of {st.tag} on {t.tag} missing inc"
                        if seen.get(t.sem, 0) >= t.val:
                            continue
                        eng.wait_ge(t.sem, t.val)
                        seen[t.sem] = t.val
                    ins = st.fn(eng)
                    if st.sem is not None:
                        assert ins is not None, f"step {st.tag} returned no instruction"
                        ins.then_inc(st.sem, st.amount)
            return body

        for e in ENGINES:
            getattr(block, e)(mk_body(e))



def _col_map():
    cols = {}
    pos = 0

    def add(name, n):
        nonlocal pos
        cols[name] = pos
        pos += n

    add("ba", 8); add("bb", 8); add("bc", 8)
    for l in range(2):
        for nm in ("ln1_w", "ln1_b", "ln2_w", "ln2_b", "bq", "bk", "bv", "bo"):
            add(f"l0_{nm}_{l}", 8)
        add(f"l0_gate_b_{l}", 1)
        for e in range(8):
            add(f"l0_eb1_{l}_{e}", 32)
            add(f"l0_eb2_{l}_{e}", 8)
    add("l1_b_gate", 8); add("l1_ln_g", 8); add("l1_ln_b", 8)
    for e in range(4):
        add(f"l1_b1_{e}", 32)
        add(f"l1_b2_{e}", 8)
    add("f_b_r", 1); add("f_b1", 32); add("f_b2", 8)
    add("f_wr", 8)
    add("w_dw", 24)
    return cols, pos


def _pack_small(inp):
    cols, total = _col_map()
    sm = np.zeros((128, total), np.float32)

    def put(name, vec):
        v = np.asarray(vec, np.float32).reshape(-1)
        n = (v.size + 127) // 128
        c0 = cols[name]
        buf = np.zeros(n * 128, np.float32)
        buf[: v.size] = v
        sm[:, c0 : c0 + n] = buf.reshape(n, 128).T

    put("ba", inp["L0_attn_ba"]); put("bb", inp["L0_attn_bb"]); put("bc", inp["L0_attn_bc"])
    for l in range(2):
        put(f"l0_ln1_w_{l}", inp["L0_ffn_ln1_w"][l]); put(f"l0_ln1_b_{l}", inp["L0_ffn_ln1_b"][l])
        put(f"l0_ln2_w_{l}", inp["L0_ffn_ln2_w"][l]); put(f"l0_ln2_b_{l}", inp["L0_ffn_ln2_b"][l])
        put(f"l0_bq_{l}", inp["L0_ffn_bq"][l]); put(f"l0_bk_{l}", inp["L0_ffn_bk"][l])
        put(f"l0_bv_{l}", inp["L0_ffn_bv"][l]); put(f"l0_bo_{l}", inp["L0_ffn_bo"][l])
        put(f"l0_gate_b_{l}", inp["L0_ffn_gate_b"][l])
        for e in range(8):
            put(f"l0_eb1_{l}_{e}", inp["L0_ffn_e_b1"][l, e])
            put(f"l0_eb2_{l}_{e}", inp["L0_ffn_e_b2"][l, e])
    put("l1_b_gate", inp["L1_attn_b_gate"]); put("l1_ln_g", inp["L1_attn_ln_g"]); put("l1_ln_b", inp["L1_attn_ln_b"])
    for e in range(4):
        put(f"l1_b1_{e}", inp["L1_attn_b1"][e])
        put(f"l1_b2_{e}", inp["L1_attn_b2"][e])
    put("f_b_r", inp["L1_ffn_b_r"]); put("f_b1", inp["L1_ffn_b1"]); put("f_b2", inp["L1_ffn_b2"])
    put("f_wr", inp["L1_ffn_w_r"])
    wd = np.asarray(inp["L1_attn_w_dw"], np.float32)
    wcols = np.zeros((128, 24), np.float32)
    for c in range(8):
        wcols[:, c * 3 : c * 3 + 3] = wd[c * 128 : (c + 1) * 128, :]
    sm[:, cols["w_dw"] : cols["w_dw"] + 24] = wcols
    return sm


def _rope_tables():
    inv_freq = (1.0 / 10000.0 ** (np.arange(0, HDIM, 2, dtype=np.float32) / HDIM))
    pos = np.arange(T, dtype=np.float32)
    theta = pos[:, None] * inv_freq[None, :].astype(np.float32)
    sin = np.sin(theta).astype(np.float32)
    cos = np.cos(theta).astype(np.float32)
    cosE = np.repeat(cos.T, 2, axis=0)
    sinE = np.repeat(sin.T, 2, axis=0)
    return (np.ascontiguousarray(np.concatenate([cosE, cosE], axis=0)),
            np.ascontiguousarray(np.concatenate([sinE, sinE], axis=0)))


def _jmat():
    J = np.zeros((64, 64), np.float32)
    for i in range(32):
        J[2 * i + 1, 2 * i] = -1.0
        J[2 * i, 2 * i + 1] = 1.0
    Z = np.zeros_like(J)
    return np.ascontiguousarray(np.block([[J, Z], [Z, J]]))


def host_prep(inp):
    f = np.float32
    w = {}
    w["waT"] = np.ascontiguousarray(np.asarray(inp["L0_attn_Wa"], f).T)
    w["wbT"] = np.ascontiguousarray(np.asarray(inp["L0_attn_Wb"], f).T)
    w["wcT"] = np.ascontiguousarray(np.asarray(inp["L0_attn_Wc"], f).T)
    for nm in ("wq", "wk", "wv", "wo"):
        w[nm] = np.ascontiguousarray(np.asarray(inp[f"L0_ffn_{nm}"], f))
    w["gateT"] = np.ascontiguousarray(np.asarray(inp["L0_ffn_gate_w"], f))
    w["ew1"] = np.ascontiguousarray(np.asarray(inp["L0_ffn_e_w1"], f))
    w["ew2"] = np.ascontiguousarray(np.asarray(inp["L0_ffn_e_w2"], f))
    w["pwT"] = np.ascontiguousarray(np.asarray(inp["L1_attn_w_pw"], f).T)
    w["qkvT"] = np.ascontiguousarray(np.asarray(inp["L1_attn_w_qkv"], f).T)
    w["woT1"] = np.ascontiguousarray(np.asarray(inp["L1_attn_w_o"], f).T)
    w["wgT"] = np.ascontiguousarray(np.asarray(inp["L1_attn_w_gate"], f).T)
    w["mgT"] = np.ascontiguousarray(np.asarray(inp["L1_attn_w_moe_gate"], f).T)
    w["l1w1T"] = np.ascontiguousarray(np.asarray(inp["L1_attn_w1"], f).transpose(0, 2, 1))
    w["l1w2T"] = np.ascontiguousarray(np.asarray(inp["L1_attn_w2"], f).transpose(0, 2, 1))
    w["fW1"] = np.ascontiguousarray(np.asarray(inp["L1_ffn_W1"], f))
    w["fW2"] = np.ascontiguousarray(np.asarray(inp["L1_ffn_W2"], f))
    w["fwr"] = np.ascontiguousarray(np.asarray(inp["L1_ffn_w_r"], f))
    w["smallv"] = _pack_small(inp)
    w["ropeC"], w["ropeS"] = _rope_tables()
    w["jmat"] = _jmat()
    return w



class KB:
    def __init__(self, phases=99, dbg=()):
        self.phases = phases
        self.dbg = set(dbg)
        self.nc = bass.Bass("TRN2", target_bir_lowering=False)
        self.es = ExitStack()
        self.sc = Sched(self.nc, self.es)
        self.cols = _col_map()[0]
        self.res = {}
        self._declare_io()
        self._globals()

    def R(self, name):
        if name not in self.res:
            self.res[name] = Res(name)
        return self.res[name]

    def _dram(self, name, shape, dtype, kind):
        t = self.nc.dram_tensor(name, shape, dtype, kind=kind)
        return t.ap()

    def _declare_io(self):
        self.x_in = self._dram("in_x", [T, D], F32, "ExternalInput")
        for nm, shape in [
            ("waT", [D, D]), ("wbT", [D, D]), ("wcT", [D, D]),
            ("wq", [2, D, D]), ("wk", [2, D, D]), ("wv", [2, D, D]), ("wo", [2, D, D]),
            ("gateT", [2, D, 8]), ("ew1", [2, 8, D, DFF]), ("ew2", [2, 8, DFF, D]),
            ("pwT", [D, D]), ("qkvT", [D, 3 * D]), ("woT1", [D, D]), ("wgT", [D, D]),
            ("mgT", [D, 4]), ("l1w1T", [4, D, DFF]), ("l1w2T", [4, DFF, D]),
            ("fW1", [D, DFF]), ("fW2", [DFF, D]), ("fwr", [D, 1]),
            ("smallv", [128, _col_map()[1]]), ("ropeC", [128, T]), ("ropeS", [128, T]),
            ("jmat", [128, 128]),
        ]:
            setattr(self, "w_" + nm, self._dram("in_" + nm, shape, F32, "ExternalInput"))
        scr = [
            ("sc_xfm", F32), ("sc_h", F32), ("sc_xnb", BF16), ("sc_s0", F32),
            ("sc_q", BF16), ("sc_k", BF16), ("sc_v", BF16), ("sc_o", BF16),
            ("sc_hn2", BF16), ("sc_moe", F32), ("sc_x1", F32), ("sc_xn1", F32),
            ("sc_xn1b", BF16), ("sc_xa", F32), ("sc_xg", F32), ("sc_s1", F32),
            ("sc_blk", F32),
        ]
        for nm, dt in scr:
            kind = "ExternalOutput" if nm in self.dbg else "Internal"
            setattr(self, nm, self._dram(nm, [D, T], dt, kind))
        self.sc_row = self._dram("sc_row", [16, T], F32,
                                 "ExternalOutput" if "sc_row" in self.dbg else "Internal")
        self.sc_row2 = self._dram("sc_row2", [1, 8 * T], F32,
                                  "ExternalOutput" if "sc_row2" in self.dbg else "Internal")
        self.sc_dbg = self._dram("sc_dbg", [128, 256], F32,
                                 "ExternalOutput" if "sc_dbg" in self.dbg else "Internal")
        self.sc_dbg2 = self._dram("sc_dbg2", [128, 4 * D], F32,
                                  "ExternalOutput" if "sc_dbg2" in self.dbg else "Internal")
        self.out = self._dram("out", [T, D], F32, "ExternalOutput")
        self.pcast = {}
        for nm, shape in [
            ("waT", [D, D]), ("wbT", [D, D]), ("wcT", [D, D]),
            ("wq", [2, D, D]), ("wk", [2, D, D]), ("wv", [2, D, D]), ("wo", [2, D, D]),
            ("ew1", [2, 8, D, DFF]), ("ew2", [2, 8, DFF, D]),
            ("pwT", [D, D]), ("qkvT", [D, 3 * D]), ("woT1", [D, D]), ("wgT", [D, D]),
            ("l1w1T", [4, D, DFF]), ("l1w2T", [4, DFF, D]),
            ("fW1", [D, DFF]), ("fW2", [DFF, D]),
        ]:
            self.pcast[nm] = self._dram("pc_" + nm, shape, BF16, "Internal")

    def sbuf(self, name, shape, dtype, stack=None):
        return (stack or self.es).enter_context(self.nc.sbuf_tensor(name, shape, dtype))

    def psum(self, name, shape, dtype, stack=None):
        return (stack or self.es).enter_context(self.nc.psum_tensor(name, shape, dtype))

    def col(self, name, i=0):
        c = self.cols[name] + i
        return self.smallv_sb[:, c : c + 1]

    def _globals(self):
        s = self.sc
        self.smallv_sb = self.sbuf("smallv_sb", [128, _col_map()[1]], F32)
        self.ident_f = self.sbuf("ident_f", [128, 128], F32)
        self.ident_b = self.sbuf("ident_b", [128, 128], BF16)
        self.triu_b = self.sbuf("triu_b", [128, 128], BF16)
        self.ones_f = self.sbuf("ones_f", [128, 1], F32)
        self.ones_b = self.sbuf("ones_b", [128, 1], BF16)
        self.ones_r = self.sbuf("ones_r", [1, 128], F32)
        self.eps5 = self.sbuf("c_eps5", [128, 1], F32)
        self.cwcol_sb = self.sbuf("G_col", [128, 8], F32)
        self.r_cwcol = self.R("G_col")
        s.step("sync", lambda e: e.dma_start(out=self.smallv_sb[:], in_=self.w_smallv),
               writes=[self.R("smallv")], dma=True, tag="smallv")

        def mk_memsets(g):
            g.memset(self.ident_f[:], 0.0)
            g.memset(self.ident_b[:], 0.0)
            g.memset(self.triu_b[:], 0.0)
            g.memset(self.ones_f[:], 1.0)
            g.memset(self.eps5[:], 1e-5)
            g.memset(self.ones_b[:], 1.0)
            return g.memset(self.ones_r[:], 1.0)
        s.step("gpsimd", mk_memsets, writes=[self.R("consts")], tag="consts0")
        s.step("gpsimd", lambda g: g.affine_select(
            out=self.ident_f[:], in_=self.ident_f[:], compare_op=ALU.not_equal,
            fill=1.0, base=0, pattern=[[-1, 128]], channel_multiplier=1),
            reads=[self.R("consts")], writes=[self.R("consts")], tag="consts1")
        s.step("gpsimd", lambda g: g.affine_select(
            out=self.ident_b[:], in_=self.ident_b[:], compare_op=ALU.not_equal,
            fill=1.0, base=0, pattern=[[-1, 128]], channel_multiplier=1),
            reads=[self.R("consts")], writes=[self.R("consts")], tag="consts2")
        s.step("gpsimd", lambda g: g.affine_select(
            out=self.triu_b[:], in_=self.triu_b[:], compare_op=ALU.is_gt,
            fill=1.0, base=0, pattern=[[-1, 128]], channel_multiplier=1),
            reads=[self.R("consts")], writes=[self.R("consts")], tag="consts3")
        def pc(nm, idx=None):
            srcap = getattr(self, "w_" + nm)
            dstap = self.pcast[nm]
            key = f"pc_{nm}" + ("" if idx is None else "_" + "_".join(map(str, idx)))
            if idx is not None:
                for i in idx:
                    srcap = srcap[i]
                    dstap = dstap[i]
            s.step("gpsimd", lambda g, srcap=srcap, dstap=dstap: g.dma_start(
                out=dstap, in_=srcap),
                writes=[self.R(key)], dma=True, tag=key)
        for nm in ("waT", "wbT", "wcT"):
            pc(nm)
        for l in range(2):
            for nm in ("wq", "wk", "wv", "wo"):
                pc(nm, (l,))
            for e in range(8):
                pc("ew1", (l, e))
                pc("ew2", (l, e))
        for nm in ("pwT", "qkvT", "woT1", "wgT", "fW1", "fW2"):
            pc(nm)
        for e in range(4):
            pc("l1w1T", (e,))
            pc("l1w2T", (e,))

    def fm_load(self, sb3, dram, t0, tw, reads, writes, tag, engine="sync"):
        src = dram.rearrange("(c p) t -> p c t", p=128)[:, :, t0 : t0 + tw]
        return self.sc.step(engine, lambda g: g.dma_start(out=sb3, in_=src),
                            reads=reads, writes=writes, dma=True, tag=tag)

    def fm_store(self, sb3, dram, t0, tw, reads, writes, tag, engine="sync", accum=False):
        dst = dram.rearrange("(c p) t -> p c t", p=128)[:, :, t0 : t0 + tw]
        if accum:
            return self.sc.step("gpsimd", lambda g: g.dma_start(out=dst, in_=sb3, accum_op=ALU.add),
                                reads=reads, writes=writes, dma=True, tag=tag)
        return self.sc.step(engine, lambda g: g.dma_start(out=dst, in_=sb3),
                            reads=reads, writes=writes, dma=True, tag=tag)

    def phase_A(self):
        s = self.sc
        with ExitStack() as ps:
            xt = [self.sbuf(f"A_x{i}", [128, D], F32, ps) for i in range(2)]
            stat = self.sbuf("A_stat", [128, 8], F32, ps)
            sqd = self.sbuf("A_sqd", [128, D], F32, ps)
            xn = [self.sbuf(f"A_xn{i}", [128, D], F32, ps) for i in range(2)]
            xT = [self.sbuf(f"A_xT{i}", [128, D], F32, ps) for i in range(2)]
            xnT = [self.sbuf(f"A_xnT{i}", [128, D], F32, ps) for i in range(2)]
            xnTb = [self.sbuf(f"A_xnTb{i}", [128, D], BF16, ps) for i in range(2)]
            pt = [self.psum(f"A_pt{i}", [128, 128], F32, ps) for i in range(2)]
            rx = [self.R(f"A_x{i}") for i in range(2)]
            rst = self.R("A_stat"); rsq = self.R("A_sqd")
            rxn = [self.R(f"A_xn{i}") for i in range(2)]
            rxT = [self.R(f"A_xT{i}") for i in range(2)]
            rxnT = [self.R(f"A_xnT{i}") for i in range(2)]
            rxnTb = [self.R(f"A_xnTb{i}") for i in range(2)]
            rpt = [self.R(f"A_pt{i}") for i in range(2)]
            for tt in range(NT1):
                b = tt % 2
                t0 = tt * 128
                s.step("sync", lambda e, b=b, t0=t0: e.dma_start(
                    out=xt[b][:], in_=self.x_in[t0 : t0 + 128, :]),
                    writes=[rx[b]], dma=True, tag=f"A_ld{tt}")
                s.step("vector", lambda e: e.memset(stat[:, 1:2], 0.0),
                       reads=[], writes=[rst], tag=f"A_z{tt}")
                s.step("scalar", lambda e, b=b: e.activation(
                    sqd[:], xt[b][:], AF.Square, accum_out=stat[:, 1:2]),
                    reads=[rx[b]], writes=[rsq, rst], tag=f"A_sq{tt}")
                s.step("vector", lambda e, b=b: e.reduce_sum(stat[:, 0:1], xt[b][:], AX.X),
                       reads=[rx[b]], writes=[rst], tag=f"A_sum{tt}")

                def stats1(v):
                    v.tensor_scalar_mul(stat[:, 2:3], stat[:, 0:1], 1.0 / D)
                    return v.tensor_scalar_mul(stat[:, 3:4], stat[:, 1:2], 1.0 / D)
                s.step("vector", stats1, reads=[rst], writes=[rst], tag=f"A_st1{tt}")
                s.step("vector", lambda e: e.tensor_tensor(
                    stat[:, 4:5], stat[:, 2:3], stat[:, 2:3], ALU.mult),
                    reads=[rst], writes=[rst], tag=f"A_st2{tt}")
                s.step("vector", lambda e: e.tensor_tensor(
                    stat[:, 3:4], stat[:, 3:4], stat[:, 4:5], ALU.subtract),
                    reads=[rst], writes=[rst], tag=f"A_st3{tt}")
                s.step("scalar", lambda e: e.activation(stat[:, 5:6], stat[:, 3:4], AF.Sqrt, bias=self.eps5[:]),
                       reads=[rst], writes=[rst], tag=f"A_sqrt{tt}")
                s.step("vector", lambda e: e.reciprocal(stat[:, 6:7], stat[:, 5:6]),
                       reads=[rst], writes=[rst], tag=f"A_rcp{tt}")
                s.step("vector", lambda e, b=b: e.tensor_scalar(
                    xn[b][:], xt[b][:], stat[:, 2:3], stat[:, 6:7], ALU.subtract, ALU.mult),
                    reads=[rx[b], rst], writes=[rxn[b]], tag=f"A_nrm{tt}")
                if "sc_dbg" in self.dbg:
                    s.step("sync", lambda e, tt=tt: e.dma_start(
                        out=self.sc_dbg[:, tt * 8 : tt * 8 + 8], in_=stat[:]),
                        reads=[rst], writes=[self.R("sc_dbg")], dma=True, tag=f"A_dbg{tt}")
                if "sc_dbg2" in self.dbg and tt < 4:
                    s.step("sync", lambda e, tt=tt, b=b: e.dma_start(
                        out=self.sc_dbg2[:, tt * D : (tt + 1) * D], in_=xn[b][:]),
                        reads=[rxn[b]], writes=[self.R("sc_dbg2")], dma=True, tag=f"A_dbg2{tt}")
                for c in range(NCH):
                    pb = c % 2
                    s.step("tensor", lambda e, b=b, c=c, pb=pb: e.transpose(
                        pt[pb][:], xt[b][:, c * 128 : (c + 1) * 128], self.ident_f[:]),
                        reads=[rx[b], self.R("consts")], writes=[rpt[pb]], tag=f"A_tx{tt}_{c}")
                    s.step("vector", lambda e, b=b, c=c, pb=pb: e.tensor_copy(
                        xT[b][:, c * 128 : (c + 1) * 128], pt[pb][:]),
                        reads=[rpt[pb]], writes=[rxT[b]], tag=f"A_ex{tt}_{c}")
                    s.step("tensor", lambda e, b=b, c=c, pb=pb: e.transpose(
                        pt[pb][:], xn[b][:, c * 128 : (c + 1) * 128], self.ident_f[:]),
                        reads=[rxn[b]], writes=[rpt[pb]], tag=f"A_txn{tt}_{c}")
                    s.step("vector", lambda e, b=b, c=c, pb=pb: e.tensor_copy(
                        xnT[b][:, c * 128 : (c + 1) * 128], pt[pb][:]),
                        reads=[rpt[pb]], writes=[rxnT[b]], tag=f"A_exn{tt}_{c}")
                s.step("scalar", lambda e, b=b: e.copy(xnTb[b][:], xnT[b][:]),
                       reads=[rxnT[b]], writes=[rxnTb[b]], tag=f"A_cast{tt}")
                self.fm_store(xT[b].rearrange("p (c t) -> p c t", c=8), self.sc_xfm,
                              t0, 128, [rxT[b]], [self.R("sc_xfm")], f"A_stx{tt}")
                self.fm_store(xnT[b].rearrange("p (c t) -> p c t", c=8), self.sc_h,
                              t0, 128, [rxnT[b]], [self.R("sc_h")], f"A_sth{tt}")
                self.fm_store(xnTb[b].rearrange("p (c t) -> p c t", c=8), self.sc_xnb,
                              t0, 128, [rxnTb[b]], [self.R("sc_xnb")], f"A_stnb{tt}")
            s.barrier()

    def phase_B(self, ps=None):
        s = self.sc
        own = ps is None
        with (ExitStack() if own else ExitStack()) as tmp:
            ps = tmp if own else ps
            xnb = self.sbuf("B_xnb", [128, NCH, T], BF16, ps)
            wT = {nm: self.sbuf(f"B_{nm}", [128, NCH, D], BF16, ps)
                  for nm in ("wa", "wb", "wc")}
            a_sb = self.sbuf("B_a", [128, T], F32, ps)
            u_sb = self.sbuf("B_u", [128, T], F32, ps)
            y_sb = self.sbuf("B_y", [128, T], F32, ps)
            ctmp = self.sbuf("B_c", [128, 512], F32, ps)
            stile = self.sbuf("B_s", [128, 512], F32, ps)
            pa = self.psum("B_pa", [128, 512], F32, ps)
            pb_ = self.psum("B_pb", [128, 512], F32, ps)
            pc = self.psum("B_pc", [128, 512], F32, ps)
            rxnb = self.R("B_xnb")
            rw = {nm: self.R(f"B_{nm}") for nm in ("wa", "wb", "wc")}
            ra, ru, ry = self.R("B_a"), self.R("B_u"), self.R("B_y")
            rc, rs = self.R("B_c"), self.R("B_s")
            rpa, rpb, rpc = self.R("B_pa"), self.R("B_pb"), self.R("B_pc")

            s.step("sync", lambda e: e.dma_start(
                out=xnb[:], in_=self.sc_xnb.rearrange("(c p) t -> p c t", p=128)),
                reads=[self.R("sc_xnb")], writes=[rxnb], dma=True, tag="B_ldx")
            for nm, srcw, key in (("wa", self.pcast["waT"], "pc_waT"),
                                  ("wb", self.pcast["wbT"], "pc_wbT"),
                                  ("wc", self.pcast["wcT"], "pc_wcT")):
                s.step("sync", lambda e, nm=nm, srcw=srcw: e.dma_start(
                    out=wT[nm][:], in_=srcw.rearrange("(c p) e -> p c e", p=128)),
                    reads=[self.R(key)], writes=[rw[nm]], dma=True, tag=f"B_lw{nm}")

            def mm_group(pp, wname, ec, t0):
                def mm(t):
                    ins = None
                    for kc in range(NCH):
                        ins = t.matmul(pp[:], wT[wname][:, kc, ec * 128 : (ec + 1) * 128],
                                       xnb[:, kc, t0 : t0 + 512],
                                       start=(kc == 0), stop=(kc == NCH - 1))
                    return ins
                return mm

            for ec in range(NCH):
                for tt in range(NT5):
                    t0 = tt * 512
                    s.step("tensor", mm_group(pa, "wa", ec, t0),
                           reads=[rxnb, rw["wa"]], writes=[rpa], tag=f"B_mma{ec}_{tt}")
                    s.step("scalar", lambda e, t0=t0, ec=ec: e.activation(
                        a_sb[:, t0 : t0 + 512], pa[:], AF.Sigmoid, bias=self.col("ba", ec)),
                        reads=[rpa, self.R("smallv")], writes=[ra], tag=f"B_eva{ec}_{tt}")
                    s.step("tensor", mm_group(pb_, "wb", ec, t0),
                           reads=[rxnb, rw["wb"]], writes=[rpb], tag=f"B_mmb{ec}_{tt}")
                    s.step("scalar", lambda e, t0=t0, ec=ec: e.activation(
                        u_sb[:, t0 : t0 + 512], pb_[:], AF.Tanh, bias=self.col("bb", ec)),
                        reads=[rpb, self.R("smallv")], writes=[ru], tag=f"B_evb{ec}_{tt}")
                s.step("vector", lambda e: e.tensor_scalar(
                    y_sb[:], a_sb[:], -1.0, 1.0, ALU.mult, ALU.add),
                    reads=[ra], writes=[ry], tag=f"B_oma{ec}")
                s.step("vector", lambda e: e.tensor_tensor(u_sb[:], u_sb[:], y_sb[:], ALU.mult),
                       reads=[ru, ry], writes=[ru], tag=f"B_u{ec}")
                s.step("vector", lambda e: e.tensor_tensor_scan(
                    y_sb[:], a_sb[:], u_sb[:], 0.0, ALU.mult, ALU.add),
                    reads=[ra, ru], writes=[ry], tag=f"B_scan{ec}")
                for tt in range(NT5):
                    t0 = tt * 512
                    s.step("tensor", mm_group(pc, "wc", ec, t0),
                           reads=[rxnb, rw["wc"]], writes=[rpc], tag=f"B_mmc{ec}_{tt}")
                    s.step("scalar", lambda e, ec=ec: e.activation(
                        ctmp[:], pc[:], AF.Sigmoid, bias=self.col("bc", ec)),
                        reads=[rpc, self.R("smallv")], writes=[rc], tag=f"B_evc{ec}_{tt}")
                    s.step("vector", lambda e, t0=t0: e.tensor_tensor(
                        stile[:], ctmp[:], y_sb[:, t0 : t0 + 512], ALU.mult),
                        reads=[rc, ry], writes=[rs], tag=f"B_s{ec}_{tt}")
                    s.step("sync", lambda e, t0=t0, ec=ec: e.dma_start(
                        out=self.sc_s0[ec * 128 : (ec + 1) * 128, t0 : t0 + 512], in_=stile[:]),
                        reads=[rs], writes=[self.R("sc_s0")], dma=True, tag=f"B_st{ec}_{tt}")
            if own:
                s.barrier()

    def ln_fm(self, tag, src_dram, wb, sink_b16, f32_hook=None, store_f32=None,
              store_b16=None, eps=1e-5):
        s = self.sc
        with ExitStack() as ps:
            h8 = [self.sbuf(f"{tag}_h{i}", [128, NCH, 512], F32, ps) for i in range(2)]
            sq = [self.sbuf(f"{tag}_sq{i}", [128, 512], BF16, ps) for i in range(2)]
            rows = self.sbuf(f"{tag}_rows", [1, 8 * 512], F32, ps)
            tmp1 = [self.sbuf(f"{tag}_t1{i}", [128, 512], F32, ps) for i in range(2)]
            of32 = [self.sbuf(f"{tag}_o{i}", [128, 512], F32, ps) for i in range(2)]
            pss = self.psum(f"{tag}_pss", [1, 512], F32, ps)
            psq = self.psum(f"{tag}_psq", [1, 512], F32, ps)
            pmb = self.psum(f"{tag}_pmb", [128, 512], F32, ps)
            prb = self.psum(f"{tag}_prb", [128, 512], F32, ps)
            rh = [self.R(f"{tag}_h{i}") for i in range(2)]
            rsq = [self.R(f"{tag}_sq{i}") for i in range(2)]
            rrows = self.R(f"{tag}_rows")
            rt1 = [self.R(f"{tag}_t1{i}") for i in range(2)]
            ro32 = [self.R(f"{tag}_o{i}") for i in range(2)]
            rpss, rpsq = self.R(f"{tag}_pss"), self.R(f"{tag}_psq")
            rpmb, rprb = self.R(f"{tag}_pmb"), self.R(f"{tag}_prb")
            RS = self.R("smallv")
            if store_b16 is not None:
                assert sink_b16 is None
                b16t = [self.sbuf(f"{tag}_b16{i}", [128, 512], BF16, ps) for i in range(2)]
                rb16 = [self.R(f"{tag}_b16{i}") for i in range(2)]
                def sink_b16(kc, tt, t0):
                    return b16t[kc % 2][:], rb16[kc % 2]
                orig_hook = f32_hook
                def f32_hook(kc, tt, t0, ap, res):
                    if orig_hook is not None:
                        orig_hook(kc, tt, t0, ap, res)
                    b2 = kc % 2
                    self.sc.step("sync", lambda e, b2=b2, kc=kc, t0=t0: e.dma_start(
                        out=store_b16[kc * 128 : (kc + 1) * 128, t0 : t0 + 512],
                        in_=b16t[b2][:]),
                        reads=[rb16[b2]], writes=[self.R(store_b16.tensor.name)],
                        dma=True, tag=f"{tag}_sb16{tt}_{kc}")
            for tt in range(NT5):
                t0 = tt * 512
                b = tt % 2
                self.fm_load(h8[b][:], src_dram, t0, 512, [self.R(src_dram.tensor.name)],
                             [rh[b]], f"{tag}_ld{tt}")
                for kc in range(NCH):
                    qb = kc % 2
                    s.step("scalar", lambda e, b=b, kc=kc, qb=qb: e.activation(
                        sq[qb][:], h8[b][:, kc, :], AF.Square),
                        reads=[rh[b]], writes=[rsq[qb]], tag=f"{tag}_sq{tt}_{kc}")
                    s.step("tensor", lambda e, qb=qb, kc=kc: e.matmul(
                        psq[:], self.ones_b[:], sq[qb][:],
                        start=(kc == 0), stop=(kc == NCH - 1)),
                        reads=[rsq[qb], self.R("consts")], writes=[rpsq], tag=f"{tag}_mq{tt}_{kc}")
                    s.step("tensor", lambda e, b=b, kc=kc: e.matmul(
                        pss[:], self.ones_f[:], h8[b][:, kc, :],
                        start=(kc == 0), stop=(kc == NCH - 1)),
                        reads=[rh[b], self.R("consts")], writes=[rpss], tag=f"{tag}_ms{tt}_{kc}")
                s.step("vector", lambda e: e.tensor_copy(rows[:, 0:512], pss[:]),
                       reads=[rpss], writes=[rrows], tag=f"{tag}_r0{tt}")
                s.step("vector", lambda e: e.tensor_copy(rows[:, 512:1024], psq[:]),
                       reads=[rpsq], writes=[rrows], tag=f"{tag}_r1{tt}")
                def rstats(v):
                    v.tensor_scalar_mul(rows[:, 1024:1536], rows[:, 0:512], 1.0 / D)
                    return v.tensor_scalar_mul(rows[:, 1536:2048], rows[:, 512:1024], 1.0 / D)
                s.step("vector", rstats, reads=[rrows], writes=[rrows], tag=f"{tag}_r2{tt}")
                s.step("vector", lambda e: e.tensor_tensor(
                    rows[:, 2048:2560], rows[:, 1024:1536], rows[:, 1024:1536], ALU.mult),
                    reads=[rrows], writes=[rrows], tag=f"{tag}_r3{tt}")
                s.step("vector", lambda e: e.tensor_tensor(
                    rows[:, 1536:2048], rows[:, 1536:2048], rows[:, 2048:2560], ALU.subtract),
                    reads=[rrows], writes=[rrows], tag=f"{tag}_r4{tt}")
                s.step("scalar", lambda e: e.activation(
                    rows[:, 2560:3072], rows[:, 1536:2048], AF.Sqrt, bias=self.eps5[0:1]),
                    reads=[rrows, self.R("consts")], writes=[rrows], tag=f"{tag}_r5{tt}")
                s.step("vector", lambda e: e.reciprocal(rows[:, 3072:3584], rows[:, 2560:3072]),
                       reads=[rrows], writes=[rrows], tag=f"{tag}_r6{tt}")
                s.step("tensor", lambda e: e.matmul(
                    pmb[:], self.ones_r[:], rows[:, 1024:1536], start=True, stop=True),
                    reads=[rrows, self.R("consts")], writes=[rpmb], tag=f"{tag}_bm{tt}")
                s.step("tensor", lambda e: e.matmul(
                    prb[:], self.ones_r[:], rows[:, 3072:3584], start=True, stop=True),
                    reads=[rrows, self.R("consts")], writes=[rprb], tag=f"{tag}_br{tt}")
                for kc in range(NCH):
                    qb = kc % 2
                    s.step("vector", lambda e, b=b, kc=kc, qb=qb: e.tensor_tensor(
                        tmp1[qb][:], h8[b][:, kc, :], pmb[:], ALU.subtract),
                        reads=[rh[b], rpmb], writes=[rt1[qb]], tag=f"{tag}_n1{tt}_{kc}")
                    if wb is None:
                        s.step("vector", lambda e, qb=qb: e.tensor_tensor(
                            of32[qb][:], tmp1[qb][:], prb[:], ALU.mult),
                            reads=[rt1[qb], rprb], writes=[ro32[qb]], tag=f"{tag}_n2{tt}_{kc}")
                    else:
                        s.step("vector", lambda e, qb=qb: e.tensor_tensor(
                            tmp1[qb][:], tmp1[qb][:], prb[:], ALU.mult),
                            reads=[rt1[qb], rprb], writes=[rt1[qb]], tag=f"{tag}_n2{tt}_{kc}")
                        wcol = self.col(wb[0], kc)
                        bcol = self.col(wb[1], kc)
                        s.step("vector", lambda e, qb=qb, wcol=wcol, bcol=bcol: e.tensor_scalar(
                            of32[qb][:], tmp1[qb][:], wcol, bcol, ALU.mult, ALU.add),
                            reads=[rt1[qb], RS], writes=[ro32[qb]], tag=f"{tag}_n3{tt}_{kc}")
                    dst, dres = sink_b16(kc, tt, t0)
                    s.step("scalar", lambda e, qb=qb, dst=dst: e.copy(dst, of32[qb][:]),
                           reads=[ro32[qb]], writes=[dres], tag=f"{tag}_cb{tt}_{kc}")
                    if store_f32 is not None:
                        s.step("sync", lambda e, qb=qb, kc=kc, t0=t0, sf=store_f32: e.dma_start(
                            out=sf[kc * 128 : (kc + 1) * 128, t0 : t0 + 512], in_=of32[qb][:]),
                            reads=[ro32[qb]], writes=[self.R(store_f32.tensor.name)],
                            dma=True, tag=f"{tag}_sf{tt}_{kc}")
                    if f32_hook is not None:
                        f32_hook(kc, tt, t0, of32[qb], ro32[qb])
            s.barrier()

    def phase_C(self, l, h_src, h_dst):
        self.phase_C12(l, h_src)
        self.phase_C3(l)
        self.phase_C456(l, h_src, h_dst)

    def phase_C12(self, l, h_src):
        s = self.sc
        with ExitStack() as ps:
            hnb = self.sbuf(f"C1hnb_{l}", [128, NCH, T], BF16, ps)
            rhnb = self.R(f"C1hnb_{l}")
            self.ln_fm(f"C1_{l}", h_src, (f"l0_ln1_w_{l}", f"l0_ln1_b_{l}"),
                       lambda kc, tt, t0: (hnb[:, kc, t0 : t0 + 512], rhnb))
            wsb = [self.sbuf(f"C2w{i}_{l}", [128, NCH, D], BF16, ps) for i in range(2)]
            rw = [self.R(f"C2w{i}_{l}") for i in range(2)]
            ot = [self.sbuf(f"C2o{i}_{l}", [128, 512], BF16, ps) for i in range(4)]
            rot = [self.R(f"C2o{i}_{l}") for i in range(4)]
            pp = [self.psum(f"C2p{i}_{l}", [128, 512], F32, ps) for i in range(4)]
            rpp = [self.R(f"C2p{i}_{l}") for i in range(4)]
            specs = [("q", self.pcast["wq"], "l0_bq", self.sc_q),
                     ("k", self.pcast["wk"], "l0_bk", self.sc_k),
                     ("v", self.pcast["wv"], "l0_bv", self.sc_v)]
            for wi, (nm, wsrc, bname, dst) in enumerate(specs):
                wb = wi % 2
                s.step("sync", lambda e, wb=wb, wsrc=wsrc: e.dma_start(
                    out=wsb[wb][:], in_=wsrc[l].rearrange("(c p) e -> p c e", p=128)),
                    reads=[self.R(f"pc_w{nm}_{l}")], writes=[rw[wb]], dma=True,
                    tag=f"C2lw{l}_{nm}")
                n = 0
                for jc in range(NCH):
                    for tt in range(NT5):
                        t0 = tt * 512
                        pb = n % 4
                        n += 1
                        def mm(t, wb=wb, jc=jc, t0=t0, pb=pb):
                            ins = None
                            for kc in range(NCH):
                                ins = t.matmul(pp[pb][:], wsb[wb][:, kc, jc * 128 : (jc + 1) * 128],
                                               hnb[:, kc, t0 : t0 + 512],
                                               start=(kc == 0), stop=(kc == NCH - 1))
                            return ins
                        s.step("tensor", mm, reads=[rw[wb], rhnb], writes=[rpp[pb]],
                               tag=f"C2mm{l}_{nm}_{jc}_{tt}")
                        s.step("vector", lambda e, pb=pb, jc=jc, bname=bname: e.tensor_scalar(
                            ot[pb][:], pp[pb][:], self.col(f"{bname}_{l}", jc), None, ALU.add),
                            reads=[rpp[pb], self.R("smallv")], writes=[rot[pb]],
                            tag=f"C2ev{l}_{nm}_{jc}_{tt}")
                        s.step("sync", lambda e, pb=pb, jc=jc, t0=t0, dst=dst: e.dma_start(
                            out=dst[jc * 128 : (jc + 1) * 128, t0 : t0 + 512], in_=ot[pb][:]),
                            reads=[rot[pb]], writes=[self.R(dst.tensor.name)],
                            dma=True, tag=f"C2st{l}_{nm}_{jc}_{tt}")
            s.barrier()

    def phase_C3(self, l):
        s = self.sc
        with ExitStack() as ps:
            qh = self.sbuf(f"C3q_{l}", [64, T], BF16, ps)
            kh = self.sbuf(f"C3k_{l}", [64, T], BF16, ps)
            vh = self.sbuf(f"C3v_{l}", [64, T], BF16, ps)
            vext = self.sbuf(f"C3vx_{l}", [128, 32 * 65], BF16, ps)
            P = [self.sbuf(f"C3P{i}_{l}", [128, 512], BF16, ps) for i in range(3)]
            drow = self.sbuf(f"C3dr_{l}", [1, 2 * 512], F32, ps)
            onum = self.sbuf(f"C3on_{l}", [64, 512], F32, ps)
            ob = self.sbuf(f"C3ob_{l}", [64, 512], BF16, ps)
            pss = [self.psum(f"C3ps{i}_{l}", [128, 512], F32, ps) for i in range(2)]
            pso = [self.psum(f"C3po{i}_{l}", [128, 512], F32, ps) for i in range(2)]
            psv = self.psum(f"C3pv_{l}", [128, 64], BF16, ps)
            psd = self.psum(f"C3pd_{l}", [128, 512], F32, ps)
            rqh, rkh, rvh = self.R(f"C3q_{l}"), self.R(f"C3k_{l}"), self.R(f"C3v_{l}")
            rvx = self.R(f"C3vx_{l}")
            rP = [self.R(f"C3P{i}_{l}") for i in range(3)]
            rdr, ron, rob = self.R(f"C3dr_{l}"), self.R(f"C3on_{l}"), self.R(f"C3ob_{l}")
            rps = [self.R(f"C3ps{i}_{l}") for i in range(2)]
            rpo = [self.R(f"C3po{i}_{l}") for i in range(2)]
            rpv, rpd = self.R(f"C3pv_{l}"), self.R(f"C3pd_{l}")
            for hd in range(NHEAD):
                r0 = hd * 64
                for nm, buf, rr, src in (("q", qh, rqh, self.sc_q), ("k", kh, rkh, self.sc_k),
                                         ("v", vh, rvh, self.sc_v)):
                    s.step("sync", lambda e, buf=buf, src=src, r0=r0: e.dma_start(
                        out=buf[:], in_=src[r0 : r0 + 64, :]),
                        reads=[self.R(src.tensor.name)], writes=[rr], dma=True,
                        tag=f"C3ld{l}_{hd}_{nm}")
                s.step("gpsimd", lambda e: e.memset(vext[:], 1.0),
                       writes=[rvx], tag=f"C3one{l}_{hd}")
                for ki in range(32):
                    s.step("tensor", lambda e, ki=ki: e.transpose(
                        psv[:], vh[:, ki * 128 : (ki + 1) * 128], self.ident_b[0:64, 0:64]),
                        reads=[rvh, self.R("consts")], writes=[rpv], tag=f"C3tv{l}_{hd}_{ki}")
                    s.step("vector", lambda e, ki=ki: e.tensor_copy(
                        vext[:, ki * 65 : ki * 65 + 64], psv[:]),
                        reads=[rpv], writes=[rvx], tag=f"C3cv{l}_{hd}_{ki}")
                for qg in range(NT5):
                    q0 = qg * 512
                    ob_i = qg % 2
                    nki = 4 * qg + 4
                    for ki in range(nki):
                        sb = ki % 2
                        pb = ki % 3
                        s.step("tensor", lambda e, ki=ki, sb=sb, q0=q0: e.matmul(
                            pss[sb][:], kh[:, ki * 128 : (ki + 1) * 128], qh[:, q0 : q0 + 512],
                            start=True, stop=True),
                            reads=[rkh, rqh], writes=[rps[sb]], tag=f"C3s{l}_{hd}_{qg}_{ki}")
                        s.step("scalar", lambda e, sb=sb, pb=pb: e.activation(
                            P[pb][:], pss[sb][:], AF.Exp, scale=0.125),
                            reads=[rps[sb]], writes=[rP[pb]], tag=f"C3e{l}_{hd}_{qg}_{ki}")
                        off = ki * 128 - q0
                        if off >= 0:
                            if off > 0:
                                s.step("vector", lambda e, pb=pb, off=off: e.memset(
                                    P[pb][:, 0:off], 0.0),
                                    reads=[], writes=[rP[pb]], tag=f"C3z{l}_{hd}_{qg}_{ki}")
                            s.step("vector", lambda e, pb=pb, off=off: e.tensor_tensor(
                                P[pb][:, off : off + 128], P[pb][:, off : off + 128],
                                self.triu_b[:], ALU.mult),
                                reads=[self.R("consts")], writes=[rP[pb]],
                                tag=f"C3m{l}_{hd}_{qg}_{ki}")
                        s.step("tensor", lambda e, ki=ki, pb=pb, ob_i=ob_i, nki=nki: e.matmul(
                            pso[ob_i][0:65, :], vext[:, ki * 65 : (ki + 1) * 65], P[pb][:],
                            start=(ki == 0), stop=(ki == nki - 1)),
                            reads=[rvx, rP[pb]], writes=[rpo[ob_i]], tag=f"C3pv{l}_{hd}_{qg}_{ki}")
                    s.step("vector", lambda e, ob_i=ob_i: e.tensor_copy(
                        drow[:, 0:512], pso[ob_i][64:65, :]),
                        reads=[rpo[ob_i]], writes=[rdr], tag=f"C3d0{l}_{hd}_{qg}")
                    s.step("vector", lambda e: e.reciprocal(drow[:, 512:1024], drow[:, 0:512]),
                           reads=[rdr], writes=[rdr], tag=f"C3d1{l}_{hd}_{qg}")
                    s.step("scalar", lambda e, ob_i=ob_i: e.copy(onum[:], pso[ob_i][0:64, :]),
                           reads=[rpo[ob_i]], writes=[ron], tag=f"C3d2{l}_{hd}_{qg}")
                    s.step("tensor", lambda e: e.matmul(
                        psd[0:64, :], self.ones_r[0:1, 0:64], drow[:, 512:1024],
                        start=True, stop=True),
                        reads=[rdr, self.R("consts")], writes=[rpd], tag=f"C3d3{l}_{hd}_{qg}")
                    s.step("vector", lambda e: e.tensor_tensor(
                        ob[:], onum[:], psd[0:64, :], ALU.mult),
                        reads=[ron, rpd], writes=[rob], tag=f"C3d4{l}_{hd}_{qg}")
                    s.step("sync", lambda e, r0=r0, q0=q0: e.dma_start(
                        out=self.sc_o[r0 : r0 + 64, q0 : q0 + 512], in_=ob[:]),
                        reads=[rob], writes=[self.R("sc_o")], dma=True,
                        tag=f"C3st{l}_{hd}_{qg}")
            s.barrier()

    def phase_C456(self, l, h_src, h_dst):
        s = self.sc
        with ExitStack() as ps:
            wo = self.sbuf(f"C4w_{l}", [128, NCH, D], BF16, ps)
            rwo = self.R(f"C4w_{l}")
            o8 = [self.sbuf(f"C4o{i}_{l}", [128, NCH, 512], BF16, ps) for i in range(2)]
            ro8 = [self.R(f"C4o{i}_{l}") for i in range(2)]
            h8 = [self.sbuf(f"C4h{i}_{l}", [128, NCH, 512], F32, ps) for i in range(2)]
            rh8 = [self.R(f"C4h{i}_{l}") for i in range(2)]
            at = [self.sbuf(f"C4a{i}_{l}", [128, 512], F32, ps) for i in range(2)]
            rat = [self.R(f"C4a{i}_{l}") for i in range(2)]
            hn = [self.sbuf(f"C4n{i}_{l}", [128, NCH, 512], F32, ps) for i in range(2)]
            rhn = [self.R(f"C4n{i}_{l}") for i in range(2)]
            pp4 = [self.psum(f"C4p{i}_{l}", [128, 512], F32, ps) for i in range(2)]
            rpp4 = [self.R(f"C4p{i}_{l}") for i in range(2)]
            s.step("sync", lambda e: e.dma_start(
                out=wo[:], in_=self.pcast["wo"][l].rearrange("(c p) e -> p c e", p=128)),
                reads=[self.R(f"pc_wo_{l}")], writes=[rwo], dma=True, tag=f"C4lw{l}")
            for tt in range(NT5):
                t0 = tt * 512
                b = tt % 2
                self.fm_load(o8[b][:], self.sc_o, t0, 512, [self.R("sc_o")], [ro8[b]],
                             f"C4ldo{l}_{tt}")
                self.fm_load(h8[b][:], h_src, t0, 512, [self.R(h_src.tensor.name)], [rh8[b]],
                             f"C4ldh{l}_{tt}")
                for jc in range(NCH):
                    pb = jc % 2
                    def mm(t, b=b, jc=jc, pb=pb):
                        ins = None
                        for kc in range(NCH):
                            ins = t.matmul(pp4[pb][:], wo[:, kc, jc * 128 : (jc + 1) * 128],
                                           o8[b][:, kc, :],
                                           start=(kc == 0), stop=(kc == NCH - 1))
                        return ins
                    s.step("tensor", mm, reads=[rwo, ro8[b]], writes=[rpp4[pb]],
                           tag=f"C4mm{l}_{tt}_{jc}")
                    s.step("vector", lambda e, pb=pb, jc=jc: e.tensor_scalar(
                        at[pb][:], pp4[pb][:], self.col(f"l0_bo_{l}", jc), None, ALU.add),
                        reads=[rpp4[pb], self.R("smallv")], writes=[rat[pb]],
                        tag=f"C4ev{l}_{tt}_{jc}")
                    s.step("vector", lambda e, b=b, jc=jc, pb=pb: e.tensor_tensor(
                        hn[b][:, jc, :], h8[b][:, jc, :], at[pb][:], ALU.add),
                        reads=[rh8[b], rat[pb]], writes=[rhn[b]], tag=f"C4add{l}_{tt}_{jc}")
                self.fm_store(hn[b][:], h_src, t0, 512, [rhn[b]],
                              [self.R(h_src.tensor.name)], f"C4st{l}_{tt}")
            s.barrier()
        with ExitStack() as ps:
            gsb = self.sbuf(f"C5g_{l}", [128, NCH, 8], F32, ps)
            rgs = self.R(f"C5g_{l}")
            hb2 = [self.sbuf(f"C5b{i}_{l}", [128, 512], BF16, ps) for i in range(2)]
            rhb2 = [self.R(f"C5b{i}_{l}") for i in range(2)]
            psg = self.psum(f"C5pg_{l}", [8, 512], F32, ps)
            rpsg = self.R(f"C5pg_{l}")
            macc = self.sbuf(f"C5ma_{l}", [128, 16], F32, ps)
            rmacc = self.R(f"C5ma_{l}")
            s.step("sync", lambda e: e.dma_start(
                out=gsb[:], in_=self.w_gateT[l].rearrange("(c p) e -> p c e", p=128)),
                writes=[rgs], dma=True, tag=f"C5lg{l}")
            if l % 2 == 1:
                s.step("vector", lambda e: e.memset(macc[:], 0.0),
                       writes=[rmacc], tag=f"C5z{l}")
            glog = self.sbuf(f"C5gl_{l}", [8, T], F32, ps)
            rglog = self.R(f"C5gl_{l}")

            def hook(kc, tt, t0, ap, res):
                if l % 2 == 0:
                    s.step("tensor", lambda e, kc=kc, ap=ap: e.matmul(
                        psg[:], gsb[:, kc, :], ap[:],
                        start=(kc == 0), stop=(kc == NCH - 1)),
                        reads=[res, rgs], writes=[rpsg], tag=f"C5gm{l}_{tt}_{kc}")
                    if kc == NCH - 1:
                        s.step("vector", lambda e, t0=t0: e.tensor_scalar(
                            glog[:, t0 : t0 + 512], psg[:],
                            self.col(f"l0_gate_b_{l}")[0:8], None, ALU.add),
                            reads=[rpsg, self.R("smallv")], writes=[rglog],
                            tag=f"C5ge{l}_{tt}")
                else:
                    s.step("vector", lambda e, kc=kc, ap=ap: e.reduce_sum(
                        macc[:, 8 + kc : 9 + kc], ap[:], AX.X),
                        reads=[res], writes=[rmacc], tag=f"C5mr{l}_{tt}_{kc}")
                    s.step("vector", lambda e, kc=kc: e.tensor_tensor(
                        macc[:, kc : kc + 1], macc[:, kc : kc + 1],
                        macc[:, 8 + kc : 9 + kc], ALU.add),
                        reads=[rmacc], writes=[rmacc], tag=f"C5mx{l}_{tt}_{kc}")

            def sink(kc, tt, t0):
                return hb2[kc % 2][:], rhb2[kc % 2]

            self._c5_state = (hb2, rhb2)
            def sink2(kc, tt, t0):
                return hb2[kc % 2][:], rhb2[kc % 2]
            def hook2(kc, tt, t0, ap, res):
                hook(kc, tt, t0, ap, res)
                b2 = kc % 2
                s.step("sync", lambda e, b2=b2, kc=kc, t0=t0: e.dma_start(
                    out=self.sc_hn2[kc * 128 : (kc + 1) * 128, t0 : t0 + 512],
                    in_=hb2[b2][:]),
                    reads=[rhb2[b2]], writes=[self.R("sc_hn2")], dma=True,
                    tag=f"C5sb{l}_{tt}_{kc}")

            self.ln_fm(f"C5_{l}", h_src, (f"l0_ln2_w_{l}", f"l0_ln2_b_{l}"), sink2,
                       f32_hook=hook2)
            self._gate_cw(l, glog, rglog, macc, rmacc, gsb, rgs, ps)
        self._moe(l, h_src, h_dst)

    def _gate_cw(self, l, glog, rglog, macc, rmacc, gsb, rgs, ps):
        s = self.sc
        ltm = self.sbuf(f"G_ltm_{l}", [128, 16], F32, ps)
        rltm = self.R(f"G_ltm_{l}")
        cwtm = self.sbuf(f"G_cw_{l}", [128, 8], F32, ps)
        rcw = self.R(f"G_cw_{l}")
        cols = self.sbuf(f"G_c_{l}", [128, 8], F32, ps)
        rc = self.R(f"G_c_{l}")
        pst = self.psum(f"G_pt_{l}", [128, 16], F32, ps)
        rpt = self.R(f"G_pt_{l}")
        if l % 2 == 0:
            for g in range(NT1):
                g0 = g * 128
                s.step("tensor", lambda e, g0=g0: e.transpose(
                    pst[:, 0:8], glog[:, g0 : g0 + 128], self.ident_f[0:8, 0:8]),
                    reads=[rglog, self.R("consts")], writes=[rpt], tag=f"G_t{l}_{g}")
                s.step("vector", lambda e: e.tensor_copy(ltm[:, 0:8], pst[:, 0:8]),
                       reads=[rpt], writes=[rltm], tag=f"G_c0{l}_{g}")
                self._cw_from_logits(l, g, ltm, rltm, cwtm, rcw, cols, rc, 128, 8)
                s.step("gpsimd", lambda e, g0=g0: e.dma_start(
                    out=self.sc_row2.rearrange("o (e t) -> (o t) e", e=8)[g0 : g0 + 128, :],
                    in_=cwtm[:, 0:8], allow_slow_non_contiguous=True),
                    reads=[rcw], writes=[self.R("sc_row2")], dma=True, tag=f"G_d{l}_{g}")

        else:
            s.step("vector", lambda e: e.tensor_scalar_mul(macc[:, 0:8], macc[:, 0:8], 1.0 / T),
                   reads=[rmacc], writes=[rmacc], tag=f"G_mm{l}")
            psg2 = self.psum(f"G_pg_{l}", [8, 8], F32, ps)
            rpg2 = self.R(f"G_pg_{l}")
            def mm(t):
                ins = None
                for kc in range(NCH):
                    ins = t.matmul(psg2[:, 0:1], gsb[:, kc, :], macc[:, kc : kc + 1],
                                   start=(kc == 0), stop=(kc == NCH - 1))
                return ins
            s.step("tensor", mm, reads=[rmacc, rgs], writes=[rpg2], tag=f"G_gm{l}")
            s.step("vector", lambda e: e.tensor_scalar(
                ltm[0:8, 0:1], psg2[:, 0:1], self.col(f"l0_gate_b_{l}")[0:8], None, ALU.add),
                reads=[rpg2, self.R("smallv")], writes=[rltm], tag=f"G_ge{l}")
            s.step("tensor", lambda e: e.transpose(
                pst[0:1, 0:8], ltm[0:8, 0:1], self.ident_f[0:8, 0:8]),
                reads=[rltm, self.R("consts")], writes=[rpt], tag=f"G_t{l}")
            s.step("vector", lambda e: e.tensor_copy(ltm[0:1, 8:16], pst[0:1, 0:8]),
                   reads=[rpt], writes=[rltm], tag=f"G_c1{l}")
            self._cw_from_logits(l, 0, ltm, rltm, cwtm, rcw, cols, rc, 1, 8, src_off=8)
            s.step("tensor", lambda e: e.matmul(
                pst[:, 0:8], self.ones_r[:], cwtm[0:1, 0:8], start=True, stop=True),
                reads=[rcw, self.R("consts")], writes=[rpt], tag=f"G_b{l}")
            s.step("vector", lambda e: e.tensor_copy(self.cwcol_sb[:], pst[:, 0:8]),
                   reads=[rpt], writes=[self.r_cwcol], tag=f"G_c2{l}")
        s.barrier()

    def _cw_from_logits(self, l, g, ltm, rltm, cwtm, rcw, cols, rc, np_, ne, src_off=0):
        s = self.sc
        L = ltm[0:np_, src_off : src_off + ne]
        s.step("vector", lambda e: e.reduce_max(cols[0:np_, 0:1], L, AX.X),
               reads=[rltm], writes=[rc], tag=f"W_m1{l}_{g}")
        s.step("vector", lambda e: e.reduce_max(cols[0:np_, 1:2], L, AX.X, negate=True),
               reads=[rltm], writes=[rc], tag=f"W_nm1{l}_{g}")
        s.step("scalar", lambda e: e.activation(
            cwtm[0:np_, 0:ne], L, AF.Exp, bias=cols[0:np_, 1:2]),
            reads=[rltm, rc], writes=[rcw], tag=f"W_e{l}_{g}")
        s.step("vector", lambda e: e.tensor_scalar(
            ltm[0:np_, 8 : 8 + ne] if src_off == 0 else ltm[0:np_, 0:ne],
            L, cols[0:np_, 0:1], None, ALU.is_ge),
            reads=[rltm, rc], writes=[rltm], tag=f"W_eq{l}_{g}")
        ZT = ltm[0:np_, 8 : 8 + ne] if src_off == 0 else ltm[0:np_, 0:ne]
        s.step("vector", lambda e: e.scalar_tensor_tensor(
            ZT, ZT, -1e30, L, ALU.mult, ALU.add),
            reads=[rltm], writes=[rltm], tag=f"W_z{l}_{g}")
        s.step("vector", lambda e: e.reduce_max(cols[0:np_, 2:3], ZT, AX.X),
               reads=[rltm], writes=[rc], tag=f"W_m2{l}_{g}")
        s.step("vector", lambda e: e.tensor_scalar(
            ZT, L, cols[0:np_, 2:3], None, ALU.is_ge),
            reads=[rltm, rc], writes=[rltm], tag=f"W_sel{l}_{g}")
        s.step("vector", lambda e: e.tensor_tensor(
            cwtm[0:np_, 0:ne], cwtm[0:np_, 0:ne], ZT, ALU.mult),
            reads=[rcw, rltm], writes=[rcw], tag=f"W_w{l}_{g}")
        s.step("vector", lambda e: e.reduce_sum(cols[0:np_, 3:4], cwtm[0:np_, 0:ne], AX.X),
               reads=[rcw], writes=[rc], tag=f"W_d{l}_{g}")
        s.step("vector", lambda e: e.reciprocal(cols[0:np_, 1:2], cols[0:np_, 3:4]),
               reads=[rc], writes=[rc], tag=f"W_r{l}_{g}")
        s.step("vector", lambda e: e.tensor_scalar(
            cwtm[0:np_, 0:ne], cwtm[0:np_, 0:ne], cols[0:np_, 1:2], None, ALU.mult),
            reads=[rcw, rc], writes=[rcw], tag=f"W_f{l}_{g}")

    def _moe(self, l, h_src, h_dst):
        s = self.sc
        s.step("sync", lambda e: e.dma_start(out=h_dst, in_=h_src),
               reads=[self.R(h_src.tensor.name)], writes=[self.R(h_dst.tensor.name)],
               dma=True, tag=f"M_init{l}")
        with ExitStack() as ps:
            w1 = self.sbuf(f"M_w1_{l}", [128, NCH, DFF], BF16, ps)
            w2 = self.sbuf(f"M_w2_{l}", [128, 32, D], BF16, ps)
            rw1, rw2 = self.R(f"M_w1_{l}"), self.R(f"M_w2_{l}")
            rhs = [self.sbuf(f"M_r{i}_{l}", [128, NCH, 512], BF16, ps) for i in range(2)]
            rrhs = [self.R(f"M_r{i}_{l}") for i in range(2)]
            hid = self.sbuf(f"M_h_{l}", [128, 32, 512], BF16, ps)
            rhid = self.R(f"M_h_{l}")
            wout = self.sbuf(f"M_o_{l}", [128, NCH, 512], F32, ps)
            rwout = self.R(f"M_o_{l}")
            cwB = [self.sbuf(f"M_c{i}_{l}", [128, 512], F32, ps) for i in range(2)]
            rcwB = [self.R(f"M_c{i}_{l}") for i in range(2)]
            pp = [self.psum(f"M_p{i}_{l}", [128, 512], F32, ps) for i in range(3)]
            rpp = [self.R(f"M_p{i}_{l}") for i in range(3)]
            pq = [self.psum(f"M_q{i}_{l}", [128, 512], F32, ps) for i in range(2)]
            rpq = [self.R(f"M_q{i}_{l}") for i in range(2)]
            pcb = self.psum(f"M_cb_{l}", [128, 512], F32, ps)
            rpcb = self.R(f"M_cb_{l}")
            cwt = [self.sbuf(f"M_t{i}_{l}", [1, 512], F32, ps) for i in range(2)]
            rcwt = [self.R(f"M_t{i}_{l}") for i in range(2)]
            NE = 8

            def load_w(e):
                s.step("sync", lambda g, e=e: g.dma_start(
                    out=w1[:], in_=self.pcast["ew1"][l, e].rearrange("(c p) f -> p c f", p=128)),
                    reads=[self.R(f"pc_ew1_{l}_{e}")], writes=[rw1], dma=True, tag=f"M_lw1{l}_{e}")
                s.step("sync", lambda g, e=e: g.dma_start(
                    out=w2[:], in_=self.pcast["ew2"][l, e].rearrange("(c p) d -> p c d", p=128)),
                    reads=[self.R(f"pc_ew2_{l}_{e}")], writes=[rw2], dma=True, tag=f"M_lw2{l}_{e}")

            for e in range(NE):
                load_w(e)
                for tt in range(NT5):
                    t0 = tt * 512
                    b = tt % 2
                    self.fm_load(rhs[b][:], self.sc_hn2, t0, 512, [self.R("sc_hn2")],
                                 [rrhs[b]], f"M_lr{l}_{e}_{tt}")
                    if l % 2 == 0:
                        s.step("sync", lambda en, e=e, t0=t0, b=b: en.dma_start(
                            out=cwt[b][:], in_=self.sc_row2[0:1, e * T + t0 : e * T + t0 + 512]),
                            reads=[self.R("sc_row2")], writes=[rcwt[b]], dma=True,
                            tag=f"M_ct{l}_{e}_{tt}")
                        s.step("tensor", lambda en, b=b: en.matmul(
                            pcb[:], self.ones_r[:], cwt[b][:], start=True, stop=True),
                            reads=[rcwt[b], self.R("consts")], writes=[rpcb],
                            tag=f"M_cb{l}_{e}_{tt}")
                        s.step("vector", lambda en, b=b: en.tensor_copy(cwB[b][:], pcb[:]),
                               reads=[rpcb], writes=[rcwB[b]], tag=f"M_cc{l}_{e}_{tt}")
                    for fc in range(32):
                        pb = fc % 3
                        def mm1(t, fc=fc, b=b, pb=pb):
                            ins = None
                            for kc in range(NCH):
                                ins = t.matmul(pp[pb][:],
                                               w1[:, kc, fc * 128 : (fc + 1) * 128],
                                               rhs[b][:, kc, :],
                                               start=(kc == 0), stop=(kc == NCH - 1))
                            return ins
                        s.step("tensor", mm1, reads=[rw1, rrhs[b]], writes=[rpp[pb]],
                               tag=f"M_m1{l}_{e}_{tt}_{fc}")
                        s.step("scalar", lambda en, pb=pb, fc=fc, e=e: en.activation(
                            hid[:, fc, :], pp[pb][:], AF.Gelu,
                            bias=self.col(f"l0_eb1_{l}_{e}", fc)),
                            reads=[rpp[pb], self.R("smallv")], writes=[rhid],
                            tag=f"M_g{l}_{e}_{tt}_{fc}")
                    for dc in range(NCH):
                        qb = dc % 2
                        def mm2(t, dc=dc, qb=qb):
                            ins = None
                            for fc in range(32):
                                ins = t.matmul(pq[qb][:],
                                               w2[:, fc, dc * 128 : (dc + 1) * 128],
                                               hid[:, fc, :],
                                               start=(fc == 0), stop=(fc == 31))
                            return ins
                        s.step("tensor", mm2, reads=[rw2, rhid], writes=[rpq[qb]],
                               tag=f"M_m2{l}_{e}_{tt}_{dc}")
                        if l % 2 == 0:
                            s.step("vector", lambda en, qb=qb, dc=dc, b=b, e=e: en.scalar_tensor_tensor(
                                wout[:, dc, :], pq[qb][:], self.col(f"l0_eb2_{l}_{e}", dc),
                                cwB[b][:], ALU.add, ALU.mult),
                                reads=[rpq[qb], rcwB[b], self.R("smallv")], writes=[rwout],
                                tag=f"M_w{l}_{e}_{tt}_{dc}")
                        else:
                            s.step("vector", lambda en, qb=qb, dc=dc, e=e: en.tensor_scalar(
                                wout[:, dc, :], pq[qb][:], self.col(f"l0_eb2_{l}_{e}", dc),
                                self.cwcol_sb[:, e : e + 1], ALU.add, ALU.mult),
                                reads=[rpq[qb], self.r_cwcol, self.R("smallv")],
                                writes=[rwout], tag=f"M_w{l}_{e}_{tt}_{dc}")
                    self.fm_store(wout[:], h_dst, t0, 512, [rwout],
                                  [self.R(h_dst.tensor.name)], f"M_st{l}_{e}_{tt}",
                                  accum=True)
            s.barrier()

    def phase_D(self, h_src):
        s = self.sc
        with ExitStack() as ps:
            h8 = [self.sbuf(f"D_h{i}", [128, NCH, 512], F32, ps) for i in range(2)]
            s8 = [self.sbuf(f"D_s{i}", [128, NCH, 512], F32, ps) for i in range(2)]
            x8 = [self.sbuf(f"D_x{i}", [128, NCH, 512], F32, ps) for i in range(2)]
            gt = [self.sbuf(f"D_g{i}", [128, 512], F32, ps) for i in range(2)]
            o8 = [self.sbuf(f"D_o{i}", [128, NCH, 512], F32, ps) for i in range(2)]
            rh = [self.R(f"D_h{i}") for i in range(2)]
            rs8 = [self.R(f"D_s{i}") for i in range(2)]
            rx8 = [self.R(f"D_x{i}") for i in range(2)]
            rgt = [self.R(f"D_g{i}") for i in range(2)]
            ro8 = [self.R(f"D_o{i}") for i in range(2)]
            for tt in range(NT5):
                t0 = tt * 512
                b = tt % 2
                self.fm_load(h8[b][:], h_src, t0, 512, [self.R(h_src.tensor.name)], [rh[b]], f"D_lh{tt}")
                self.fm_load(s8[b][:], self.sc_s0, t0, 512, [self.R("sc_s0")], [rs8[b]], f"D_ls{tt}")
                self.fm_load(x8[b][:], self.sc_xfm, t0, 512, [self.R("sc_xfm")], [rx8[b]], f"D_lx{tt}")
                for kc in range(NCH):
                    qb = kc % 2
                    s.step("scalar", lambda e, b=b, kc=kc, qb=qb: e.activation(
                        gt[qb][:], h8[b][:, kc, :], AF.Silu),
                        reads=[rh[b]], writes=[rgt[qb]], tag=f"D_si{tt}_{kc}")
                    s.step("vector", lambda e, b=b, kc=kc, qb=qb: e.tensor_tensor(
                        gt[qb][:], gt[qb][:], s8[b][:, kc, :], ALU.mult),
                        reads=[rgt[qb], rs8[b]], writes=[rgt[qb]], tag=f"D_sg{tt}_{kc}")
                    s.step("vector", lambda e, b=b, kc=kc, qb=qb: e.tensor_tensor(
                        o8[b][:, kc, :], x8[b][:, kc, :], gt[qb][:], ALU.add),
                        reads=[rx8[b], rgt[qb]], writes=[ro8[b]], tag=f"D_x1{tt}_{kc}")
                self.fm_store(o8[b][:], self.sc_x1, t0, 512, [ro8[b]],
                              [self.R("sc_x1")], f"D_st{tt}")
            s.barrier()
        self.ln_fm("D2", self.sc_x1, None, None, store_b16=self.sc_xn1b,
                   store_f32=self.sc_xn1)

    def phase_E(self):
        self._e1()
        self._e2()
        self._e3()
        self._e4()
        self._e5()
        self._e6()
        self._e7()

    def _e1(self):
        s = self.sc
        with ExitStack() as ps:
            yb = self.sbuf("E1_yb", [128, NCH, T], BF16, ps)
            ryb = self.R("E1_yb")
            ps_in = ExitStack()
            xc = [self.sbuf(f"E1_x{i}", [128, T], F32, ps_in) for i in range(2)]
            rxc = [self.R(f"E1_x{i}") for i in range(2)]
            cb = [self.sbuf(f"E1_c{i}", [128, T], F32, ps_in) for i in range(3)]
            rcb = [self.R(f"E1_c{i}") for i in range(3)]
            for ec in range(NCH):
                b = ec % 2
                s.step("sync", lambda e, ec=ec, b=b: e.dma_start(
                    out=xc[b][:], in_=self.sc_xn1[ec * 128 : (ec + 1) * 128, :]),
                    reads=[self.R("sc_xn1")], writes=[rxc[b]], dma=True, tag=f"E1_lx{ec}")
                s.step("vector", lambda e, ec=ec, b=b: e.tensor_scalar(
                    cb[0][:], xc[b][:], self.col("w_dw", ec * 3 + 2), None, ALU.mult),
                    reads=[rxc[b], self.R("smallv")], writes=[rcb[0]], tag=f"E1_t2{ec}")
                s.step("vector", lambda e, ec=ec, b=b: e.scalar_tensor_tensor(
                    cb[1][:, 1:T], xc[b][:, 0 : T - 1], self.col("w_dw", ec * 3 + 1),
                    cb[0][:, 1:T], ALU.mult, ALU.add),
                    reads=[rxc[b], rcb[0], self.R("smallv")], writes=[rcb[1]], tag=f"E1_t1{ec}")
                s.step("vector", lambda e: e.tensor_copy(cb[1][:, 0:1], cb[0][:, 0:1]),
                       reads=[rcb[0]], writes=[rcb[1]], tag=f"E1_c1{ec}")
                s.step("vector", lambda e, ec=ec, b=b: e.scalar_tensor_tensor(
                    cb[2][:, 2:T], xc[b][:, 0 : T - 2], self.col("w_dw", ec * 3 + 0),
                    cb[1][:, 2:T], ALU.mult, ALU.add),
                    reads=[rxc[b], rcb[1], self.R("smallv")], writes=[rcb[2]], tag=f"E1_t0{ec}")
                s.step("vector", lambda e: e.tensor_copy(cb[2][:, 0:2], cb[1][:, 0:2]),
                       reads=[rcb[1]], writes=[rcb[2]], tag=f"E1_c0{ec}")
                s.step("scalar", lambda e, ec=ec: e.copy(yb[:, ec, :], cb[2][:]),
                       reads=[rcb[2]], writes=[ryb], tag=f"E1_cb{ec}")
            ps_in.close()
            s.barrier()
            wsb = self.sbuf("E1_w", [128, NCH, D], BF16, ps)
            rw = self.R("E1_w")
            xn8 = [self.sbuf(f"E1_n{i}", [128, NCH, 512], F32, ps) for i in range(2)]
            rxn8 = [self.R(f"E1_n{i}") for i in range(2)]
            xa8 = [self.sbuf(f"E1_a{i}", [128, NCH, 512], F32, ps) for i in range(2)]
            rxa8 = [self.R(f"E1_a{i}") for i in range(2)]
            ab = [self.sbuf(f"E1_ab{i}", [128, 512], BF16, ps) for i in range(2)]
            rab = [self.R(f"E1_ab{i}") for i in range(2)]
            pp = [self.psum(f"E1_p{i}", [128, 512], F32, ps) for i in range(2)]
            rpp = [self.R(f"E1_p{i}") for i in range(2)]
            s.step("sync", lambda e: e.dma_start(
                out=wsb[:], in_=self.pcast["pwT"].rearrange("(c p) e -> p c e", p=128)),
                reads=[self.R("pc_pwT")], writes=[rw], dma=True, tag="E1_lw")
            for tt in range(NT5):
                t0 = tt * 512
                b = tt % 2
                self.fm_load(xn8[b][:], self.sc_xn1, t0, 512, [self.R("sc_xn1")],
                             [rxn8[b]], f"E1_ln{tt}")
                for jc in range(NCH):
                    pb = jc % 2
                    def mm(t, jc=jc, t0=t0, pb=pb):
                        ins = None
                        for kc in range(NCH):
                            ins = t.matmul(pp[pb][:], wsb[:, kc, jc * 128 : (jc + 1) * 128],
                                           yb[:, kc, t0 : t0 + 512],
                                           start=(kc == 0), stop=(kc == NCH - 1))
                        return ins
                    s.step("tensor", mm, reads=[rw, ryb], writes=[rpp[pb]], tag=f"E1_mm{tt}_{jc}")
                    s.step("vector", lambda e, b=b, jc=jc, pb=pb: e.tensor_tensor(
                        xa8[b][:, jc, :], xn8[b][:, jc, :], pp[pb][:], ALU.add),
                        reads=[rxn8[b], rpp[pb]], writes=[rxa8[b]], tag=f"E1_ad{tt}_{jc}")
                    s.step("scalar", lambda e, b=b, jc=jc, pb=pb: e.copy(
                        ab[pb][:], xa8[b][:, jc, :]),
                        reads=[rxa8[b]], writes=[rab[pb]], tag=f"E1_cb2{tt}_{jc}")
                    s.step("sync", lambda e, jc=jc, t0=t0, pb=pb: e.dma_start(
                        out=self.sc_q[jc * 128 : (jc + 1) * 128, t0 : t0 + 512], in_=ab[pb][:]),
                        reads=[rab[pb]], writes=[self.R("sc_q")], dma=True, tag=f"E1_sb{tt}_{jc}")
                self.fm_store(xa8[b][:], self.sc_xa, t0, 512, [rxa8[b]],
                              [self.R("sc_xa")], f"E1_sa{tt}")
            s.barrier()
    def _e2(self):
        s = self.sc
        with ExitStack() as ps:
            wsb = self.sbuf("E2_w", [128, NCH, 3 * D], BF16, ps)
            rw = self.R("E2_w")
            r8 = [self.sbuf(f"E2_r{i}", [128, NCH, 512], BF16, ps) for i in range(2)]
            rr8 = [self.R(f"E2_r{i}") for i in range(2)]
            ob = [self.sbuf(f"E2_o{i}", [128, 512], BF16, ps) for i in range(3)]
            rob = [self.R(f"E2_o{i}") for i in range(3)]
            pp = [self.psum(f"E2_p{i}", [128, 512], F32, ps) for i in range(3)]
            rpp = [self.R(f"E2_p{i}") for i in range(3)]
            s.step("sync", lambda e: e.dma_start(
                out=wsb[:], in_=self.pcast["qkvT"].rearrange("(c p) e -> p c e", p=128)),
                reads=[self.R("pc_qkvT")], writes=[rw], dma=True, tag="E2_lw")
            dsts = [self.sc_o, self.sc_k, self.sc_v]
            for tt in range(NT5):
                t0 = tt * 512
                b = tt % 2
                self.fm_load(r8[b][:], self.sc_q, t0, 512, [self.R("sc_q")],
                             [rr8[b]], f"E2_lr{tt}")
                for jc in range(24):
                    pb = jc % 3
                    dst = dsts[jc // 8]
                    def mm(t, jc=jc, t0=t0, pb=pb, b=b):
                        ins = None
                        for kc in range(NCH):
                            ins = t.matmul(pp[pb][:], wsb[:, kc, jc * 128 : (jc + 1) * 128],
                                           r8[b][:, kc, :],
                                           start=(kc == 0), stop=(kc == NCH - 1))
                        return ins
                    s.step("tensor", mm, reads=[rw, rr8[b]], writes=[rpp[pb]], tag=f"E2_mm{tt}_{jc}")
                    s.step("scalar", lambda e, pb=pb: e.copy(ob[pb][:], pp[pb][:]),
                           reads=[rpp[pb]], writes=[rob[pb]], tag=f"E2_ev{tt}_{jc}")
                    s.step("sync", lambda e, jc=jc, t0=t0, pb=pb, dst=dst: e.dma_start(
                        out=dst[(jc % 8) * 128 : (jc % 8 + 1) * 128, t0 : t0 + 512],
                        in_=ob[pb][:]),
                        reads=[rob[pb]], writes=[self.R(dst.tensor.name)], dma=True,
                        tag=f"E2_st{tt}_{jc}")
            s.barrier()
    def _e3(self):
        s = self.sc
        with ExitStack() as ps:
            jm = self.sbuf("E3_j", [128, 128], BF16, ps)
            rjm = self.R("E3_j")
            rc_sb = self.sbuf("E3_rc", [128, T], F32, ps)
            rs_sb = self.sbuf("E3_rs", [128, T], F32, ps)
            rrc, rrs = self.R("E3_rc"), self.R("E3_rs")
            qc = [self.sbuf(f"E3_q{i}", [128, T], BF16, ps) for i in range(2)]
            rqc = [self.R(f"E3_q{i}") for i in range(2)]
            t1 = [self.sbuf(f"E3_t1{i}", [128, 512], F32, ps) for i in range(2)]
            t2 = [self.sbuf(f"E3_t2{i}", [128, 512], F32, ps) for i in range(2)]
            t3 = [self.sbuf(f"E3_t3{i}", [128, 512], F32, ps) for i in range(2)]
            t4 = [self.sbuf(f"E3_t4{i}", [128, 512], F32, ps) for i in range(2)]
            rt = [[self.R(f"E3_t{j}{i}") for i in range(2)] for j in range(1, 5)]
            pj = [self.psum(f"E3_p{i}", [128, 512], F32, ps) for i in range(2)]
            rpj = [self.R(f"E3_p{i}") for i in range(2)]
            s.step("gpsimd", lambda e: e.dma_start(out=jm[:], in_=self.w_jmat),
                   writes=[rjm], dma=True, tag="E3_lj")
            s.step("sync", lambda e: e.dma_start(out=rc_sb[:], in_=self.w_ropeC),
                   writes=[rrc], dma=True, tag="E3_lc")
            s.step("sync", lambda e: e.dma_start(out=rs_sb[:], in_=self.w_ropeS),
                   writes=[rrs], dma=True, tag="E3_ls")
            for src_i, srcd in enumerate((self.sc_o, self.sc_k)):
                rsrc = self.R(srcd.tensor.name)
                for hc in range(NCH):
                    b = hc % 2
                    s.step("sync", lambda e, hc=hc, b=b, srcd=srcd: e.dma_start(
                        out=qc[b][:], in_=srcd[hc * 128 : (hc + 1) * 128, :]),
                        reads=[rsrc], writes=[rqc[b]], dma=True, tag=f"E3_l{src_i}_{hc}")
                    for tt in range(NT5):
                        t0 = tt * 512
                        pb = tt % 2
                        s.step("tensor", lambda e, b=b, t0=t0, pb=pb: e.matmul(
                            pj[pb][:], jm[:], qc[b][:, t0 : t0 + 512], start=True, stop=True),
                            reads=[rjm, rqc[b]], writes=[rpj[pb]], tag=f"E3_mm{src_i}_{hc}_{tt}")
                        s.step("vector", lambda e, b=b, t0=t0, pb=pb: e.tensor_tensor(
                            t1[pb][:], qc[b][:, t0 : t0 + 512], rc_sb[:, t0 : t0 + 512], ALU.mult),
                            reads=[rqc[b], rrc], writes=[rt[0][pb]], tag=f"E3_a{src_i}_{hc}_{tt}")
                        s.step("vector", lambda e, t0=t0, pb=pb: e.tensor_tensor(
                            t2[pb][:], pj[pb][:], rs_sb[:, t0 : t0 + 512], ALU.mult),
                            reads=[rpj[pb], rrs], writes=[rt[1][pb]], tag=f"E3_b{src_i}_{hc}_{tt}")
                        s.step("vector", lambda e, pb=pb: e.tensor_tensor(
                            t3[pb][:], t1[pb][:], t2[pb][:], ALU.add),
                            reads=[rt[0][pb], rt[1][pb]], writes=[rt[2][pb]],
                            tag=f"E3_c{src_i}_{hc}_{tt}")
                        s.step("vector", lambda e, pb=pb: e.tensor_scalar_min(
                            t4[pb][:], t3[pb][:], 0.0),
                            reads=[rt[2][pb]], writes=[rt[3][pb]], tag=f"E3_d{src_i}_{hc}_{tt}")
                        s.step("vector", lambda e, pb=pb: e.tensor_tensor(
                            t1[pb][:], t3[pb][:], t4[pb][:], ALU.subtract),
                            reads=[rt[2][pb], rt[3][pb]], writes=[rt[0][pb]],
                            tag=f"E3_e{src_i}_{hc}_{tt}")
                        s.step("scalar", lambda e, pb=pb: e.activation(
                            t2[pb][:], t4[pb][:], AF.Exp),
                            reads=[rt[3][pb]], writes=[rt[1][pb]], tag=f"E3_f{src_i}_{hc}_{tt}")
                        s.step("vector", lambda e, b=b, t0=t0, pb=pb: e.tensor_tensor(
                            qc[b][:, t0 : t0 + 512], t1[pb][:], t2[pb][:], ALU.add),
                            reads=[rt[0][pb], rt[1][pb]], writes=[rqc[b]],
                            tag=f"E3_g{src_i}_{hc}_{tt}")
                    s.step("sync", lambda e, hc=hc, b=b, srcd=srcd: e.dma_start(
                        out=srcd[hc * 128 : (hc + 1) * 128, :], in_=qc[b][:]),
                        reads=[rqc[b]], writes=[rsrc], dma=True, tag=f"E3_s{src_i}_{hc}")
            s.barrier()
    def _e4(self):
        s = self.sc
        with ExitStack() as ps:
            qp = self.sbuf("E4_q", [64, T], BF16, ps)
            kp = self.sbuf("E4_k", [64, T], BF16, ps)
            vp = self.sbuf("E4_v", [64, T], BF16, ps)
            vext = self.sbuf("E4_vx", [128, 32 * 65], BF16, ps)
            kT = self.sbuf("E4_kT", [128, 32 * 64], BF16, ps)
            ssnap = self.sbuf("E4_ss", [64, 65], BF16, ps)
            aT = self.sbuf("E4_aT", [128, 128], BF16, ps)
            drow = self.sbuf("E4_dr", [1, 256], F32, ps)
            onum = self.sbuf("E4_on", [64, 128], F32, ps)
            obuf = self.sbuf("E4_ob", [64, T], BF16, ps)
            rqp, rkp, rvp = self.R("E4_q"), self.R("E4_k"), self.R("E4_v")
            rvx, rkT = self.R("E4_vx"), self.R("E4_kT")
            rss, raT = self.R("E4_ss"), self.R("E4_aT")
            rdr, ron, rob = self.R("E4_dr"), self.R("E4_on"), self.R("E4_ob")
            pA = self.psum("E4_pA", [128, 128], F32, ps)
            rpA = self.R("E4_pA")
            pO = [self.psum(f"E4_pO{i}", [128, 128], F32, ps) for i in range(2)]
            rpO = [self.R(f"E4_pO{i}") for i in range(2)]
            pST = self.psum("E4_pS", [128, 65], F32, ps)
            rpST = self.R("E4_pS")
            pvt = self.psum("E4_pv", [128, 64], BF16, ps)
            rpvt = self.R("E4_pv")
            pdb = self.psum("E4_pd", [128, 128], F32, ps)
            rpdb = self.R("E4_pd")
            for hd in range(NHEAD):
                r0 = hd * 64
                for nm, buf, rr, srcd in (("q", qp, rqp, self.sc_o), ("k", kp, rkp, self.sc_k),
                                          ("v", vp, rvp, self.sc_v)):
                    s.step("sync", lambda e, buf=buf, srcd=srcd, r0=r0: e.dma_start(
                        out=buf[:], in_=srcd[r0 : r0 + 64, :]),
                        reads=[self.R(srcd.tensor.name)], writes=[rr], dma=True,
                        tag=f"E4_l{hd}_{nm}")
                s.step("gpsimd", lambda e: e.memset(vext[:], 1.0),
                       writes=[rvx], tag=f"E4_o{hd}")
                for ki in range(32):
                    s.step("tensor", lambda e, ki=ki: e.transpose(
                        pvt[:], vp[:, ki * 128 : (ki + 1) * 128], self.ident_b[0:64, 0:64]),
                        reads=[rvp, self.R("consts")], writes=[rpvt], tag=f"E4_tv{hd}_{ki}")
                    s.step("vector", lambda e, ki=ki: e.tensor_copy(
                        vext[:, ki * 65 : ki * 65 + 64], pvt[:]),
                        reads=[rpvt], writes=[rvx], tag=f"E4_cv{hd}_{ki}")
                    s.step("tensor", lambda e, ki=ki: e.transpose(
                        pvt[:], kp[:, ki * 128 : (ki + 1) * 128], self.ident_b[0:64, 0:64]),
                        reads=[rkp, self.R("consts")], writes=[rpvt], tag=f"E4_tk{hd}_{ki}")
                    s.step("vector", lambda e, ki=ki: e.tensor_copy(
                        kT[:, ki * 64 : (ki + 1) * 64], pvt[:]),
                        reads=[rpvt], writes=[rkT], tag=f"E4_ck{hd}_{ki}")
                for ci in range(32):
                    c0 = ci * 128
                    ob_ = ci % 2
                    s.step("tensor", lambda e, c0=c0: e.matmul(
                        pA[:], kp[:, c0 : c0 + 128], qp[:, c0 : c0 + 128],
                        start=True, stop=True),
                        reads=[rkp, rqp], writes=[rpA], tag=f"E4_a{hd}_{ci}")
                    s.step("vector", lambda e: e.tensor_tensor(
                        aT[:], pA[:], self.triu_b[:], ALU.mult),
                        reads=[rpA, self.R("consts")], writes=[raT], tag=f"E4_am{hd}_{ci}")
                    if ci > 0:
                        s.step("vector", lambda e: e.tensor_copy(ssnap[:], pST[0:64, 0:65]),
                               reads=[rpST], writes=[rss], tag=f"E4_sn{hd}_{ci}")
                    s.step("tensor", lambda e, ci=ci, ob_=ob_: e.matmul(
                        pO[ob_][0:65, :], vext[:, ci * 65 : (ci + 1) * 65], aT[:],
                        start=True, stop=(ci == 0)),
                        reads=[rvx, raT], writes=[rpO[ob_]], tag=f"E4_o1{hd}_{ci}")
                    if ci > 0:
                        s.step("tensor", lambda e, c0=c0, ob_=ob_: e.matmul(
                            pO[ob_][0:65, :], ssnap[:], qp[:, c0 : c0 + 128],
                            start=False, stop=True),
                            reads=[rss, rqp], writes=[rpO[ob_]], tag=f"E4_o2{hd}_{ci}")
                    s.step("tensor", lambda e, ci=ci: e.matmul(
                        pST[0:64, 0:65], kT[:, ci * 64 : (ci + 1) * 64],
                        vext[:, ci * 65 : (ci + 1) * 65],
                        start=(ci == 0), stop=(ci == 31)),
                        reads=[rkT, rvx], writes=[rpST], tag=f"E4_su{hd}_{ci}")
                    s.step("vector", lambda e, ob_=ob_: e.tensor_copy(
                        drow[:, 0:128], pO[ob_][64:65, :]),
                        reads=[rpO[ob_]], writes=[rdr], tag=f"E4_d0{hd}_{ci}")
                    s.step("vector", lambda e: e.tensor_scalar_add(
                        drow[:, 0:128], drow[:, 0:128], 1e-6),
                        reads=[rdr], writes=[rdr], tag=f"E4_d1{hd}_{ci}")
                    s.step("vector", lambda e: e.reciprocal(drow[:, 128:256], drow[:, 0:128]),
                           reads=[rdr], writes=[rdr], tag=f"E4_d2{hd}_{ci}")
                    s.step("scalar", lambda e, ob_=ob_: e.copy(onum[:], pO[ob_][0:64, :]),
                           reads=[rpO[ob_]], writes=[ron], tag=f"E4_d3{hd}_{ci}")
                    s.step("tensor", lambda e: e.matmul(
                        pdb[0:64, :], self.ones_r[0:1, 0:64], drow[:, 128:256],
                        start=True, stop=True),
                        reads=[rdr, self.R("consts")], writes=[rpdb], tag=f"E4_d4{hd}_{ci}")
                    s.step("vector", lambda e, c0=c0: e.tensor_tensor(
                        obuf[:, c0 : c0 + 128], onum[:], pdb[0:64, :], ALU.mult),
                        reads=[ron, rpdb], writes=[rob], tag=f"E4_d5{hd}_{ci}")
                s.step("sync", lambda e, r0=r0: e.dma_start(
                    out=self.sc_hn2[r0 : r0 + 64, :], in_=obuf[:]),
                    reads=[rob], writes=[self.R("sc_hn2")], dma=True, tag=f"E4_st{hd}")
            s.barrier()
    def _e5(self):
        s = self.sc
        with ExitStack() as ps:
            dxb = self.sbuf("E5_dx", [128, NCH, T], BF16, ps)
            rdx = self.R("E5_dx")
            ps_in = ExitStack()
            xc = [self.sbuf(f"E5_x{i}", [128, T], F32, ps_in) for i in range(2)]
            rxc = [self.R(f"E5_x{i}") for i in range(2)]
            dt_ = self.sbuf("E5_d", [128, T], F32, ps_in)
            rdt = self.R("E5_d")
            for ec in range(NCH):
                b = ec % 2
                s.step("sync", lambda e, ec=ec, b=b: e.dma_start(
                    out=xc[b][:], in_=self.sc_xa[ec * 128 : (ec + 1) * 128, :]),
                    reads=[self.R("sc_xa")], writes=[rxc[b]], dma=True, tag=f"E5_lx{ec}")
                s.step("vector", lambda e, b=b: e.tensor_tensor(
                    dt_[:, 1:T], xc[b][:, 1:T], xc[b][:, 0 : T - 1], ALU.subtract),
                    reads=[rxc[b]], writes=[rdt], tag=f"E5_df{ec}")
                s.step("vector", lambda e: e.memset(dt_[:, 0:1], 0.0),
                       writes=[rdt], tag=f"E5_d0{ec}")
                s.step("scalar", lambda e, ec=ec: e.copy(dxb[:, ec, :], dt_[:]),
                       reads=[rdt], writes=[rdx], tag=f"E5_cb{ec}")
            ps_in.close()
            s.barrier()
            wo1 = self.sbuf("E5_w1", [128, NCH, D], BF16, ps)
            wg = self.sbuf("E5_w2", [128, NCH, D], BF16, ps)
            rwo1, rwg = self.R("E5_w1"), self.R("E5_w2")
            o8 = [self.sbuf(f"E5_o{i}", [128, NCH, 512], BF16, ps) for i in range(2)]
            ro8 = [self.R(f"E5_o{i}") for i in range(2)]
            xa8 = [self.sbuf(f"E5_a{i}", [128, NCH, 512], F32, ps) for i in range(2)]
            rxa8 = [self.R(f"E5_a{i}") for i in range(2)]
            gsb = [self.sbuf(f"E5_g{i}", [128, 512], F32, ps) for i in range(2)]
            rgsb = [self.R(f"E5_g{i}") for i in range(2)]
            tb = [self.sbuf(f"E5_t{i}", [128, 512], F32, ps) for i in range(2)]
            rtb = [self.R(f"E5_t{i}") for i in range(2)]
            xg8 = [self.sbuf(f"E5_xg{i}", [128, NCH, 512], F32, ps) for i in range(2)]
            rxg8 = [self.R(f"E5_xg{i}") for i in range(2)]
            pa = [self.psum(f"E5_pa{i}", [128, 512], F32, ps) for i in range(2)]
            rpa = [self.R(f"E5_pa{i}") for i in range(2)]
            pg = [self.psum(f"E5_pg{i}", [128, 512], F32, ps) for i in range(2)]
            rpg = [self.R(f"E5_pg{i}") for i in range(2)]
            s.step("sync", lambda e: e.dma_start(
                out=wo1[:], in_=self.pcast["woT1"].rearrange("(c p) e -> p c e", p=128)),
                reads=[self.R("pc_woT1")], writes=[rwo1], dma=True, tag="E5_lw1")
            s.step("sync", lambda e: e.dma_start(
                out=wg[:], in_=self.pcast["wgT"].rearrange("(c p) e -> p c e", p=128)),
                reads=[self.R("pc_wgT")], writes=[rwg], dma=True, tag="E5_lw2")
            for tt in range(NT5):
                t0 = tt * 512
                b = tt % 2
                self.fm_load(o8[b][:], self.sc_hn2, t0, 512, [self.R("sc_hn2")],
                             [ro8[b]], f"E5_lo{tt}")
                self.fm_load(xa8[b][:], self.sc_xa, t0, 512, [self.R("sc_xa")],
                             [rxa8[b]], f"E5_la{tt}")
                for jc in range(NCH):
                    pb = jc % 2
                    def mma(t, jc=jc, t0=t0, pb=pb, b=b):
                        ins = None
                        for kc in range(NCH):
                            ins = t.matmul(pa[pb][:], wo1[:, kc, jc * 128 : (jc + 1) * 128],
                                           o8[b][:, kc, :],
                                           start=(kc == 0), stop=(kc == NCH - 1))
                        return ins
                    s.step("tensor", mma, reads=[rwo1, ro8[b]], writes=[rpa[pb]],
                           tag=f"E5_ma{tt}_{jc}")
                    def mmg(t, jc=jc, t0=t0, pb=pb):
                        ins = None
                        for kc in range(NCH):
                            ins = t.matmul(pg[pb][:], wg[:, kc, jc * 128 : (jc + 1) * 128],
                                           dxb[:, kc, t0 : t0 + 512],
                                           start=(kc == 0), stop=(kc == NCH - 1))
                        return ins
                    s.step("tensor", mmg, reads=[rwg, rdx], writes=[rpg[pb]],
                           tag=f"E5_mg{tt}_{jc}")
                    s.step("scalar", lambda e, pb=pb, jc=jc: e.activation(
                        gsb[pb][:], pg[pb][:], AF.Sigmoid, bias=self.col("l1_b_gate", jc)),
                        reads=[rpg[pb], self.R("smallv")], writes=[rgsb[pb]],
                        tag=f"E5_sg{tt}_{jc}")
                    s.step("vector", lambda e, pb=pb: e.tensor_tensor(
                        tb[pb][:], gsb[pb][:], pa[pb][:], ALU.mult),
                        reads=[rgsb[pb], rpa[pb]], writes=[rtb[pb]], tag=f"E5_gm{tt}_{jc}")
                    s.step("vector", lambda e, b=b, jc=jc, pb=pb: e.tensor_tensor(
                        xg8[b][:, jc, :], xa8[b][:, jc, :], tb[pb][:], ALU.add),
                        reads=[rxa8[b], rtb[pb]], writes=[rxg8[b]], tag=f"E5_xg{tt}_{jc}")
                self.fm_store(xg8[b][:], self.sc_xg, t0, 512, [rxg8[b]],
                              [self.R("sc_xg")], f"E5_s1{tt}")
                self.fm_store(xg8[b][:], self.sc_s1, t0, 512, [rxg8[b]],
                              [self.R("sc_s1")], f"E5_s2{tt}")
            s.barrier()
    def _e6(self):
        s = self.sc
        with ExitStack() as ps2:
            mg = self.sbuf("E6_mg", [128, NCH, 4], F32, ps2)
            rmg = self.R("E6_mg")
            glog = self.sbuf("E6_gl", [4, T], F32, ps2)
            rglog = self.R("E6_gl")
            psg = self.psum("E6_pg", [4, 512], F32, ps2)
            rpsg = self.R("E6_pg")
            s.step("sync", lambda e: e.dma_start(
                out=mg[:], in_=self.w_mgT.rearrange("(c p) e -> p c e", p=128)),
                writes=[rmg], dma=True, tag="E6_lg")

            def hook(kc, tt, t0, ap, res):
                s.step("tensor", lambda e, kc=kc, ap=ap: e.matmul(
                    psg[:], mg[:, kc, :], ap[:],
                    start=(kc == 0), stop=(kc == NCH - 1)),
                    reads=[res, rmg], writes=[rpsg], tag=f"E6_gm{tt}_{kc}")
                if kc == NCH - 1:
                    s.step("vector", lambda e, t0=t0: e.tensor_copy(
                        glog[:, t0 : t0 + 512], psg[:]),
                        reads=[rpsg], writes=[rglog], tag=f"E6_ge{tt}")

            self.ln_fm("E6", self.sc_xg, ("l1_ln_g", "l1_ln_b"), None,
                       store_b16=self.sc_o, f32_hook=hook)
            ltm = self.sbuf("E6_ltm", [128, 8], F32, ps2)
            rltm = self.R("E6_ltm")
            cols = self.sbuf("E6_c", [128, 4], F32, ps2)
            rc = self.R("E6_c")
            gw = self.sbuf("E6_w", [128, 4], F32, ps2)
            rgw = self.R("E6_w")
            pst = self.psum("E6_pt", [128, 8], F32, ps2)
            rpt = self.R("E6_pt")
            for g in range(NT1):
                g0 = g * 128
                s.step("tensor", lambda e, g0=g0: e.transpose(
                    pst[:, 0:4], glog[:, g0 : g0 + 128], self.ident_f[0:4, 0:4]),
                    reads=[rglog, self.R("consts")], writes=[rpt], tag=f"E6_t{g}")
                s.step("vector", lambda e: e.tensor_copy(ltm[:, 0:4], pst[:, 0:4]),
                       reads=[rpt], writes=[rltm], tag=f"E6_c0{g}")
                s.step("vector", lambda e: e.reduce_max(cols[:, 0:1], ltm[:, 0:4], AX.X, negate=True),
                       reads=[rltm], writes=[rc], tag=f"E6_m{g}")
                s.step("scalar", lambda e: e.activation(
                    gw[:], ltm[:, 0:4], AF.Exp, bias=cols[:, 0:1]),
                    reads=[rltm, rc], writes=[rgw], tag=f"E6_e{g}")
                s.step("vector", lambda e: e.reduce_sum(cols[:, 1:2], gw[:], AX.X),
                       reads=[rgw], writes=[rc], tag=f"E6_s{g}")
                s.step("vector", lambda e: e.reciprocal(cols[:, 2:3], cols[:, 1:2]),
                       reads=[rc], writes=[rc], tag=f"E6_r{g}")
                s.step("vector", lambda e: e.tensor_scalar(
                    gw[:], gw[:], cols[:, 2:3], None, ALU.mult),
                    reads=[rgw, rc], writes=[rgw], tag=f"E6_f{g}")
                s.step("gpsimd", lambda e, g0=g0: e.dma_start(
                    out=self.sc_row2.rearrange("o (e t) -> (o t) e", e=8)[g0 : g0 + 128, 0:4],
                    in_=gw[:], allow_slow_non_contiguous=True),
                    reads=[rgw], writes=[self.R("sc_row2")], dma=True, tag=f"E6_d{g}")
            s.barrier()
    def _e7(self):
        self._moe_generic("E7", 4, lambda e: (self.pcast["l1w1T"][e], f"pc_l1w1T_{e}"),
                          lambda e: (self.pcast["l1w2T"][e], f"pc_l1w2T_{e}"),
                          AF.Silu, lambda e: f"l1_b1_{e}", lambda e: f"l1_b2_{e}",
                          self.sc_o, self.sc_s1, cw="row", init_src=None)

    def _moe_generic(self, tag, NE, w1src, w2src, act, b1name, b2name, rhs_src, dst,
                     cw="row", init_src=None):
        s = self.sc
        if init_src is not None:
            s.step("sync", lambda e: e.dma_start(out=dst, in_=init_src),
                   reads=[self.R(init_src.tensor.name)], writes=[self.R(dst.tensor.name)],
                   dma=True, tag=f"{tag}_init")
        with ExitStack() as ps:
            w1 = self.sbuf(f"{tag}_w1", [128, NCH, DFF], BF16, ps)
            w2 = self.sbuf(f"{tag}_w2", [128, 32, D], BF16, ps)
            rw1, rw2 = self.R(f"{tag}_w1"), self.R(f"{tag}_w2")
            rhs = [self.sbuf(f"{tag}_r{i}", [128, NCH, 512], BF16, ps) for i in range(2)]
            rrhs = [self.R(f"{tag}_r{i}") for i in range(2)]
            hid = self.sbuf(f"{tag}_h", [128, 32, 512], BF16, ps)
            rhid = self.R(f"{tag}_h")
            wout = self.sbuf(f"{tag}_o", [128, NCH, 512], F32, ps)
            rwout = self.R(f"{tag}_o")
            cwB = [self.sbuf(f"{tag}_c{i}", [128, 512], F32, ps) for i in range(2)]
            rcwB = [self.R(f"{tag}_c{i}") for i in range(2)]
            cwt = [self.sbuf(f"{tag}_t{i}", [1, 512], F32, ps) for i in range(2)]
            rcwt = [self.R(f"{tag}_t{i}") for i in range(2)]
            pp = [self.psum(f"{tag}_p{i}", [128, 512], F32, ps) for i in range(3)]
            rpp = [self.R(f"{tag}_p{i}") for i in range(3)]
            pq = [self.psum(f"{tag}_q{i}", [128, 512], F32, ps) for i in range(2)]
            rpq = [self.R(f"{tag}_q{i}") for i in range(2)]
            pcb = self.psum(f"{tag}_cb", [128, 512], F32, ps)
            rpcb = self.R(f"{tag}_cb")
            rdst = self.R(dst.tensor.name)
            rsrc = self.R(rhs_src.tensor.name)

            for e in range(NE):
                w1ap, k1 = w1src(e)
                w2ap, k2 = w2src(e)
                s.step("sync", lambda g, w1ap=w1ap: g.dma_start(
                    out=w1[:], in_=w1ap.rearrange("(c p) f -> p c f", p=128)),
                    reads=[self.R(k1)], writes=[rw1], dma=True, tag=f"{tag}_lw1_{e}")
                s.step("sync", lambda g, w2ap=w2ap: g.dma_start(
                    out=w2[:], in_=w2ap.rearrange("(c p) d -> p c d", p=128)),
                    reads=[self.R(k2)], writes=[rw2], dma=True, tag=f"{tag}_lw2_{e}")
                for tt in range(NT5):
                    t0 = tt * 512
                    b = tt % 2
                    self.fm_load(rhs[b][:], rhs_src, t0, 512, [rsrc], [rrhs[b]],
                                 f"{tag}_lr{e}_{tt}")
                    if cw == "row":
                        s.step("sync", lambda en, e=e, t0=t0, b=b: en.dma_start(
                            out=cwt[b][:], in_=self.sc_row2[0:1, e * T + t0 : e * T + t0 + 512]),
                            reads=[self.R("sc_row2")], writes=[rcwt[b]], dma=True,
                            tag=f"{tag}_ct{e}_{tt}")
                        s.step("tensor", lambda en, b=b: en.matmul(
                            pcb[:], self.ones_r[:], cwt[b][:], start=True, stop=True),
                            reads=[rcwt[b], self.R("consts")], writes=[rpcb],
                            tag=f"{tag}_cb{e}_{tt}")
                        s.step("vector", lambda en, b=b: en.tensor_copy(cwB[b][:], pcb[:]),
                               reads=[rpcb], writes=[rcwB[b]], tag=f"{tag}_cc{e}_{tt}")
                    for fc in range(32):
                        pb = fc % 3
                        def mm1(t, fc=fc, b=b, pb=pb):
                            ins = None
                            for kc in range(NCH):
                                ins = t.matmul(pp[pb][:],
                                               w1[:, kc, fc * 128 : (fc + 1) * 128],
                                               rhs[b][:, kc, :],
                                               start=(kc == 0), stop=(kc == NCH - 1))
                            return ins
                        s.step("tensor", mm1, reads=[rw1, rrhs[b]], writes=[rpp[pb]],
                               tag=f"{tag}_m1{e}_{tt}_{fc}")
                        s.step("scalar", lambda en, pb=pb, fc=fc, e=e: en.activation(
                            hid[:, fc, :], pp[pb][:], act,
                            bias=self.col(b1name(e), fc)),
                            reads=[rpp[pb], self.R("smallv")], writes=[rhid],
                            tag=f"{tag}_g{e}_{tt}_{fc}")
                    for dc in range(NCH):
                        qb = dc % 2
                        def mm2(t, dc=dc, qb=qb):
                            ins = None
                            for fc in range(32):
                                ins = t.matmul(pq[qb][:],
                                               w2[:, fc, dc * 128 : (dc + 1) * 128],
                                               hid[:, fc, :],
                                               start=(fc == 0), stop=(fc == 31))
                            return ins
                        s.step("tensor", mm2, reads=[rw2, rhid], writes=[rpq[qb]],
                               tag=f"{tag}_m2{e}_{tt}_{dc}")
                        if cw == "row":
                            s.step("vector", lambda en, qb=qb, dc=dc, b=b, e=e: en.scalar_tensor_tensor(
                                wout[:, dc, :], pq[qb][:], self.col(b2name(e), dc),
                                cwB[b][:], ALU.add, ALU.mult),
                                reads=[rpq[qb], rcwB[b], self.R("smallv")], writes=[rwout],
                                tag=f"{tag}_w{e}_{tt}_{dc}")
                        else:
                            s.step("vector", lambda en, qb=qb, dc=dc, e=e: en.tensor_scalar(
                                wout[:, dc, :], pq[qb][:], self.col(b2name(e), dc),
                                None, ALU.add),
                                reads=[rpq[qb], self.R("smallv")], writes=[rwout],
                                tag=f"{tag}_w{e}_{tt}_{dc}")
                    self.fm_store(wout[:], dst, t0, 512, [rwout], [rdst],
                                  f"{tag}_st{e}_{tt}", accum=(cw != "none"))
            s.barrier()

    def phase_F(self):
        s = self.sc
        self._moe_generic("F1", 1, lambda e: (self.pcast["fW1"], "pc_fW1"),
                          lambda e: (self.pcast["fW2"], "pc_fW2"),
                          AF.Gelu, lambda e: "f_b1", lambda e: "f_b2",
                          self.sc_xn1b, self.sc_blk, cw="none")
        with ExitStack() as ps:
            x8 = [self.sbuf(f"F2_x{i}", [128, NCH, 512], F32, ps) for i in range(2)]
            rx8 = [self.R(f"F2_x{i}") for i in range(2)]
            wrow = self.sbuf("F2_w", [1, T], F32, ps)
            rwrow = self.R("F2_w")
            psr = self.psum("F2_p", [1, 512], F32, ps)
            rpsr = self.R("F2_p")
            for tt in range(NT5):
                t0 = tt * 512
                b = tt % 2
                self.fm_load(x8[b][:], self.sc_xn1, t0, 512, [self.R("sc_xn1")],
                             [rx8[b]], f"F2_l{tt}")
                def mm(t, b=b):
                    ins = None
                    for kc in range(NCH):
                        ins = t.matmul(psr[:], self.col("f_wr", kc), x8[b][:, kc, :],
                                       start=(kc == 0), stop=(kc == NCH - 1))
                    return ins
                s.step("tensor", mm, reads=[rx8[b], self.R("smallv")], writes=[rpsr],
                       tag=f"F2_m{tt}")
                s.step("vector", lambda e, t0=t0: e.tensor_scalar(
                    wrow[:, t0 : t0 + 512], psr[:], self.col("f_b_r")[0:1], None, ALU.add),
                    reads=[rpsr, self.R("smallv")], writes=[rwrow], tag=f"F2_e{tt}")
            wB = self.sbuf("F3_wB", [128, T], F32, ps)
            rwB = self.R("F3_wB")
            wpart = self.sbuf("F3_wp", [128, 32], F32, ps)
            rwp = self.R("F3_wp")
            cnt = self.sbuf("F3_c", [128, 32], F32, ps)
            rcnt = self.R("F3_c")
            dmy = self.sbuf("F3_d", [128, T], F32, ps)
            rdmy = self.R("F3_d")
            psb = self.psum("F3_p", [128, 512], F32, ps)
            rpsb = self.R("F3_p")
            for tt in range(NT5):
                t0 = tt * 512
                s.step("tensor", lambda e, t0=t0: e.matmul(
                    psb[:], self.ones_r[:], wrow[:, t0 : t0 + 512], start=True, stop=True),
                    reads=[rwrow, self.R("consts")], writes=[rpsb], tag=f"F3_b{tt}")
                s.step("vector", lambda e, t0=t0: e.tensor_copy(wB[:, t0 : t0 + 512], psb[:]),
                       reads=[rpsb], writes=[rwB], tag=f"F3_c{tt}")
            s.step("sync", lambda e: e.dma_start(out=self.sc_row[0:1, :], in_=wrow[:]),
                   reads=[rwrow], writes=[self.R("sc_row")], dma=True, tag="F3_s0")
            s.step("sync", lambda e: e.dma_start(
                out=wpart[:], in_=self.sc_row[0:1, :].rearrange("o (p c) -> (o p) c", p=128)),
                reads=[self.R("sc_row")], writes=[rwp], dma=True, tag="F3_l0")
            for c in range(32):
                s.step("vector", lambda e, c=c: e.tensor_scalar(
                    dmy[:], wB[:], wpart[:, c : c + 1], None, ALU.is_gt),
                    reads=[rwB, rwp], writes=[rdmy], tag=f"F3_g{c}")
                s.step("vector", lambda e, c=c: e.reduce_sum(cnt[:, c : c + 1], dmy[:], AX.X),
                       reads=[rdmy], writes=[rcnt], tag=f"F3_r{c}")
            s.step("vector", lambda e: e.tensor_scalar(
                cnt[:], cnt[:], 2046.0, None, ALU.is_le),
                reads=[rcnt], writes=[rcnt], tag="F3_sel")
            s.step("sync", lambda e: e.dma_start(
                out=self.sc_row[1:2, :].rearrange("o (p c) -> (o p) c", p=128), in_=cnt[:]),
                reads=[rcnt], writes=[self.R("sc_row")], dma=True, tag="F3_s1")
            s.barrier()
        with ExitStack() as ps:
            b8 = [self.sbuf(f"F4_b{i}", [128, NCH, 512], F32, ps) for i in range(2)]
            n8 = [self.sbuf(f"F4_n{i}", [128, NCH, 512], F32, ps) for i in range(2)]
            s8 = [self.sbuf(f"F4_s{i}", [128, NCH, 512], F32, ps) for i in range(2)]
            xx8 = [self.sbuf(f"F4_x{i}", [128, NCH, 512], F32, ps) for i in range(2)]
            rb8 = [self.R(f"F4_b{i}") for i in range(2)]
            rn8 = [self.R(f"F4_n{i}") for i in range(2)]
            rs8 = [self.R(f"F4_s{i}") for i in range(2)]
            rxx8 = [self.R(f"F4_x{i}") for i in range(2)]
            mrow = [self.sbuf(f"F4_m{i}", [1, 512], F32, ps) for i in range(2)]
            rmrow = [self.R(f"F4_m{i}") for i in range(2)]
            t1 = [self.sbuf(f"F4_t1{i}", [128, 512], F32, ps) for i in range(2)]
            rt1 = [self.R(f"F4_t1{i}") for i in range(2)]
            t2 = [self.sbuf(f"F4_t2{i}", [128, 512], F32, ps) for i in range(2)]
            rt2 = [self.R(f"F4_t2{i}") for i in range(2)]
            of8 = [self.sbuf(f"F4_o{i}", [128, NCH, 512], F32, ps) for i in range(2)]
            rof8 = [self.R(f"F4_o{i}") for i in range(2)]
            outT = [self.sbuf(f"F4_T{i}", [128, D], F32, ps) for i in range(2)]
            routT = [self.R(f"F4_T{i}") for i in range(2)]
            pmb = self.psum("F4_pm", [128, 512], F32, ps)
            rpmb = self.R("F4_pm")
            ptp = [self.psum(f"F4_pt{i}", [128, 128], F32, ps) for i in range(2)]
            rptp = [self.R(f"F4_pt{i}") for i in range(2)]
            for tt in range(NT5):
                t0 = tt * 512
                b = tt % 2
                self.fm_load(b8[b][:], self.sc_blk, t0, 512, [self.R("sc_blk")], [rb8[b]], f"F4_lb{tt}")
                self.fm_load(n8[b][:], self.sc_xn1, t0, 512, [self.R("sc_xn1")], [rn8[b]], f"F4_ln{tt}")
                self.fm_load(s8[b][:], self.sc_s1, t0, 512, [self.R("sc_s1")], [rs8[b]], f"F4_ls{tt}")
                self.fm_load(xx8[b][:], self.sc_x1, t0, 512, [self.R("sc_x1")], [rxx8[b]], f"F4_lx{tt}")
                s.step("sync", lambda e, t0=t0, b=b: e.dma_start(
                    out=mrow[b][:], in_=self.sc_row[1:2, t0 : t0 + 512]),
                    reads=[self.R("sc_row")], writes=[rmrow[b]], dma=True, tag=f"F4_lm{tt}")
                s.step("tensor", lambda e, b=b: e.matmul(
                    pmb[:], self.ones_r[:], mrow[b][:], start=True, stop=True),
                    reads=[rmrow[b], self.R("consts")], writes=[rpmb], tag=f"F4_bm{tt}")
                for kc in range(NCH):
                    qb = kc % 2
                    s.step("vector", lambda e, b=b, kc=kc, qb=qb: e.tensor_tensor(
                        t1[qb][:], b8[b][:, kc, :], n8[b][:, kc, :], ALU.subtract),
                        reads=[rb8[b], rn8[b]], writes=[rt1[qb]], tag=f"F4_d{tt}_{kc}")
                    s.step("vector", lambda e, qb=qb: e.tensor_tensor(
                        t1[qb][:], t1[qb][:], pmb[:], ALU.mult),
                        reads=[rt1[qb], rpmb], writes=[rt1[qb]], tag=f"F4_m{tt}_{kc}")
                    s.step("vector", lambda e, b=b, kc=kc, qb=qb: e.tensor_tensor(
                        t1[qb][:], t1[qb][:], n8[b][:, kc, :], ALU.add),
                        reads=[rt1[qb], rn8[b]], writes=[rt1[qb]], tag=f"F4_f{tt}_{kc}")
                    s.step("scalar", lambda e, qb=qb: e.activation(
                        t2[qb][:], t1[qb][:], AF.Silu),
                        reads=[rt1[qb]], writes=[rt2[qb]], tag=f"F4_si{tt}_{kc}")
                    s.step("vector", lambda e, b=b, kc=kc, qb=qb: e.tensor_tensor(
                        t2[qb][:], t2[qb][:], s8[b][:, kc, :], ALU.mult),
                        reads=[rt2[qb], rs8[b]], writes=[rt2[qb]], tag=f"F4_sm{tt}_{kc}")
                    s.step("vector", lambda e, b=b, kc=kc, qb=qb: e.tensor_tensor(
                        of8[b][:, kc, :], xx8[b][:, kc, :], t2[qb][:], ALU.add),
                        reads=[rxx8[b], rt2[qb]], writes=[rof8[b]], tag=f"F4_o{tt}_{kc}")
                for g in range(4):
                    gb = g % 2
                    for kc in range(NCH):
                        pb = kc % 2
                        s.step("tensor", lambda e, b=b, kc=kc, g=g, pb=pb: e.transpose(
                            ptp[pb][:], of8[b][:, kc, g * 128 : (g + 1) * 128], self.ident_f[:]),
                            reads=[rof8[b], self.R("consts")], writes=[rptp[pb]],
                            tag=f"F4_tp{tt}_{g}_{kc}")
                        s.step("vector", lambda e, kc=kc, gb=gb, pb=pb: e.tensor_copy(
                            outT[gb][:, kc * 128 : (kc + 1) * 128], ptp[pb][:]),
                            reads=[rptp[pb]], writes=[routT[gb]], tag=f"F4_tc{tt}_{g}_{kc}")
                    s.step("sync", lambda e, t0=t0, g=g, gb=gb: e.dma_start(
                        out=self.out[t0 + g * 128 : t0 + (g + 1) * 128, :], in_=outT[gb][:]),
                        reads=[routT[gb]], writes=[self.R("out")], dma=True,
                        tag=f"F4_so{tt}_{g}")
            s.barrier()

    def finalize(self):
        s = self.sc
        s.barrier()
        s.step("gpsimd", lambda e: e.engine_nop(), tag="fin")
        blk = self.es.enter_context(self.nc.Block())
        s.finalize(blk)
        self.es.close()
        return self.nc

    def build(self):
        self.phase_A()
        if self.phases >= 2:
            self.phase_B()
        if self.phases >= 3:
            self.phase_C(0, self.sc_h, self.sc_moe)
        if self.phases >= 4:
            self.phase_C(1, self.sc_moe, self.sc_h)
        if self.phases >= 5:
            self.phase_D(self.sc_h)
        if self.phases >= 6:
            self.phase_E()
        if self.phases >= 7:
            self.phase_F()
        return self.finalize()



def kernel(**inputs):
    x = np.ascontiguousarray(np.asarray(inputs["L0_attn_inputs"], np.float32))
    w = host_prep(inputs)
    kb = KB()
    nc = kb.build()
    in_maps = []
    for core in range(8):
        m = {"in_" + k: v for k, v in w.items()}
        m["in_x"] = np.ascontiguousarray(x[core])
        in_maps.append(m)
    res = run_bass_kernel_spmd(nc, in_maps, core_ids=list(range(8)))
    return np.stack([r["out"] for r in res.results], axis=0)


# version 06 (final)
# speedup vs baseline: 1.1355x; 1.1355x over previous
import numpy as np
from contextlib import ExitStack

import concourse.bass as bass
import concourse.mybir as mybir
from concourse.bass_utils import run_bass_kernel_spmd

F32 = mybir.dt.float32
BF16 = mybir.dt.bfloat16
AF = mybir.ActivationFunctionType
ALU = mybir.AluOpType
AX = mybir.AxisListType

T = 4096
D = 1024
NCH = 8
NT5 = 8
NT1 = 32
DFF = 4096
NHEAD = 16
HDIM = 64

ENGINES = ("tensor", "vector", "scalar", "sync", "gpsimd")
SEM_ROLLOVER = 48000


class Step:
    __slots__ = ("engine", "fn", "deps", "amount", "needs_inc", "sem", "val", "tag")

    def __init__(self, engine, fn, deps, amount, tag):
        self.engine = engine
        self.fn = fn
        self.deps = deps
        self.amount = amount
        self.needs_inc = False
        self.sem = None
        self.val = None
        self.tag = tag


class Res:
    __slots__ = ("writer", "readers", "name")

    def __init__(self, name=""):
        self.writer = None
        self.readers = []
        self.name = name


class Sched:
    def __init__(self, nc, es):
        self.nc = nc
        self.es = es
        self.steps = {e: [] for e in ENGINES}
        self.pending = {e: [] for e in ENGINES}

    def step(self, engine, fn, reads=(), writes=(), deps=(), dma=False, tag=""):
        d = list(deps) + self.pending[engine]
        self.pending[engine] = []
        for r in reads:
            if r.writer is not None:
                d.append(r.writer)
        for w in writes:
            d.extend(w.readers)
            if w.writer is not None:
                d.append(w.writer)
        st = Step(engine, fn, [t for t in d if t is not None], 16 if dma else 1, tag)
        if dma:
            st.needs_inc = True
        self.steps[engine].append(st)
        for t in st.deps:
            if not (t.engine == engine == "tensor"):
                t.needs_inc = True
        for r in reads:
            r.readers.append(st)
        for w in writes:
            w.writer = st
            w.readers = []
        return st

    marker_buf = None
    def barrier(self, mark=None):
        last = {e: (self.steps[e][-1] if self.steps[e] else None) for e in ENGINES}
        for e in ENGINES:
            self.pending[e] = [last[o] for o in ENGINES if o != e and last[o] is not None]
        if mark is not None and self.marker_buf is not None:
            mb = self.marker_buf
            self.step("gpsimd", lambda g: g.memset(mb[0:1, 0:8], 3.0),
                      tag=f"MARK_{mark}")

    def finalize(self, block):
        sems = {}
        for e in ENGINES:
            cnt, si = 0, 0
            sem = self.es.enter_context(self.nc.semaphore(f"S_{e}_{si}"))
            for st in self.steps[e]:
                if st.needs_inc:
                    if cnt + st.amount > SEM_ROLLOVER:
                        si += 1
                        sem = self.es.enter_context(self.nc.semaphore(f"S_{e}_{si}"))
                        cnt = 0
                    cnt += st.amount
                    st.sem = sem
                    st.val = cnt

        def mk_body(e):
            def body(eng):
                seen = {}
                for st in self.steps[e]:
                    for t in st.deps:
                        if t.engine == e == "tensor":
                            continue
                        assert t.sem is not None, f"dep of {st.tag} on {t.tag} missing inc"
                        if seen.get(t.sem, 0) >= t.val:
                            continue
                        eng.wait_ge(t.sem, t.val)
                        seen[t.sem] = t.val
                    ins = st.fn(eng)
                    if st.sem is not None:
                        assert ins is not None, f"step {st.tag} returned no instruction"
                        ins.then_inc(st.sem, st.amount)
            return body

        for e in ENGINES:
            getattr(block, e)(mk_body(e))



def _col_map():
    cols = {}
    pos = 0

    def add(name, n):
        nonlocal pos
        cols[name] = pos
        pos += n

    add("ba", 8); add("bb", 8); add("bc", 8)
    for l in range(2):
        for nm in ("ln1_w", "ln1_b", "ln2_w", "ln2_b", "bq", "bk", "bv", "bo"):
            add(f"l0_{nm}_{l}", 8)
        add(f"l0_gate_b_{l}", 1)
        for e in range(8):
            add(f"l0_eb1_{l}_{e}", 32)
            add(f"l0_eb2_{l}_{e}", 8)
    add("l1_b_gate", 8); add("l1_ln_g", 8); add("l1_ln_b", 8)
    for e in range(4):
        add(f"l1_b1_{e}", 32)
        add(f"l1_b2_{e}", 8)
    add("f_b_r", 1); add("f_b1", 32); add("f_b2", 8)
    add("f_wr", 8)
    add("w_dw", 24)
    return cols, pos


def _pack_small(inp):
    cols, total = _col_map()
    sm = np.zeros((128, total), np.float32)

    def put(name, vec):
        v = np.asarray(vec, np.float32).reshape(-1)
        n = (v.size + 127) // 128
        c0 = cols[name]
        buf = np.zeros(n * 128, np.float32)
        buf[: v.size] = v
        sm[:, c0 : c0 + n] = buf.reshape(n, 128).T

    put("ba", inp["L0_attn_ba"]); put("bb", inp["L0_attn_bb"]); put("bc", inp["L0_attn_bc"])
    for l in range(2):
        put(f"l0_ln1_w_{l}", inp["L0_ffn_ln1_w"][l]); put(f"l0_ln1_b_{l}", inp["L0_ffn_ln1_b"][l])
        put(f"l0_ln2_w_{l}", inp["L0_ffn_ln2_w"][l]); put(f"l0_ln2_b_{l}", inp["L0_ffn_ln2_b"][l])
        put(f"l0_bq_{l}", inp["L0_ffn_bq"][l]); put(f"l0_bk_{l}", inp["L0_ffn_bk"][l])
        put(f"l0_bv_{l}", inp["L0_ffn_bv"][l]); put(f"l0_bo_{l}", inp["L0_ffn_bo"][l])
        put(f"l0_gate_b_{l}", inp["L0_ffn_gate_b"][l])
        for e in range(8):
            put(f"l0_eb1_{l}_{e}", inp["L0_ffn_e_b1"][l, e])
            put(f"l0_eb2_{l}_{e}", inp["L0_ffn_e_b2"][l, e])
    put("l1_b_gate", inp["L1_attn_b_gate"]); put("l1_ln_g", inp["L1_attn_ln_g"]); put("l1_ln_b", inp["L1_attn_ln_b"])
    for e in range(4):
        put(f"l1_b1_{e}", inp["L1_attn_b1"][e])
        put(f"l1_b2_{e}", inp["L1_attn_b2"][e])
    put("f_b_r", inp["L1_ffn_b_r"]); put("f_b1", inp["L1_ffn_b1"]); put("f_b2", inp["L1_ffn_b2"])
    put("f_wr", inp["L1_ffn_w_r"])
    wd = np.asarray(inp["L1_attn_w_dw"], np.float32)
    wcols = np.zeros((128, 24), np.float32)
    for c in range(8):
        wcols[:, c * 3 : c * 3 + 3] = wd[c * 128 : (c + 1) * 128, :]
    sm[:, cols["w_dw"] : cols["w_dw"] + 24] = wcols
    return sm


def _rope_tables():
    inv_freq = (1.0 / 10000.0 ** (np.arange(0, HDIM, 2, dtype=np.float32) / HDIM))
    pos = np.arange(T, dtype=np.float32)
    theta = pos[:, None] * inv_freq[None, :].astype(np.float32)
    sin = np.sin(theta).astype(np.float32)
    cos = np.cos(theta).astype(np.float32)
    cosE = np.repeat(cos.T, 2, axis=0)
    sinE = np.repeat(sin.T, 2, axis=0)
    return (np.ascontiguousarray(np.concatenate([cosE, cosE], axis=0)),
            np.ascontiguousarray(np.concatenate([sinE, sinE], axis=0)))


def _jmat():
    J = np.zeros((64, 64), np.float32)
    for i in range(32):
        J[2 * i + 1, 2 * i] = -1.0
        J[2 * i, 2 * i + 1] = 1.0
    Z = np.zeros_like(J)
    return np.ascontiguousarray(np.block([[J, Z], [Z, J]]))


def host_prep(inp):
    f = np.float32
    w = {}
    w["waT"] = np.ascontiguousarray(np.asarray(inp["L0_attn_Wa"], f).T)
    w["wbT"] = np.ascontiguousarray(np.asarray(inp["L0_attn_Wb"], f).T)
    w["wcT"] = np.ascontiguousarray(np.asarray(inp["L0_attn_Wc"], f).T)
    for nm in ("wq", "wk", "wv", "wo"):
        w[nm] = np.ascontiguousarray(np.asarray(inp[f"L0_ffn_{nm}"], f))
    w["gateT"] = np.ascontiguousarray(np.asarray(inp["L0_ffn_gate_w"], f))
    w["ew1"] = np.ascontiguousarray(np.asarray(inp["L0_ffn_e_w1"], f))
    w["ew2"] = np.ascontiguousarray(np.asarray(inp["L0_ffn_e_w2"], f))
    w["pwT"] = np.ascontiguousarray(np.asarray(inp["L1_attn_w_pw"], f).T)
    w["qkvT"] = np.ascontiguousarray(np.asarray(inp["L1_attn_w_qkv"], f).T)
    w["woT1"] = np.ascontiguousarray(np.asarray(inp["L1_attn_w_o"], f).T)
    w["wgT"] = np.ascontiguousarray(np.asarray(inp["L1_attn_w_gate"], f).T)
    w["mgT"] = np.ascontiguousarray(np.asarray(inp["L1_attn_w_moe_gate"], f).T)
    w["l1w1T"] = np.ascontiguousarray(np.asarray(inp["L1_attn_w1"], f).transpose(0, 2, 1))
    w["l1w2T"] = np.ascontiguousarray(np.asarray(inp["L1_attn_w2"], f).transpose(0, 2, 1))
    w["fW1"] = np.ascontiguousarray(np.asarray(inp["L1_ffn_W1"], f))
    w["fW2"] = np.ascontiguousarray(np.asarray(inp["L1_ffn_W2"], f))
    w["fwr"] = np.ascontiguousarray(np.asarray(inp["L1_ffn_w_r"], f))
    w["smallv"] = _pack_small(inp)
    w["ropeC"], w["ropeS"] = _rope_tables()
    w["jmat"] = _jmat()
    return w



class KB:
    def __init__(self, phases=99, dbg=()):
        self.phases = phases
        self.dbg = set(dbg)
        self.nc = bass.Bass("TRN2", target_bir_lowering=False)
        self.es = ExitStack()
        self.sc = Sched(self.nc, self.es)
        self.cols = _col_map()[0]
        self.res = {}
        self._declare_io()
        self._globals()

    def R(self, name):
        if name not in self.res:
            self.res[name] = Res(name)
        return self.res[name]

    def _dram(self, name, shape, dtype, kind):
        t = self.nc.dram_tensor(name, shape, dtype, kind=kind)
        return t.ap()

    def _declare_io(self):
        self.x_in = self._dram("in_x", [T, D], F32, "ExternalInput")
        for nm, shape in [
            ("waT", [D, D]), ("wbT", [D, D]), ("wcT", [D, D]),
            ("wq", [2, D, D]), ("wk", [2, D, D]), ("wv", [2, D, D]), ("wo", [2, D, D]),
            ("gateT", [2, D, 8]), ("ew1", [2, 8, D, DFF]), ("ew2", [2, 8, DFF, D]),
            ("pwT", [D, D]), ("qkvT", [D, 3 * D]), ("woT1", [D, D]), ("wgT", [D, D]),
            ("mgT", [D, 4]), ("l1w1T", [4, D, DFF]), ("l1w2T", [4, DFF, D]),
            ("fW1", [D, DFF]), ("fW2", [DFF, D]), ("fwr", [D, 1]),
            ("smallv", [128, _col_map()[1]]), ("ropeC", [128, T]), ("ropeS", [128, T]),
            ("jmat", [128, 128]),
        ]:
            setattr(self, "w_" + nm, self._dram("in_" + nm, shape, F32, "ExternalInput"))
        scr = [
            ("sc_xfm", F32), ("sc_h", F32), ("sc_xnb", BF16), ("sc_s0", F32),
            ("sc_q", BF16), ("sc_k", BF16), ("sc_v", BF16), ("sc_o", BF16),
            ("sc_hn2", BF16), ("sc_moe", F32), ("sc_x1", F32), ("sc_xn1", F32),
            ("sc_xn1b", BF16), ("sc_xa", F32), ("sc_xg", F32), ("sc_s1", F32),
            ("sc_blk", F32),
        ]
        for nm, dt in scr:
            kind = "ExternalOutput" if nm in self.dbg else "Internal"
            setattr(self, nm, self._dram(nm, [D, T], dt, kind))
        self.sc_row = self._dram("sc_row", [16, T], F32,
                                 "ExternalOutput" if "sc_row" in self.dbg else "Internal")
        self.sc_row2 = self._dram("sc_row2", [1, 8 * T], F32,
                                  "ExternalOutput" if "sc_row2" in self.dbg else "Internal")
        self.sc_dbg = self._dram("sc_dbg", [128, 256], F32,
                                 "ExternalOutput" if "sc_dbg" in self.dbg else "Internal")
        self.sc_dbg2 = self._dram("sc_dbg2", [128, 4 * D], F32,
                                  "ExternalOutput" if "sc_dbg2" in self.dbg else "Internal")
        self.out = self._dram("out", [T, D], F32, "ExternalOutput")
        self.pcast = {}
        for nm, shape in [
            ("waT", [D, D]), ("wbT", [D, D]), ("wcT", [D, D]),
            ("wq", [2, D, D]), ("wk", [2, D, D]), ("wv", [2, D, D]), ("wo", [2, D, D]),
            ("ew1", [2, 8, D, DFF]), ("ew2", [2, 8, DFF, D]),
            ("pwT", [D, D]), ("qkvT", [D, 3 * D]), ("woT1", [D, D]), ("wgT", [D, D]),
            ("l1w1T", [4, D, DFF]), ("l1w2T", [4, DFF, D]),
            ("fW1", [D, DFF]), ("fW2", [DFF, D]),
        ]:
            self.pcast[nm] = self._dram("pc_" + nm, shape, BF16, "Internal")

    def sbuf(self, name, shape, dtype, stack=None):
        return (stack or self.es).enter_context(self.nc.sbuf_tensor(name, shape, dtype))

    def psum(self, name, shape, dtype, stack=None):
        return (stack or self.es).enter_context(self.nc.psum_tensor(name, shape, dtype))

    def col(self, name, i=0):
        c = self.cols[name] + i
        return self.smallv_sb[:, c : c + 1]

    def _globals(self):
        s = self.sc
        self.smallv_sb = self.sbuf("smallv_sb", [128, _col_map()[1]], F32)
        self.ident_f = self.sbuf("ident_f", [128, 128], F32)
        self.ident_b = self.sbuf("ident_b", [128, 128], BF16)
        self.triu_b = self.sbuf("triu_b", [128, 128], BF16)
        self.ones_f = self.sbuf("ones_f", [128, 1], F32)
        self.ones_b = self.sbuf("ones_b", [128, 1], BF16)
        self.ones_r = self.sbuf("ones_r", [1, 128], F32)
        self.eps5 = self.sbuf("c_eps5", [128, 1], F32)
        self.cwcol_sb = self.sbuf("G_col", [128, 8], F32)
        self.r_cwcol = self.R("G_col")
        self.sc.marker_buf = self.sbuf("markb", [1, 8], F32)
        s.step("sync", lambda e: e.dma_start(out=self.smallv_sb[:], in_=self.w_smallv),
               writes=[self.R("smallv")], dma=True, tag="smallv")

        def mk_memsets(g):
            g.memset(self.ident_f[:], 0.0)
            g.memset(self.ident_b[:], 0.0)
            g.memset(self.triu_b[:], 0.0)
            g.memset(self.ones_f[:], 1.0)
            g.memset(self.eps5[:], 1e-5)
            g.memset(self.ones_b[:], 1.0)
            return g.memset(self.ones_r[:], 1.0)
        s.step("gpsimd", mk_memsets, writes=[self.R("consts")], tag="consts0")
        s.step("gpsimd", lambda g: g.affine_select(
            out=self.ident_f[:], in_=self.ident_f[:], compare_op=ALU.not_equal,
            fill=1.0, base=0, pattern=[[-1, 128]], channel_multiplier=1),
            reads=[self.R("consts")], writes=[self.R("consts")], tag="consts1")
        s.step("gpsimd", lambda g: g.affine_select(
            out=self.ident_b[:], in_=self.ident_b[:], compare_op=ALU.not_equal,
            fill=1.0, base=0, pattern=[[-1, 128]], channel_multiplier=1),
            reads=[self.R("consts")], writes=[self.R("consts")], tag="consts2")
        s.step("gpsimd", lambda g: g.affine_select(
            out=self.triu_b[:], in_=self.triu_b[:], compare_op=ALU.is_gt,
            fill=1.0, base=0, pattern=[[-1, 128]], channel_multiplier=1),
            reads=[self.R("consts")], writes=[self.R("consts")], tag="consts3")
        def pc(nm, idx=None):
            srcap = getattr(self, "w_" + nm)
            dstap = self.pcast[nm]
            key = f"pc_{nm}" + ("" if idx is None else "_" + "_".join(map(str, idx)))
            if idx is not None:
                for i in idx:
                    srcap = srcap[i]
                    dstap = dstap[i]
            s.step("gpsimd", lambda g, srcap=srcap, dstap=dstap: g.dma_start(
                out=dstap, in_=srcap),
                writes=[self.R(key)], dma=True, tag=key)
        for nm in ("waT", "wbT", "wcT"):
            pc(nm)
        for l in range(2):
            for nm in ("wq", "wk", "wv", "wo"):
                pc(nm, (l,))
            for e in range(8):
                pc("ew1", (l, e))
                pc("ew2", (l, e))
        for nm in ("pwT", "qkvT", "woT1", "wgT", "fW1", "fW2"):
            pc(nm)
        for e in range(4):
            pc("l1w1T", (e,))
            pc("l1w2T", (e,))

    def fm_load(self, sb3, dram, t0, tw, reads, writes, tag, engine="sync"):
        src = dram.rearrange("(c p) t -> p c t", p=128)[:, :, t0 : t0 + tw]
        return self.sc.step(engine, lambda g: g.dma_start(out=sb3, in_=src),
                            reads=reads, writes=writes, dma=True, tag=tag)

    def fm_store(self, sb3, dram, t0, tw, reads, writes, tag, engine="sync", accum=False):
        dst = dram.rearrange("(c p) t -> p c t", p=128)[:, :, t0 : t0 + tw]
        if accum:
            return self.sc.step("gpsimd", lambda g: g.dma_start(out=dst, in_=sb3, accum_op=ALU.add),
                                reads=reads, writes=writes, dma=True, tag=tag)
        return self.sc.step(engine, lambda g: g.dma_start(out=dst, in_=sb3),
                            reads=reads, writes=writes, dma=True, tag=tag)

    def phase_A(self):
        s = self.sc
        with ExitStack() as ps:
            xt = [self.sbuf(f"A_x{i}", [128, D], F32, ps) for i in range(2)]
            stat = self.sbuf("A_stat", [128, 8], F32, ps)
            sqd = self.sbuf("A_sqd", [128, D], F32, ps)
            xn = [self.sbuf(f"A_xn{i}", [128, D], F32, ps) for i in range(2)]
            xT = [self.sbuf(f"A_xT{i}", [128, D], F32, ps) for i in range(2)]
            xnT = [self.sbuf(f"A_xnT{i}", [128, D], F32, ps) for i in range(2)]
            xnTb = [self.sbuf(f"A_xnTb{i}", [128, D], BF16, ps) for i in range(2)]
            pt = [self.psum(f"A_pt{i}", [128, 128], F32, ps) for i in range(2)]
            rx = [self.R(f"A_x{i}") for i in range(2)]
            rst = self.R("A_stat"); rsq = self.R("A_sqd")
            rxn = [self.R(f"A_xn{i}") for i in range(2)]
            rxT = [self.R(f"A_xT{i}") for i in range(2)]
            rxnT = [self.R(f"A_xnT{i}") for i in range(2)]
            rxnTb = [self.R(f"A_xnTb{i}") for i in range(2)]
            rpt = [self.R(f"A_pt{i}") for i in range(2)]
            for tt in range(NT1):
                b = tt % 2
                t0 = tt * 128
                s.step("sync", lambda e, b=b, t0=t0: e.dma_start(
                    out=xt[b][:], in_=self.x_in[t0 : t0 + 128, :]),
                    writes=[rx[b]], dma=True, tag=f"A_ld{tt}")
                s.step("vector", lambda e: e.memset(stat[:, 1:2], 0.0),
                       reads=[], writes=[rst], tag=f"A_z{tt}")
                s.step("scalar", lambda e, b=b: e.activation(
                    sqd[:], xt[b][:], AF.Square, accum_out=stat[:, 1:2]),
                    reads=[rx[b]], writes=[rsq, rst], tag=f"A_sq{tt}")
                s.step("vector", lambda e, b=b: e.reduce_sum(stat[:, 0:1], xt[b][:], AX.X),
                       reads=[rx[b]], writes=[rst], tag=f"A_sum{tt}")

                def stats1(v):
                    v.tensor_scalar_mul(stat[:, 2:3], stat[:, 0:1], 1.0 / D)
                    return v.tensor_scalar_mul(stat[:, 3:4], stat[:, 1:2], 1.0 / D)
                s.step("vector", stats1, reads=[rst], writes=[rst], tag=f"A_st1{tt}")
                s.step("vector", lambda e: e.tensor_tensor(
                    stat[:, 4:5], stat[:, 2:3], stat[:, 2:3], ALU.mult),
                    reads=[rst], writes=[rst], tag=f"A_st2{tt}")
                s.step("vector", lambda e: e.tensor_tensor(
                    stat[:, 3:4], stat[:, 3:4], stat[:, 4:5], ALU.subtract),
                    reads=[rst], writes=[rst], tag=f"A_st3{tt}")
                s.step("scalar", lambda e: e.activation(stat[:, 5:6], stat[:, 3:4], AF.Sqrt, bias=self.eps5[:]),
                       reads=[rst], writes=[rst], tag=f"A_sqrt{tt}")
                s.step("vector", lambda e: e.reciprocal(stat[:, 6:7], stat[:, 5:6]),
                       reads=[rst], writes=[rst], tag=f"A_rcp{tt}")
                s.step("vector", lambda e, b=b: e.tensor_scalar(
                    xn[b][:], xt[b][:], stat[:, 2:3], stat[:, 6:7], ALU.subtract, ALU.mult),
                    reads=[rx[b], rst], writes=[rxn[b]], tag=f"A_nrm{tt}")
                if "sc_dbg" in self.dbg:
                    s.step("sync", lambda e, tt=tt: e.dma_start(
                        out=self.sc_dbg[:, tt * 8 : tt * 8 + 8], in_=stat[:]),
                        reads=[rst], writes=[self.R("sc_dbg")], dma=True, tag=f"A_dbg{tt}")
                if "sc_dbg2" in self.dbg and tt < 4:
                    s.step("sync", lambda e, tt=tt, b=b: e.dma_start(
                        out=self.sc_dbg2[:, tt * D : (tt + 1) * D], in_=xn[b][:]),
                        reads=[rxn[b]], writes=[self.R("sc_dbg2")], dma=True, tag=f"A_dbg2{tt}")
                for c in range(NCH):
                    pb = c % 2
                    s.step("tensor", lambda e, b=b, c=c, pb=pb: e.transpose(
                        pt[pb][:], xt[b][:, c * 128 : (c + 1) * 128], self.ident_f[:]),
                        reads=[rx[b], self.R("consts")], writes=[rpt[pb]], tag=f"A_tx{tt}_{c}")
                    s.step("vector", lambda e, b=b, c=c, pb=pb: e.tensor_copy(
                        xT[b][:, c * 128 : (c + 1) * 128], pt[pb][:]),
                        reads=[rpt[pb]], writes=[rxT[b]], tag=f"A_ex{tt}_{c}")
                    s.step("tensor", lambda e, b=b, c=c, pb=pb: e.transpose(
                        pt[pb][:], xn[b][:, c * 128 : (c + 1) * 128], self.ident_f[:]),
                        reads=[rxn[b]], writes=[rpt[pb]], tag=f"A_txn{tt}_{c}")
                    s.step("vector", lambda e, b=b, c=c, pb=pb: e.tensor_copy(
                        xnT[b][:, c * 128 : (c + 1) * 128], pt[pb][:]),
                        reads=[rpt[pb]], writes=[rxnT[b]], tag=f"A_exn{tt}_{c}")
                s.step("scalar", lambda e, b=b: e.copy(xnTb[b][:], xnT[b][:]),
                       reads=[rxnT[b]], writes=[rxnTb[b]], tag=f"A_cast{tt}")
                self.fm_store(xT[b].rearrange("p (c t) -> p c t", c=8), self.sc_xfm,
                              t0, 128, [rxT[b]], [self.R("sc_xfm")], f"A_stx{tt}")
                self.fm_store(xnT[b].rearrange("p (c t) -> p c t", c=8), self.sc_h,
                              t0, 128, [rxnT[b]], [self.R("sc_h")], f"A_sth{tt}")
                self.fm_store(xnTb[b].rearrange("p (c t) -> p c t", c=8), self.sc_xnb,
                              t0, 128, [rxnTb[b]], [self.R("sc_xnb")], f"A_stnb{tt}")
            s.barrier(mark="A")

    def phase_B(self, ps=None):
        s = self.sc
        own = ps is None
        with (ExitStack() if own else ExitStack()) as tmp:
            ps = tmp if own else ps
            xnb = self.sbuf("B_xnb", [128, NCH, T], BF16, ps)
            wT = {nm: self.sbuf(f"B_{nm}", [128, NCH, D], BF16, ps)
                  for nm in ("wa", "wb", "wc")}
            a_sb = self.sbuf("B_a", [128, T], F32, ps)
            u_sb = self.sbuf("B_u", [128, T], F32, ps)
            y_sb = self.sbuf("B_y", [128, T], F32, ps)
            ctmp = self.sbuf("B_c", [128, 512], F32, ps)
            stile = self.sbuf("B_s", [128, 512], F32, ps)
            pa = self.psum("B_pa", [128, 512], F32, ps)
            pb_ = self.psum("B_pb", [128, 512], F32, ps)
            pc = self.psum("B_pc", [128, 512], F32, ps)
            rxnb = self.R("B_xnb")
            rw = {nm: self.R(f"B_{nm}") for nm in ("wa", "wb", "wc")}
            ra, ru, ry = self.R("B_a"), self.R("B_u"), self.R("B_y")
            rc, rs = self.R("B_c"), self.R("B_s")
            rpa, rpb, rpc = self.R("B_pa"), self.R("B_pb"), self.R("B_pc")

            s.step("sync", lambda e: e.dma_start(
                out=xnb[:], in_=self.sc_xnb.rearrange("(c p) t -> p c t", p=128)),
                reads=[self.R("sc_xnb")], writes=[rxnb], dma=True, tag="B_ldx")
            for nm, srcw, key in (("wa", self.pcast["waT"], "pc_waT"),
                                  ("wb", self.pcast["wbT"], "pc_wbT"),
                                  ("wc", self.pcast["wcT"], "pc_wcT")):
                s.step("sync", lambda e, nm=nm, srcw=srcw: e.dma_start(
                    out=wT[nm][:], in_=srcw.rearrange("(c p) e -> p c e", p=128)),
                    reads=[self.R(key)], writes=[rw[nm]], dma=True, tag=f"B_lw{nm}")

            def mm_group(pp, wname, ec, t0):
                def mm(t):
                    ins = None
                    for kc in range(NCH):
                        ins = t.matmul(pp[:], wT[wname][:, kc, ec * 128 : (ec + 1) * 128],
                                       xnb[:, kc, t0 : t0 + 512],
                                       start=(kc == 0), stop=(kc == NCH - 1))
                    return ins
                return mm

            for ec in range(NCH):
                for tt in range(NT5):
                    t0 = tt * 512
                    s.step("tensor", mm_group(pa, "wa", ec, t0),
                           reads=[rxnb, rw["wa"]], writes=[rpa], tag=f"B_mma{ec}_{tt}")
                    s.step("scalar", lambda e, t0=t0, ec=ec: e.activation(
                        a_sb[:, t0 : t0 + 512], pa[:], AF.Sigmoid, bias=self.col("ba", ec)),
                        reads=[rpa, self.R("smallv")], writes=[ra], tag=f"B_eva{ec}_{tt}")
                    s.step("tensor", mm_group(pb_, "wb", ec, t0),
                           reads=[rxnb, rw["wb"]], writes=[rpb], tag=f"B_mmb{ec}_{tt}")
                    s.step("scalar", lambda e, t0=t0, ec=ec: e.activation(
                        u_sb[:, t0 : t0 + 512], pb_[:], AF.Tanh, bias=self.col("bb", ec)),
                        reads=[rpb, self.R("smallv")], writes=[ru], tag=f"B_evb{ec}_{tt}")
                s.step("vector", lambda e: e.tensor_scalar(
                    y_sb[:], a_sb[:], -1.0, 1.0, ALU.mult, ALU.add),
                    reads=[ra], writes=[ry], tag=f"B_oma{ec}")
                s.step("vector", lambda e: e.tensor_tensor(u_sb[:], u_sb[:], y_sb[:], ALU.mult),
                       reads=[ru, ry], writes=[ru], tag=f"B_u{ec}")
                s.step("vector", lambda e: e.tensor_tensor_scan(
                    y_sb[:], a_sb[:], u_sb[:], 0.0, ALU.mult, ALU.add),
                    reads=[ra, ru], writes=[ry], tag=f"B_scan{ec}")
                for tt in range(NT5):
                    t0 = tt * 512
                    s.step("tensor", mm_group(pc, "wc", ec, t0),
                           reads=[rxnb, rw["wc"]], writes=[rpc], tag=f"B_mmc{ec}_{tt}")
                    s.step("scalar", lambda e, ec=ec: e.activation(
                        ctmp[:], pc[:], AF.Sigmoid, bias=self.col("bc", ec)),
                        reads=[rpc, self.R("smallv")], writes=[rc], tag=f"B_evc{ec}_{tt}")
                    s.step("vector", lambda e, t0=t0: e.tensor_tensor(
                        stile[:], ctmp[:], y_sb[:, t0 : t0 + 512], ALU.mult),
                        reads=[rc, ry], writes=[rs], tag=f"B_s{ec}_{tt}")
                    s.step("sync", lambda e, t0=t0, ec=ec: e.dma_start(
                        out=self.sc_s0[ec * 128 : (ec + 1) * 128, t0 : t0 + 512], in_=stile[:]),
                        reads=[rs], writes=[self.R("sc_s0")], dma=True, tag=f"B_st{ec}_{tt}")
            if own:
                s.barrier(mark="B")

    def ln_fm(self, tag, src_dram, wb, sink_b16, f32_hook=None, store_f32=None,
              store_b16=None, eps=1e-5):
        s = self.sc
        with ExitStack() as ps:
            h8 = [self.sbuf(f"{tag}_h{i}", [128, NCH, 512], F32, ps) for i in range(2)]
            sq = [self.sbuf(f"{tag}_sq{i}", [128, 512], BF16, ps) for i in range(2)]
            rows = self.sbuf(f"{tag}_rows", [1, 8 * 512], F32, ps)
            tmp1 = [self.sbuf(f"{tag}_t1{i}", [128, 512], F32, ps) for i in range(2)]
            of32 = [self.sbuf(f"{tag}_o{i}", [128, 512], F32, ps) for i in range(2)]
            pss = self.psum(f"{tag}_pss", [1, 512], F32, ps)
            psq = self.psum(f"{tag}_psq", [1, 512], F32, ps)
            pmb = self.psum(f"{tag}_pmb", [128, 512], F32, ps)
            prb = self.psum(f"{tag}_prb", [128, 512], F32, ps)
            rh = [self.R(f"{tag}_h{i}") for i in range(2)]
            rsq = [self.R(f"{tag}_sq{i}") for i in range(2)]
            rrows = self.R(f"{tag}_rows")
            rt1 = [self.R(f"{tag}_t1{i}") for i in range(2)]
            ro32 = [self.R(f"{tag}_o{i}") for i in range(2)]
            rpss, rpsq = self.R(f"{tag}_pss"), self.R(f"{tag}_psq")
            rpmb, rprb = self.R(f"{tag}_pmb"), self.R(f"{tag}_prb")
            RS = self.R("smallv")
            if store_b16 is not None:
                assert sink_b16 is None
                b16t = [self.sbuf(f"{tag}_b16{i}", [128, 512], BF16, ps) for i in range(2)]
                rb16 = [self.R(f"{tag}_b16{i}") for i in range(2)]
                def sink_b16(kc, tt, t0):
                    return b16t[kc % 2][:], rb16[kc % 2]
                orig_hook = f32_hook
                def f32_hook(kc, tt, t0, ap, res):
                    if orig_hook is not None:
                        orig_hook(kc, tt, t0, ap, res)
                    b2 = kc % 2
                    self.sc.step("sync", lambda e, b2=b2, kc=kc, t0=t0: e.dma_start(
                        out=store_b16[kc * 128 : (kc + 1) * 128, t0 : t0 + 512],
                        in_=b16t[b2][:]),
                        reads=[rb16[b2]], writes=[self.R(store_b16.tensor.name)],
                        dma=True, tag=f"{tag}_sb16{tt}_{kc}")
            for tt in range(NT5):
                t0 = tt * 512
                b = tt % 2
                self.fm_load(h8[b][:], src_dram, t0, 512, [self.R(src_dram.tensor.name)],
                             [rh[b]], f"{tag}_ld{tt}")
                for kc in range(NCH):
                    qb = kc % 2
                    s.step("scalar", lambda e, b=b, kc=kc, qb=qb: e.activation(
                        sq[qb][:], h8[b][:, kc, :], AF.Square),
                        reads=[rh[b]], writes=[rsq[qb]], tag=f"{tag}_sq{tt}_{kc}")
                    s.step("tensor", lambda e, qb=qb, kc=kc: e.matmul(
                        psq[:], self.ones_b[:], sq[qb][:],
                        start=(kc == 0), stop=(kc == NCH - 1)),
                        reads=[rsq[qb], self.R("consts")], writes=[rpsq], tag=f"{tag}_mq{tt}_{kc}")
                    s.step("tensor", lambda e, b=b, kc=kc: e.matmul(
                        pss[:], self.ones_f[:], h8[b][:, kc, :],
                        start=(kc == 0), stop=(kc == NCH - 1)),
                        reads=[rh[b], self.R("consts")], writes=[rpss], tag=f"{tag}_ms{tt}_{kc}")
                s.step("vector", lambda e: e.tensor_copy(rows[:, 0:512], pss[:]),
                       reads=[rpss], writes=[rrows], tag=f"{tag}_r0{tt}")
                s.step("vector", lambda e: e.tensor_copy(rows[:, 512:1024], psq[:]),
                       reads=[rpsq], writes=[rrows], tag=f"{tag}_r1{tt}")
                def rstats(v):
                    v.tensor_scalar_mul(rows[:, 1024:1536], rows[:, 0:512], 1.0 / D)
                    return v.tensor_scalar_mul(rows[:, 1536:2048], rows[:, 512:1024], 1.0 / D)
                s.step("vector", rstats, reads=[rrows], writes=[rrows], tag=f"{tag}_r2{tt}")
                s.step("vector", lambda e: e.tensor_tensor(
                    rows[:, 2048:2560], rows[:, 1024:1536], rows[:, 1024:1536], ALU.mult),
                    reads=[rrows], writes=[rrows], tag=f"{tag}_r3{tt}")
                s.step("vector", lambda e: e.tensor_tensor(
                    rows[:, 1536:2048], rows[:, 1536:2048], rows[:, 2048:2560], ALU.subtract),
                    reads=[rrows], writes=[rrows], tag=f"{tag}_r4{tt}")
                s.step("scalar", lambda e: e.activation(
                    rows[:, 2560:3072], rows[:, 1536:2048], AF.Sqrt, bias=self.eps5[0:1]),
                    reads=[rrows, self.R("consts")], writes=[rrows], tag=f"{tag}_r5{tt}")
                s.step("vector", lambda e: e.reciprocal(rows[:, 3072:3584], rows[:, 2560:3072]),
                       reads=[rrows], writes=[rrows], tag=f"{tag}_r6{tt}")
                s.step("tensor", lambda e: e.matmul(
                    pmb[:], self.ones_r[:], rows[:, 1024:1536], start=True, stop=True),
                    reads=[rrows, self.R("consts")], writes=[rpmb], tag=f"{tag}_bm{tt}")
                s.step("tensor", lambda e: e.matmul(
                    prb[:], self.ones_r[:], rows[:, 3072:3584], start=True, stop=True),
                    reads=[rrows, self.R("consts")], writes=[rprb], tag=f"{tag}_br{tt}")
                for kc in range(NCH):
                    qb = kc % 2
                    s.step("vector", lambda e, b=b, kc=kc, qb=qb: e.tensor_tensor(
                        tmp1[qb][:], h8[b][:, kc, :], pmb[:], ALU.subtract),
                        reads=[rh[b], rpmb], writes=[rt1[qb]], tag=f"{tag}_n1{tt}_{kc}")
                    if wb is None:
                        s.step("vector", lambda e, qb=qb: e.tensor_tensor(
                            of32[qb][:], tmp1[qb][:], prb[:], ALU.mult),
                            reads=[rt1[qb], rprb], writes=[ro32[qb]], tag=f"{tag}_n2{tt}_{kc}")
                    else:
                        s.step("vector", lambda e, qb=qb: e.tensor_tensor(
                            tmp1[qb][:], tmp1[qb][:], prb[:], ALU.mult),
                            reads=[rt1[qb], rprb], writes=[rt1[qb]], tag=f"{tag}_n2{tt}_{kc}")
                        wcol = self.col(wb[0], kc)
                        bcol = self.col(wb[1], kc)
                        s.step("vector", lambda e, qb=qb, wcol=wcol, bcol=bcol: e.tensor_scalar(
                            of32[qb][:], tmp1[qb][:], wcol, bcol, ALU.mult, ALU.add),
                            reads=[rt1[qb], RS], writes=[ro32[qb]], tag=f"{tag}_n3{tt}_{kc}")
                    dst, dres = sink_b16(kc, tt, t0)
                    s.step("scalar", lambda e, qb=qb, dst=dst: e.copy(dst, of32[qb][:]),
                           reads=[ro32[qb]], writes=[dres], tag=f"{tag}_cb{tt}_{kc}")
                    if store_f32 is not None:
                        s.step("sync", lambda e, qb=qb, kc=kc, t0=t0, sf=store_f32: e.dma_start(
                            out=sf[kc * 128 : (kc + 1) * 128, t0 : t0 + 512], in_=of32[qb][:]),
                            reads=[ro32[qb]], writes=[self.R(store_f32.tensor.name)],
                            dma=True, tag=f"{tag}_sf{tt}_{kc}")
                    if f32_hook is not None:
                        f32_hook(kc, tt, t0, of32[qb], ro32[qb])
            s.barrier(mark="ln" + tag)

    def phase_C(self, l, h_src, h_dst):
        self.phase_C12(l, h_src)
        self.phase_C3(l)
        self.phase_C456(l, h_src, h_dst)

    def phase_C12(self, l, h_src):
        s = self.sc
        with ExitStack() as ps:
            hnb = self.sbuf(f"C1hnb_{l}", [128, NCH, T], BF16, ps)
            rhnb = self.R(f"C1hnb_{l}")
            self.ln_fm(f"C1_{l}", h_src, (f"l0_ln1_w_{l}", f"l0_ln1_b_{l}"),
                       lambda kc, tt, t0: (hnb[:, kc, t0 : t0 + 512], rhnb))
            wsb = [self.sbuf(f"C2w{i}_{l}", [128, NCH, D], BF16, ps) for i in range(2)]
            rw = [self.R(f"C2w{i}_{l}") for i in range(2)]
            ot = [self.sbuf(f"C2o{i}_{l}", [128, 512], BF16, ps) for i in range(4)]
            rot = [self.R(f"C2o{i}_{l}") for i in range(4)]
            pp = [self.psum(f"C2p{i}_{l}", [128, 512], F32, ps) for i in range(4)]
            rpp = [self.R(f"C2p{i}_{l}") for i in range(4)]
            specs = [("q", self.pcast["wq"], "l0_bq", self.sc_q),
                     ("k", self.pcast["wk"], "l0_bk", self.sc_k),
                     ("v", self.pcast["wv"], "l0_bv", self.sc_v)]
            for wi, (nm, wsrc, bname, dst) in enumerate(specs):
                wb = wi % 2
                s.step("sync", lambda e, wb=wb, wsrc=wsrc: e.dma_start(
                    out=wsb[wb][:], in_=wsrc[l].rearrange("(c p) e -> p c e", p=128)),
                    reads=[self.R(f"pc_w{nm}_{l}")], writes=[rw[wb]], dma=True,
                    tag=f"C2lw{l}_{nm}")
                n = 0
                for jc in range(NCH):
                    for tt in range(NT5):
                        t0 = tt * 512
                        pb = n % 4
                        n += 1
                        def mm(t, wb=wb, jc=jc, t0=t0, pb=pb):
                            ins = None
                            for kc in range(NCH):
                                ins = t.matmul(pp[pb][:], wsb[wb][:, kc, jc * 128 : (jc + 1) * 128],
                                               hnb[:, kc, t0 : t0 + 512],
                                               start=(kc == 0), stop=(kc == NCH - 1))
                            return ins
                        s.step("tensor", mm, reads=[rw[wb], rhnb], writes=[rpp[pb]],
                               tag=f"C2mm{l}_{nm}_{jc}_{tt}")
                        s.step("vector", lambda e, pb=pb, jc=jc, bname=bname: e.tensor_scalar(
                            ot[pb][:], pp[pb][:], self.col(f"{bname}_{l}", jc), None, ALU.add),
                            reads=[rpp[pb], self.R("smallv")], writes=[rot[pb]],
                            tag=f"C2ev{l}_{nm}_{jc}_{tt}")
                        s.step("sync", lambda e, pb=pb, jc=jc, t0=t0, dst=dst: e.dma_start(
                            out=dst[jc * 128 : (jc + 1) * 128, t0 : t0 + 512], in_=ot[pb][:]),
                            reads=[rot[pb]], writes=[self.R(dst.tensor.name)],
                            dma=True, tag=f"C2st{l}_{nm}_{jc}_{tt}")
            s.barrier(mark=f"C2_{l}")

    def phase_C3(self, l):
        s = self.sc
        with ExitStack() as ps:
            qh = self.sbuf(f"C3q_{l}", [64, T], BF16, ps)
            kh = self.sbuf(f"C3k_{l}", [64, T], BF16, ps)
            vh = self.sbuf(f"C3v_{l}", [64, T], BF16, ps)
            vext = self.sbuf(f"C3vx_{l}", [128, 32 * 65], BF16, ps)
            P = [self.sbuf(f"C3P{i}_{l}", [128, 512], BF16, ps) for i in range(6)]
            drow = [self.sbuf(f"C3dr{i}_{l}", [1, 2 * 512], F32, ps) for i in range(2)]
            onum = [self.sbuf(f"C3on{i}_{l}", [64, 512], F32, ps) for i in range(2)]
            ob = [self.sbuf(f"C3ob{i}_{l}", [64, 512], BF16, ps) for i in range(2)]
            pss = [self.psum(f"C3ps{i}_{l}", [128, 512], F32, ps) for i in range(4)]
            pso = [self.psum(f"C3po{i}_{l}", [128, 512], F32, ps) for i in range(2)]
            psv = self.psum(f"C3pv_{l}", [128, 64], BF16, ps)
            psd = self.psum(f"C3pd_{l}", [128, 512], F32, ps)
            rqh, rkh, rvh = self.R(f"C3q_{l}"), self.R(f"C3k_{l}"), self.R(f"C3v_{l}")
            rvx = self.R(f"C3vx_{l}")
            rP = [self.R(f"C3P{i}_{l}") for i in range(6)]
            rdr = [self.R(f"C3dr{i}_{l}") for i in range(2)]
            ron = [self.R(f"C3on{i}_{l}") for i in range(2)]
            rob = [self.R(f"C3ob{i}_{l}") for i in range(2)]
            rps = [self.R(f"C3ps{i}_{l}") for i in range(4)]
            rpo = [self.R(f"C3po{i}_{l}") for i in range(2)]
            rpv, rpd = self.R(f"C3pv_{l}"), self.R(f"C3pd_{l}")
            for hd in range(NHEAD):
                r0 = hd * 64
                for nm, buf, rr, srcd in (("q", qh, rqh, self.sc_q), ("k", kh, rkh, self.sc_k),
                                          ("v", vh, rvh, self.sc_v)):
                    s.step("sync", lambda e, buf=buf, srcd=srcd, r0=r0: e.dma_start(
                        out=buf[:], in_=srcd[r0 : r0 + 64, :]),
                        reads=[self.R(srcd.tensor.name)], writes=[rr], dma=True,
                        tag=f"C3ld{l}_{hd}_{nm}")
                s.step("gpsimd", lambda e: e.memset(vext[:], 1.0),
                       writes=[rvx], tag=f"C3one{l}_{hd}")
                for ki in range(32):
                    s.step("tensor", lambda e, ki=ki: e.transpose(
                        psv[:], vh[:, ki * 128 : (ki + 1) * 128], self.ident_b[0:64, 0:64]),
                        reads=[rvh, self.R("consts")], writes=[rpv], tag=f"C3tv{l}_{hd}_{ki}")
                    s.step("vector", lambda e, ki=ki: e.tensor_copy(
                        vext[:, ki * 65 : ki * 65 + 64], psv[:]),
                        reads=[rpv], writes=[rvx], tag=f"C3cv{l}_{hd}_{ki}")
                for qg in range(NT5):
                    q0 = qg * 512
                    ob_i = qg % 2
                    nki = 4 * qg + 4

                    def emit_S(ki):
                        sb = ki % 4
                        s.step("tensor", lambda e, ki=ki, sb=sb, q0=q0: e.matmul(
                            pss[sb][:], kh[:, ki * 128 : (ki + 1) * 128], qh[:, q0 : q0 + 512],
                            start=True, stop=True),
                            reads=[rkh, rqh], writes=[rps[sb]], tag=f"C3s{l}_{hd}_{qg}_{ki}")

                    def emit_rest(ki):
                        sb = ki % 4
                        pb = ki % 6
                        s.step("scalar", lambda e, sb=sb, pb=pb: e.activation(
                            P[pb][:], pss[sb][:], AF.Exp, scale=0.125),
                            reads=[rps[sb]], writes=[rP[pb]], tag=f"C3e{l}_{hd}_{qg}_{ki}")
                        off = ki * 128 - q0
                        if off >= 0:
                            if off > 0:
                                s.step("vector", lambda e, pb=pb, off=off: e.memset(
                                    P[pb][:, 0:off], 0.0),
                                    reads=[], writes=[rP[pb]], tag=f"C3z{l}_{hd}_{qg}_{ki}")
                            s.step("vector", lambda e, pb=pb, off=off: e.tensor_tensor(
                                P[pb][:, off : off + 128], P[pb][:, off : off + 128],
                                self.triu_b[:], ALU.mult),
                                reads=[self.R("consts")], writes=[rP[pb]],
                                tag=f"C3m{l}_{hd}_{qg}_{ki}")
                        s.step("tensor", lambda e, ki=ki, pb=pb, ob_i=ob_i, nki=nki: e.matmul(
                            pso[ob_i][0:65, :], vext[:, ki * 65 : (ki + 1) * 65], P[pb][:],
                            start=(ki == 0), stop=(ki == nki - 1)),
                            reads=[rvx, rP[pb]], writes=[rpo[ob_i]], tag=f"C3pv{l}_{hd}_{qg}_{ki}")

                    emit_S(0)
                    for ki in range(nki):
                        if ki + 1 < nki:
                            emit_S(ki + 1)
                        emit_rest(ki)
                    d_i = qg % 2
                    s.step("vector", lambda e, ob_i=ob_i, d_i=d_i: e.tensor_copy(
                        drow[d_i][:, 0:512], pso[ob_i][64:65, :]),
                        reads=[rpo[ob_i]], writes=[rdr[d_i]], tag=f"C3d0{l}_{hd}_{qg}")
                    s.step("vector", lambda e, d_i=d_i: e.reciprocal(
                        drow[d_i][:, 512:1024], drow[d_i][:, 0:512]),
                        reads=[rdr[d_i]], writes=[rdr[d_i]], tag=f"C3d1{l}_{hd}_{qg}")
                    s.step("scalar", lambda e, ob_i=ob_i, d_i=d_i: e.copy(
                        onum[d_i][:], pso[ob_i][0:64, :]),
                        reads=[rpo[ob_i]], writes=[ron[d_i]], tag=f"C3d2{l}_{hd}_{qg}")
                    s.step("tensor", lambda e, d_i=d_i: e.matmul(
                        psd[0:64, :], self.ones_r[0:1, 0:64], drow[d_i][:, 512:1024],
                        start=True, stop=True),
                        reads=[rdr[d_i], self.R("consts")], writes=[rpd], tag=f"C3d3{l}_{hd}_{qg}")
                    s.step("vector", lambda e, d_i=d_i: e.tensor_tensor(
                        ob[d_i][:], onum[d_i][:], psd[0:64, :], ALU.mult),
                        reads=[ron[d_i], rpd], writes=[rob[d_i]], tag=f"C3d4{l}_{hd}_{qg}")
                    s.step("sync", lambda e, r0=r0, q0=q0, d_i=d_i: e.dma_start(
                        out=self.sc_o[r0 : r0 + 64, q0 : q0 + 512], in_=ob[d_i][:]),
                        reads=[rob[d_i]], writes=[self.R("sc_o")], dma=True,
                        tag=f"C3st{l}_{hd}_{qg}")
            s.barrier(mark=f"C3_{l}")

    def phase_C456(self, l, h_src, h_dst):
        s = self.sc
        with ExitStack() as ps:
            wo = self.sbuf(f"C4w_{l}", [128, NCH, D], BF16, ps)
            rwo = self.R(f"C4w_{l}")
            o8 = [self.sbuf(f"C4o{i}_{l}", [128, NCH, 512], BF16, ps) for i in range(2)]
            ro8 = [self.R(f"C4o{i}_{l}") for i in range(2)]
            h8 = [self.sbuf(f"C4h{i}_{l}", [128, NCH, 512], F32, ps) for i in range(2)]
            rh8 = [self.R(f"C4h{i}_{l}") for i in range(2)]
            at = [self.sbuf(f"C4a{i}_{l}", [128, 512], F32, ps) for i in range(2)]
            rat = [self.R(f"C4a{i}_{l}") for i in range(2)]
            hn = [self.sbuf(f"C4n{i}_{l}", [128, NCH, 512], F32, ps) for i in range(2)]
            rhn = [self.R(f"C4n{i}_{l}") for i in range(2)]
            pp4 = [self.psum(f"C4p{i}_{l}", [128, 512], F32, ps) for i in range(2)]
            rpp4 = [self.R(f"C4p{i}_{l}") for i in range(2)]
            s.step("sync", lambda e: e.dma_start(
                out=wo[:], in_=self.pcast["wo"][l].rearrange("(c p) e -> p c e", p=128)),
                reads=[self.R(f"pc_wo_{l}")], writes=[rwo], dma=True, tag=f"C4lw{l}")
            for tt in range(NT5):
                t0 = tt * 512
                b = tt % 2
                self.fm_load(o8[b][:], self.sc_o, t0, 512, [self.R("sc_o")], [ro8[b]],
                             f"C4ldo{l}_{tt}")
                self.fm_load(h8[b][:], h_src, t0, 512, [self.R(h_src.tensor.name)], [rh8[b]],
                             f"C4ldh{l}_{tt}")
                for jc in range(NCH):
                    pb = jc % 2
                    def mm(t, b=b, jc=jc, pb=pb):
                        ins = None
                        for kc in range(NCH):
                            ins = t.matmul(pp4[pb][:], wo[:, kc, jc * 128 : (jc + 1) * 128],
                                           o8[b][:, kc, :],
                                           start=(kc == 0), stop=(kc == NCH - 1))
                        return ins
                    s.step("tensor", mm, reads=[rwo, ro8[b]], writes=[rpp4[pb]],
                           tag=f"C4mm{l}_{tt}_{jc}")
                    s.step("vector", lambda e, pb=pb, jc=jc: e.tensor_scalar(
                        at[pb][:], pp4[pb][:], self.col(f"l0_bo_{l}", jc), None, ALU.add),
                        reads=[rpp4[pb], self.R("smallv")], writes=[rat[pb]],
                        tag=f"C4ev{l}_{tt}_{jc}")
                    s.step("vector", lambda e, b=b, jc=jc, pb=pb: e.tensor_tensor(
                        hn[b][:, jc, :], h8[b][:, jc, :], at[pb][:], ALU.add),
                        reads=[rh8[b], rat[pb]], writes=[rhn[b]], tag=f"C4add{l}_{tt}_{jc}")
                self.fm_store(hn[b][:], h_src, t0, 512, [rhn[b]],
                              [self.R(h_src.tensor.name)], f"C4st{l}_{tt}")
            s.barrier(mark=f"C4_{l}")
        with ExitStack() as ps:
            gsb = self.sbuf(f"C5g_{l}", [128, NCH, 8], F32, ps)
            rgs = self.R(f"C5g_{l}")
            hb2 = [self.sbuf(f"C5b{i}_{l}", [128, 512], BF16, ps) for i in range(2)]
            rhb2 = [self.R(f"C5b{i}_{l}") for i in range(2)]
            psg = self.psum(f"C5pg_{l}", [8, 512], F32, ps)
            rpsg = self.R(f"C5pg_{l}")
            macc = self.sbuf(f"C5ma_{l}", [128, 16], F32, ps)
            rmacc = self.R(f"C5ma_{l}")
            s.step("sync", lambda e: e.dma_start(
                out=gsb[:], in_=self.w_gateT[l].rearrange("(c p) e -> p c e", p=128)),
                writes=[rgs], dma=True, tag=f"C5lg{l}")
            if l % 2 == 1:
                s.step("vector", lambda e: e.memset(macc[:], 0.0),
                       writes=[rmacc], tag=f"C5z{l}")
            glog = self.sbuf(f"C5gl_{l}", [8, T], F32, ps)
            rglog = self.R(f"C5gl_{l}")

            def hook(kc, tt, t0, ap, res):
                if l % 2 == 0:
                    s.step("tensor", lambda e, kc=kc, ap=ap: e.matmul(
                        psg[:], gsb[:, kc, :], ap[:],
                        start=(kc == 0), stop=(kc == NCH - 1)),
                        reads=[res, rgs], writes=[rpsg], tag=f"C5gm{l}_{tt}_{kc}")
                    if kc == NCH - 1:
                        s.step("vector", lambda e, t0=t0: e.tensor_scalar(
                            glog[:, t0 : t0 + 512], psg[:],
                            self.col(f"l0_gate_b_{l}")[0:8], None, ALU.add),
                            reads=[rpsg, self.R("smallv")], writes=[rglog],
                            tag=f"C5ge{l}_{tt}")
                else:
                    s.step("vector", lambda e, kc=kc, ap=ap: e.reduce_sum(
                        macc[:, 8 + kc : 9 + kc], ap[:], AX.X),
                        reads=[res], writes=[rmacc], tag=f"C5mr{l}_{tt}_{kc}")
                    s.step("vector", lambda e, kc=kc: e.tensor_tensor(
                        macc[:, kc : kc + 1], macc[:, kc : kc + 1],
                        macc[:, 8 + kc : 9 + kc], ALU.add),
                        reads=[rmacc], writes=[rmacc], tag=f"C5mx{l}_{tt}_{kc}")

            def sink(kc, tt, t0):
                return hb2[kc % 2][:], rhb2[kc % 2]

            self._c5_state = (hb2, rhb2)
            def sink2(kc, tt, t0):
                return hb2[kc % 2][:], rhb2[kc % 2]
            def hook2(kc, tt, t0, ap, res):
                hook(kc, tt, t0, ap, res)
                b2 = kc % 2
                s.step("sync", lambda e, b2=b2, kc=kc, t0=t0: e.dma_start(
                    out=self.sc_hn2[kc * 128 : (kc + 1) * 128, t0 : t0 + 512],
                    in_=hb2[b2][:]),
                    reads=[rhb2[b2]], writes=[self.R("sc_hn2")], dma=True,
                    tag=f"C5sb{l}_{tt}_{kc}")

            self.ln_fm(f"C5_{l}", h_src, (f"l0_ln2_w_{l}", f"l0_ln2_b_{l}"), sink2,
                       f32_hook=hook2)
            self._gate_cw(l, glog, rglog, macc, rmacc, gsb, rgs, ps)
        self._moe(l, h_src, h_dst)

    def _gate_cw(self, l, glog, rglog, macc, rmacc, gsb, rgs, ps):
        s = self.sc
        ltm = self.sbuf(f"G_ltm_{l}", [128, 16], F32, ps)
        rltm = self.R(f"G_ltm_{l}")
        cwtm = self.sbuf(f"G_cw_{l}", [128, 8], F32, ps)
        rcw = self.R(f"G_cw_{l}")
        cols = self.sbuf(f"G_c_{l}", [128, 8], F32, ps)
        rc = self.R(f"G_c_{l}")
        pst = self.psum(f"G_pt_{l}", [128, 16], F32, ps)
        rpt = self.R(f"G_pt_{l}")
        if l % 2 == 0:
            for g in range(NT1):
                g0 = g * 128
                s.step("tensor", lambda e, g0=g0: e.transpose(
                    pst[:, 0:8], glog[:, g0 : g0 + 128], self.ident_f[0:8, 0:8]),
                    reads=[rglog, self.R("consts")], writes=[rpt], tag=f"G_t{l}_{g}")
                s.step("vector", lambda e: e.tensor_copy(ltm[:, 0:8], pst[:, 0:8]),
                       reads=[rpt], writes=[rltm], tag=f"G_c0{l}_{g}")
                self._cw_from_logits(l, g, ltm, rltm, cwtm, rcw, cols, rc, 128, 8)
                s.step("gpsimd", lambda e, g0=g0: e.dma_start(
                    out=self.sc_row2.rearrange("o (e t) -> (o t) e", e=8)[g0 : g0 + 128, :],
                    in_=cwtm[:, 0:8], allow_slow_non_contiguous=True),
                    reads=[rcw], writes=[self.R("sc_row2")], dma=True, tag=f"G_d{l}_{g}")

        else:
            s.step("vector", lambda e: e.tensor_scalar_mul(macc[:, 0:8], macc[:, 0:8], 1.0 / T),
                   reads=[rmacc], writes=[rmacc], tag=f"G_mm{l}")
            psg2 = self.psum(f"G_pg_{l}", [8, 8], F32, ps)
            rpg2 = self.R(f"G_pg_{l}")
            def mm(t):
                ins = None
                for kc in range(NCH):
                    ins = t.matmul(psg2[:, 0:1], gsb[:, kc, :], macc[:, kc : kc + 1],
                                   start=(kc == 0), stop=(kc == NCH - 1))
                return ins
            s.step("tensor", mm, reads=[rmacc, rgs], writes=[rpg2], tag=f"G_gm{l}")
            s.step("vector", lambda e: e.tensor_scalar(
                ltm[0:8, 0:1], psg2[:, 0:1], self.col(f"l0_gate_b_{l}")[0:8], None, ALU.add),
                reads=[rpg2, self.R("smallv")], writes=[rltm], tag=f"G_ge{l}")
            s.step("tensor", lambda e: e.transpose(
                pst[0:1, 0:8], ltm[0:8, 0:1], self.ident_f[0:8, 0:8]),
                reads=[rltm, self.R("consts")], writes=[rpt], tag=f"G_t{l}")
            s.step("vector", lambda e: e.tensor_copy(ltm[0:1, 8:16], pst[0:1, 0:8]),
                   reads=[rpt], writes=[rltm], tag=f"G_c1{l}")
            self._cw_from_logits(l, 0, ltm, rltm, cwtm, rcw, cols, rc, 1, 8, src_off=8)
            s.step("tensor", lambda e: e.matmul(
                pst[:, 0:8], self.ones_r[:], cwtm[0:1, 0:8], start=True, stop=True),
                reads=[rcw, self.R("consts")], writes=[rpt], tag=f"G_b{l}")
            s.step("vector", lambda e: e.tensor_copy(self.cwcol_sb[:], pst[:, 0:8]),
                   reads=[rpt], writes=[self.r_cwcol], tag=f"G_c2{l}")
        s.barrier()

    def _cw_from_logits(self, l, g, ltm, rltm, cwtm, rcw, cols, rc, np_, ne, src_off=0):
        s = self.sc
        L = ltm[0:np_, src_off : src_off + ne]
        s.step("vector", lambda e: e.reduce_max(cols[0:np_, 0:1], L, AX.X),
               reads=[rltm], writes=[rc], tag=f"W_m1{l}_{g}")
        s.step("vector", lambda e: e.reduce_max(cols[0:np_, 1:2], L, AX.X, negate=True),
               reads=[rltm], writes=[rc], tag=f"W_nm1{l}_{g}")
        s.step("scalar", lambda e: e.activation(
            cwtm[0:np_, 0:ne], L, AF.Exp, bias=cols[0:np_, 1:2]),
            reads=[rltm, rc], writes=[rcw], tag=f"W_e{l}_{g}")
        s.step("vector", lambda e: e.tensor_scalar(
            ltm[0:np_, 8 : 8 + ne] if src_off == 0 else ltm[0:np_, 0:ne],
            L, cols[0:np_, 0:1], None, ALU.is_ge),
            reads=[rltm, rc], writes=[rltm], tag=f"W_eq{l}_{g}")
        ZT = ltm[0:np_, 8 : 8 + ne] if src_off == 0 else ltm[0:np_, 0:ne]
        s.step("vector", lambda e: e.scalar_tensor_tensor(
            ZT, ZT, -1e30, L, ALU.mult, ALU.add),
            reads=[rltm], writes=[rltm], tag=f"W_z{l}_{g}")
        s.step("vector", lambda e: e.reduce_max(cols[0:np_, 2:3], ZT, AX.X),
               reads=[rltm], writes=[rc], tag=f"W_m2{l}_{g}")
        s.step("vector", lambda e: e.tensor_scalar(
            ZT, L, cols[0:np_, 2:3], None, ALU.is_ge),
            reads=[rltm, rc], writes=[rltm], tag=f"W_sel{l}_{g}")
        s.step("vector", lambda e: e.tensor_tensor(
            cwtm[0:np_, 0:ne], cwtm[0:np_, 0:ne], ZT, ALU.mult),
            reads=[rcw, rltm], writes=[rcw], tag=f"W_w{l}_{g}")
        s.step("vector", lambda e: e.reduce_sum(cols[0:np_, 3:4], cwtm[0:np_, 0:ne], AX.X),
               reads=[rcw], writes=[rc], tag=f"W_d{l}_{g}")
        s.step("vector", lambda e: e.reciprocal(cols[0:np_, 1:2], cols[0:np_, 3:4]),
               reads=[rc], writes=[rc], tag=f"W_r{l}_{g}")
        s.step("vector", lambda e: e.tensor_scalar(
            cwtm[0:np_, 0:ne], cwtm[0:np_, 0:ne], cols[0:np_, 1:2], None, ALU.mult),
            reads=[rcw, rc], writes=[rcw], tag=f"W_f{l}_{g}")

    def _moe(self, l, h_src, h_dst):
        s = self.sc
        s.step("sync", lambda e: e.dma_start(out=h_dst, in_=h_src),
               reads=[self.R(h_src.tensor.name)], writes=[self.R(h_dst.tensor.name)],
               dma=True, tag=f"M_init{l}")
        with ExitStack() as ps:
            w1 = self.sbuf(f"M_w1_{l}", [128, NCH, DFF], BF16, ps)
            w2 = self.sbuf(f"M_w2_{l}", [128, 32, D], BF16, ps)
            rw1, rw2 = self.R(f"M_w1_{l}"), self.R(f"M_w2_{l}")
            rhs = [self.sbuf(f"M_r{i}_{l}", [128, NCH, 512], BF16, ps) for i in range(2)]
            rrhs = [self.R(f"M_r{i}_{l}") for i in range(2)]
            hid = self.sbuf(f"M_h_{l}", [128, 32, 512], BF16, ps)
            rhid = self.R(f"M_h_{l}")
            wout = self.sbuf(f"M_o_{l}", [128, NCH, 512], F32, ps)
            rwout = self.R(f"M_o_{l}")
            cwB = [self.sbuf(f"M_c{i}_{l}", [128, 512], F32, ps) for i in range(2)]
            rcwB = [self.R(f"M_c{i}_{l}") for i in range(2)]
            pp = [self.psum(f"M_p{i}_{l}", [128, 512], F32, ps) for i in range(3)]
            rpp = [self.R(f"M_p{i}_{l}") for i in range(3)]
            pq = [self.psum(f"M_q{i}_{l}", [128, 512], F32, ps) for i in range(2)]
            rpq = [self.R(f"M_q{i}_{l}") for i in range(2)]
            pcb = self.psum(f"M_cb_{l}", [128, 512], F32, ps)
            rpcb = self.R(f"M_cb_{l}")
            cwt = [self.sbuf(f"M_t{i}_{l}", [1, 512], F32, ps) for i in range(2)]
            rcwt = [self.R(f"M_t{i}_{l}") for i in range(2)]
            NE = 8

            def load_w(e):
                s.step("sync", lambda g, e=e: g.dma_start(
                    out=w1[:], in_=self.pcast["ew1"][l, e].rearrange("(c p) f -> p c f", p=128)),
                    reads=[self.R(f"pc_ew1_{l}_{e}")], writes=[rw1], dma=True, tag=f"M_lw1{l}_{e}")
                s.step("sync", lambda g, e=e: g.dma_start(
                    out=w2[:], in_=self.pcast["ew2"][l, e].rearrange("(c p) d -> p c d", p=128)),
                    reads=[self.R(f"pc_ew2_{l}_{e}")], writes=[rw2], dma=True, tag=f"M_lw2{l}_{e}")

            for e in range(NE):
                load_w(e)
                for tt in range(NT5):
                    t0 = tt * 512
                    b = tt % 2
                    self.fm_load(rhs[b][:], self.sc_hn2, t0, 512, [self.R("sc_hn2")],
                                 [rrhs[b]], f"M_lr{l}_{e}_{tt}")
                    if l % 2 == 0:
                        s.step("sync", lambda en, e=e, t0=t0, b=b: en.dma_start(
                            out=cwt[b][:], in_=self.sc_row2[0:1, e * T + t0 : e * T + t0 + 512]),
                            reads=[self.R("sc_row2")], writes=[rcwt[b]], dma=True,
                            tag=f"M_ct{l}_{e}_{tt}")
                        s.step("tensor", lambda en, b=b: en.matmul(
                            pcb[:], self.ones_r[:], cwt[b][:], start=True, stop=True),
                            reads=[rcwt[b], self.R("consts")], writes=[rpcb],
                            tag=f"M_cb{l}_{e}_{tt}")
                        s.step("vector", lambda en, b=b: en.tensor_copy(cwB[b][:], pcb[:]),
                               reads=[rpcb], writes=[rcwB[b]], tag=f"M_cc{l}_{e}_{tt}")
                    for fc in range(32):
                        pb = fc % 3
                        def mm1(t, fc=fc, b=b, pb=pb):
                            ins = None
                            for kc in range(NCH):
                                ins = t.matmul(pp[pb][:],
                                               w1[:, kc, fc * 128 : (fc + 1) * 128],
                                               rhs[b][:, kc, :],
                                               start=(kc == 0), stop=(kc == NCH - 1))
                            return ins
                        s.step("tensor", mm1, reads=[rw1, rrhs[b]], writes=[rpp[pb]],
                               tag=f"M_m1{l}_{e}_{tt}_{fc}")
                        s.step("scalar", lambda en, pb=pb, fc=fc, e=e: en.activation(
                            hid[:, fc, :], pp[pb][:], AF.Gelu,
                            bias=self.col(f"l0_eb1_{l}_{e}", fc)),
                            reads=[rpp[pb], self.R("smallv")], writes=[rhid],
                            tag=f"M_g{l}_{e}_{tt}_{fc}")
                    for dc in range(NCH):
                        qb = dc % 2
                        def mm2(t, dc=dc, qb=qb):
                            ins = None
                            for fc in range(32):
                                ins = t.matmul(pq[qb][:],
                                               w2[:, fc, dc * 128 : (dc + 1) * 128],
                                               hid[:, fc, :],
                                               start=(fc == 0), stop=(fc == 31))
                            return ins
                        s.step("tensor", mm2, reads=[rw2, rhid], writes=[rpq[qb]],
                               tag=f"M_m2{l}_{e}_{tt}_{dc}")
                        if l % 2 == 0:
                            s.step("vector", lambda en, qb=qb, dc=dc, b=b, e=e: en.scalar_tensor_tensor(
                                wout[:, dc, :], pq[qb][:], self.col(f"l0_eb2_{l}_{e}", dc),
                                cwB[b][:], ALU.add, ALU.mult),
                                reads=[rpq[qb], rcwB[b], self.R("smallv")], writes=[rwout],
                                tag=f"M_w{l}_{e}_{tt}_{dc}")
                        else:
                            s.step("vector", lambda en, qb=qb, dc=dc, e=e: en.tensor_scalar(
                                wout[:, dc, :], pq[qb][:], self.col(f"l0_eb2_{l}_{e}", dc),
                                self.cwcol_sb[:, e : e + 1], ALU.add, ALU.mult),
                                reads=[rpq[qb], self.r_cwcol, self.R("smallv")],
                                writes=[rwout], tag=f"M_w{l}_{e}_{tt}_{dc}")
                    self.fm_store(wout[:], h_dst, t0, 512, [rwout],
                                  [self.R(h_dst.tensor.name)], f"M_st{l}_{e}_{tt}",
                                  accum=True)
            s.barrier(mark=f"M_{l}")

    def phase_D(self, h_src):
        s = self.sc
        with ExitStack() as ps:
            h8 = [self.sbuf(f"D_h{i}", [128, NCH, 512], F32, ps) for i in range(2)]
            s8 = [self.sbuf(f"D_s{i}", [128, NCH, 512], F32, ps) for i in range(2)]
            x8 = [self.sbuf(f"D_x{i}", [128, NCH, 512], F32, ps) for i in range(2)]
            gt = [self.sbuf(f"D_g{i}", [128, 512], F32, ps) for i in range(2)]
            o8 = [self.sbuf(f"D_o{i}", [128, NCH, 512], F32, ps) for i in range(2)]
            rh = [self.R(f"D_h{i}") for i in range(2)]
            rs8 = [self.R(f"D_s{i}") for i in range(2)]
            rx8 = [self.R(f"D_x{i}") for i in range(2)]
            rgt = [self.R(f"D_g{i}") for i in range(2)]
            ro8 = [self.R(f"D_o{i}") for i in range(2)]
            for tt in range(NT5):
                t0 = tt * 512
                b = tt % 2
                self.fm_load(h8[b][:], h_src, t0, 512, [self.R(h_src.tensor.name)], [rh[b]], f"D_lh{tt}")
                self.fm_load(s8[b][:], self.sc_s0, t0, 512, [self.R("sc_s0")], [rs8[b]], f"D_ls{tt}")
                self.fm_load(x8[b][:], self.sc_xfm, t0, 512, [self.R("sc_xfm")], [rx8[b]], f"D_lx{tt}")
                for kc in range(NCH):
                    qb = kc % 2
                    s.step("scalar", lambda e, b=b, kc=kc, qb=qb: e.activation(
                        gt[qb][:], h8[b][:, kc, :], AF.Silu),
                        reads=[rh[b]], writes=[rgt[qb]], tag=f"D_si{tt}_{kc}")
                    s.step("vector", lambda e, b=b, kc=kc, qb=qb: e.tensor_tensor(
                        gt[qb][:], gt[qb][:], s8[b][:, kc, :], ALU.mult),
                        reads=[rgt[qb], rs8[b]], writes=[rgt[qb]], tag=f"D_sg{tt}_{kc}")
                    s.step("vector", lambda e, b=b, kc=kc, qb=qb: e.tensor_tensor(
                        o8[b][:, kc, :], x8[b][:, kc, :], gt[qb][:], ALU.add),
                        reads=[rx8[b], rgt[qb]], writes=[ro8[b]], tag=f"D_x1{tt}_{kc}")
                self.fm_store(o8[b][:], self.sc_x1, t0, 512, [ro8[b]],
                              [self.R("sc_x1")], f"D_st{tt}")
            s.barrier(mark="D")
        self.ln_fm("D2", self.sc_x1, None, None, store_b16=self.sc_xn1b,
                   store_f32=self.sc_xn1)

    def phase_E(self):
        self._e1()
        self._e2()
        self._e3()
        self._e4()
        self._e5()
        self._e6()
        self._e7()

    def _e1(self):
        s = self.sc
        with ExitStack() as ps:
            yb = self.sbuf("E1_yb", [128, NCH, T], BF16, ps)
            ryb = self.R("E1_yb")
            ps_in = ExitStack()
            xc = [self.sbuf(f"E1_x{i}", [128, T], F32, ps_in) for i in range(2)]
            rxc = [self.R(f"E1_x{i}") for i in range(2)]
            cb = [self.sbuf(f"E1_c{i}", [128, T], F32, ps_in) for i in range(3)]
            rcb = [self.R(f"E1_c{i}") for i in range(3)]
            for ec in range(NCH):
                b = ec % 2
                s.step("sync", lambda e, ec=ec, b=b: e.dma_start(
                    out=xc[b][:], in_=self.sc_xn1[ec * 128 : (ec + 1) * 128, :]),
                    reads=[self.R("sc_xn1")], writes=[rxc[b]], dma=True, tag=f"E1_lx{ec}")
                s.step("vector", lambda e, ec=ec, b=b: e.tensor_scalar(
                    cb[0][:], xc[b][:], self.col("w_dw", ec * 3 + 2), None, ALU.mult),
                    reads=[rxc[b], self.R("smallv")], writes=[rcb[0]], tag=f"E1_t2{ec}")
                s.step("vector", lambda e, ec=ec, b=b: e.scalar_tensor_tensor(
                    cb[1][:, 1:T], xc[b][:, 0 : T - 1], self.col("w_dw", ec * 3 + 1),
                    cb[0][:, 1:T], ALU.mult, ALU.add),
                    reads=[rxc[b], rcb[0], self.R("smallv")], writes=[rcb[1]], tag=f"E1_t1{ec}")
                s.step("vector", lambda e: e.tensor_copy(cb[1][:, 0:1], cb[0][:, 0:1]),
                       reads=[rcb[0]], writes=[rcb[1]], tag=f"E1_c1{ec}")
                s.step("vector", lambda e, ec=ec, b=b: e.scalar_tensor_tensor(
                    cb[2][:, 2:T], xc[b][:, 0 : T - 2], self.col("w_dw", ec * 3 + 0),
                    cb[1][:, 2:T], ALU.mult, ALU.add),
                    reads=[rxc[b], rcb[1], self.R("smallv")], writes=[rcb[2]], tag=f"E1_t0{ec}")
                s.step("vector", lambda e: e.tensor_copy(cb[2][:, 0:2], cb[1][:, 0:2]),
                       reads=[rcb[1]], writes=[rcb[2]], tag=f"E1_c0{ec}")
                s.step("scalar", lambda e, ec=ec: e.copy(yb[:, ec, :], cb[2][:]),
                       reads=[rcb[2]], writes=[ryb], tag=f"E1_cb{ec}")
            ps_in.close()
            s.barrier()
            wsb = self.sbuf("E1_w", [128, NCH, D], BF16, ps)
            rw = self.R("E1_w")
            xn8 = [self.sbuf(f"E1_n{i}", [128, NCH, 512], F32, ps) for i in range(2)]
            rxn8 = [self.R(f"E1_n{i}") for i in range(2)]
            xa8 = [self.sbuf(f"E1_a{i}", [128, NCH, 512], F32, ps) for i in range(2)]
            rxa8 = [self.R(f"E1_a{i}") for i in range(2)]
            ab = [self.sbuf(f"E1_ab{i}", [128, 512], BF16, ps) for i in range(2)]
            rab = [self.R(f"E1_ab{i}") for i in range(2)]
            pp = [self.psum(f"E1_p{i}", [128, 512], F32, ps) for i in range(2)]
            rpp = [self.R(f"E1_p{i}") for i in range(2)]
            s.step("sync", lambda e: e.dma_start(
                out=wsb[:], in_=self.pcast["pwT"].rearrange("(c p) e -> p c e", p=128)),
                reads=[self.R("pc_pwT")], writes=[rw], dma=True, tag="E1_lw")
            for tt in range(NT5):
                t0 = tt * 512
                b = tt % 2
                self.fm_load(xn8[b][:], self.sc_xn1, t0, 512, [self.R("sc_xn1")],
                             [rxn8[b]], f"E1_ln{tt}")
                for jc in range(NCH):
                    pb = jc % 2
                    def mm(t, jc=jc, t0=t0, pb=pb):
                        ins = None
                        for kc in range(NCH):
                            ins = t.matmul(pp[pb][:], wsb[:, kc, jc * 128 : (jc + 1) * 128],
                                           yb[:, kc, t0 : t0 + 512],
                                           start=(kc == 0), stop=(kc == NCH - 1))
                        return ins
                    s.step("tensor", mm, reads=[rw, ryb], writes=[rpp[pb]], tag=f"E1_mm{tt}_{jc}")
                    s.step("vector", lambda e, b=b, jc=jc, pb=pb: e.tensor_tensor(
                        xa8[b][:, jc, :], xn8[b][:, jc, :], pp[pb][:], ALU.add),
                        reads=[rxn8[b], rpp[pb]], writes=[rxa8[b]], tag=f"E1_ad{tt}_{jc}")
                    s.step("scalar", lambda e, b=b, jc=jc, pb=pb: e.copy(
                        ab[pb][:], xa8[b][:, jc, :]),
                        reads=[rxa8[b]], writes=[rab[pb]], tag=f"E1_cb2{tt}_{jc}")
                    s.step("sync", lambda e, jc=jc, t0=t0, pb=pb: e.dma_start(
                        out=self.sc_q[jc * 128 : (jc + 1) * 128, t0 : t0 + 512], in_=ab[pb][:]),
                        reads=[rab[pb]], writes=[self.R("sc_q")], dma=True, tag=f"E1_sb{tt}_{jc}")
                self.fm_store(xa8[b][:], self.sc_xa, t0, 512, [rxa8[b]],
                              [self.R("sc_xa")], f"E1_sa{tt}")
            s.barrier(mark="E1")
    def _e2(self):
        s = self.sc
        with ExitStack() as ps:
            wsb = self.sbuf("E2_w", [128, NCH, 3 * D], BF16, ps)
            rw = self.R("E2_w")
            r8 = [self.sbuf(f"E2_r{i}", [128, NCH, 512], BF16, ps) for i in range(2)]
            rr8 = [self.R(f"E2_r{i}") for i in range(2)]
            ob = [self.sbuf(f"E2_o{i}", [128, 512], BF16, ps) for i in range(3)]
            rob = [self.R(f"E2_o{i}") for i in range(3)]
            pp = [self.psum(f"E2_p{i}", [128, 512], F32, ps) for i in range(3)]
            rpp = [self.R(f"E2_p{i}") for i in range(3)]
            s.step("sync", lambda e: e.dma_start(
                out=wsb[:], in_=self.pcast["qkvT"].rearrange("(c p) e -> p c e", p=128)),
                reads=[self.R("pc_qkvT")], writes=[rw], dma=True, tag="E2_lw")
            dsts = [self.sc_o, self.sc_k, self.sc_v]
            for tt in range(NT5):
                t0 = tt * 512
                b = tt % 2
                self.fm_load(r8[b][:], self.sc_q, t0, 512, [self.R("sc_q")],
                             [rr8[b]], f"E2_lr{tt}")
                for jc in range(24):
                    pb = jc % 3
                    dst = dsts[jc // 8]
                    def mm(t, jc=jc, t0=t0, pb=pb, b=b):
                        ins = None
                        for kc in range(NCH):
                            ins = t.matmul(pp[pb][:], wsb[:, kc, jc * 128 : (jc + 1) * 128],
                                           r8[b][:, kc, :],
                                           start=(kc == 0), stop=(kc == NCH - 1))
                        return ins
                    s.step("tensor", mm, reads=[rw, rr8[b]], writes=[rpp[pb]], tag=f"E2_mm{tt}_{jc}")
                    s.step("scalar", lambda e, pb=pb: e.copy(ob[pb][:], pp[pb][:]),
                           reads=[rpp[pb]], writes=[rob[pb]], tag=f"E2_ev{tt}_{jc}")
                    s.step("sync", lambda e, jc=jc, t0=t0, pb=pb, dst=dst: e.dma_start(
                        out=dst[(jc % 8) * 128 : (jc % 8 + 1) * 128, t0 : t0 + 512],
                        in_=ob[pb][:]),
                        reads=[rob[pb]], writes=[self.R(dst.tensor.name)], dma=True,
                        tag=f"E2_st{tt}_{jc}")
            s.barrier(mark="E2")
    def _e3(self):
        s = self.sc
        with ExitStack() as ps:
            jm = self.sbuf("E3_j", [128, 128], BF16, ps)
            rjm = self.R("E3_j")
            rc_sb = self.sbuf("E3_rc", [128, T], F32, ps)
            rs_sb = self.sbuf("E3_rs", [128, T], F32, ps)
            rrc, rrs = self.R("E3_rc"), self.R("E3_rs")
            qc = [self.sbuf(f"E3_q{i}", [128, T], BF16, ps) for i in range(2)]
            rqc = [self.R(f"E3_q{i}") for i in range(2)]
            t1 = [self.sbuf(f"E3_t1{i}", [128, 512], F32, ps) for i in range(2)]
            t2 = [self.sbuf(f"E3_t2{i}", [128, 512], F32, ps) for i in range(2)]
            t3 = [self.sbuf(f"E3_t3{i}", [128, 512], F32, ps) for i in range(2)]
            t4 = [self.sbuf(f"E3_t4{i}", [128, 512], F32, ps) for i in range(2)]
            rt = [[self.R(f"E3_t{j}{i}") for i in range(2)] for j in range(1, 5)]
            pj = [self.psum(f"E3_p{i}", [128, 512], F32, ps) for i in range(2)]
            rpj = [self.R(f"E3_p{i}") for i in range(2)]
            s.step("gpsimd", lambda e: e.dma_start(out=jm[:], in_=self.w_jmat),
                   writes=[rjm], dma=True, tag="E3_lj")
            s.step("sync", lambda e: e.dma_start(out=rc_sb[:], in_=self.w_ropeC),
                   writes=[rrc], dma=True, tag="E3_lc")
            s.step("sync", lambda e: e.dma_start(out=rs_sb[:], in_=self.w_ropeS),
                   writes=[rrs], dma=True, tag="E3_ls")
            for src_i, srcd in enumerate((self.sc_o, self.sc_k)):
                rsrc = self.R(srcd.tensor.name)
                for hc in range(NCH):
                    b = hc % 2
                    s.step("sync", lambda e, hc=hc, b=b, srcd=srcd: e.dma_start(
                        out=qc[b][:], in_=srcd[hc * 128 : (hc + 1) * 128, :]),
                        reads=[rsrc], writes=[rqc[b]], dma=True, tag=f"E3_l{src_i}_{hc}")
                    for tt in range(NT5):
                        t0 = tt * 512
                        pb = tt % 2
                        s.step("tensor", lambda e, b=b, t0=t0, pb=pb: e.matmul(
                            pj[pb][:], jm[:], qc[b][:, t0 : t0 + 512], start=True, stop=True),
                            reads=[rjm, rqc[b]], writes=[rpj[pb]], tag=f"E3_mm{src_i}_{hc}_{tt}")
                        s.step("vector", lambda e, b=b, t0=t0, pb=pb: e.tensor_tensor(
                            t1[pb][:], qc[b][:, t0 : t0 + 512], rc_sb[:, t0 : t0 + 512], ALU.mult),
                            reads=[rqc[b], rrc], writes=[rt[0][pb]], tag=f"E3_a{src_i}_{hc}_{tt}")
                        s.step("vector", lambda e, t0=t0, pb=pb: e.tensor_tensor(
                            t2[pb][:], pj[pb][:], rs_sb[:, t0 : t0 + 512], ALU.mult),
                            reads=[rpj[pb], rrs], writes=[rt[1][pb]], tag=f"E3_b{src_i}_{hc}_{tt}")
                        s.step("vector", lambda e, pb=pb: e.tensor_tensor(
                            t3[pb][:], t1[pb][:], t2[pb][:], ALU.add),
                            reads=[rt[0][pb], rt[1][pb]], writes=[rt[2][pb]],
                            tag=f"E3_c{src_i}_{hc}_{tt}")
                        s.step("vector", lambda e, pb=pb: e.tensor_scalar_min(
                            t4[pb][:], t3[pb][:], 0.0),
                            reads=[rt[2][pb]], writes=[rt[3][pb]], tag=f"E3_d{src_i}_{hc}_{tt}")
                        s.step("vector", lambda e, pb=pb: e.tensor_tensor(
                            t1[pb][:], t3[pb][:], t4[pb][:], ALU.subtract),
                            reads=[rt[2][pb], rt[3][pb]], writes=[rt[0][pb]],
                            tag=f"E3_e{src_i}_{hc}_{tt}")
                        s.step("scalar", lambda e, pb=pb: e.activation(
                            t2[pb][:], t4[pb][:], AF.Exp),
                            reads=[rt[3][pb]], writes=[rt[1][pb]], tag=f"E3_f{src_i}_{hc}_{tt}")
                        s.step("vector", lambda e, b=b, t0=t0, pb=pb: e.tensor_tensor(
                            qc[b][:, t0 : t0 + 512], t1[pb][:], t2[pb][:], ALU.add),
                            reads=[rt[0][pb], rt[1][pb]], writes=[rqc[b]],
                            tag=f"E3_g{src_i}_{hc}_{tt}")
                    s.step("sync", lambda e, hc=hc, b=b, srcd=srcd: e.dma_start(
                        out=srcd[hc * 128 : (hc + 1) * 128, :], in_=qc[b][:]),
                        reads=[rqc[b]], writes=[rsrc], dma=True, tag=f"E3_s{src_i}_{hc}")
            s.barrier(mark="E3")
    def _e4(self):
        s = self.sc
        with ExitStack() as ps:
            qp = self.sbuf("E4_q", [64, T], BF16, ps)
            kp = self.sbuf("E4_k", [64, T], BF16, ps)
            vp = self.sbuf("E4_v", [64, T], BF16, ps)
            vext = self.sbuf("E4_vx", [128, 32 * 65], BF16, ps)
            kT = self.sbuf("E4_kT", [128, 32 * 64], BF16, ps)
            ssnap = self.sbuf("E4_ss", [64, 65], BF16, ps)
            aT = self.sbuf("E4_aT", [128, 128], BF16, ps)
            drow = self.sbuf("E4_dr", [1, 256], F32, ps)
            onum = self.sbuf("E4_on", [64, 128], F32, ps)
            obuf = self.sbuf("E4_ob", [64, T], BF16, ps)
            rqp, rkp, rvp = self.R("E4_q"), self.R("E4_k"), self.R("E4_v")
            rvx, rkT = self.R("E4_vx"), self.R("E4_kT")
            rss, raT = self.R("E4_ss"), self.R("E4_aT")
            rdr, ron, rob = self.R("E4_dr"), self.R("E4_on"), self.R("E4_ob")
            pA = self.psum("E4_pA", [128, 128], F32, ps)
            rpA = self.R("E4_pA")
            pO = [self.psum(f"E4_pO{i}", [128, 128], F32, ps) for i in range(2)]
            rpO = [self.R(f"E4_pO{i}") for i in range(2)]
            pST = self.psum("E4_pS", [128, 65], F32, ps)
            rpST = self.R("E4_pS")
            pvt = self.psum("E4_pv", [128, 64], BF16, ps)
            rpvt = self.R("E4_pv")
            pdb = self.psum("E4_pd", [128, 128], F32, ps)
            rpdb = self.R("E4_pd")
            for hd in range(NHEAD):
                r0 = hd * 64
                for nm, buf, rr, srcd in (("q", qp, rqp, self.sc_o), ("k", kp, rkp, self.sc_k),
                                          ("v", vp, rvp, self.sc_v)):
                    s.step("sync", lambda e, buf=buf, srcd=srcd, r0=r0: e.dma_start(
                        out=buf[:], in_=srcd[r0 : r0 + 64, :]),
                        reads=[self.R(srcd.tensor.name)], writes=[rr], dma=True,
                        tag=f"E4_l{hd}_{nm}")
                s.step("gpsimd", lambda e: e.memset(vext[:], 1.0),
                       writes=[rvx], tag=f"E4_o{hd}")
                for ki in range(32):
                    s.step("tensor", lambda e, ki=ki: e.transpose(
                        pvt[:], vp[:, ki * 128 : (ki + 1) * 128], self.ident_b[0:64, 0:64]),
                        reads=[rvp, self.R("consts")], writes=[rpvt], tag=f"E4_tv{hd}_{ki}")
                    s.step("vector", lambda e, ki=ki: e.tensor_copy(
                        vext[:, ki * 65 : ki * 65 + 64], pvt[:]),
                        reads=[rpvt], writes=[rvx], tag=f"E4_cv{hd}_{ki}")
                    s.step("tensor", lambda e, ki=ki: e.transpose(
                        pvt[:], kp[:, ki * 128 : (ki + 1) * 128], self.ident_b[0:64, 0:64]),
                        reads=[rkp, self.R("consts")], writes=[rpvt], tag=f"E4_tk{hd}_{ki}")
                    s.step("vector", lambda e, ki=ki: e.tensor_copy(
                        kT[:, ki * 64 : (ki + 1) * 64], pvt[:]),
                        reads=[rpvt], writes=[rkT], tag=f"E4_ck{hd}_{ki}")
                for ci in range(32):
                    c0 = ci * 128
                    ob_ = ci % 2
                    s.step("tensor", lambda e, c0=c0: e.matmul(
                        pA[:], kp[:, c0 : c0 + 128], qp[:, c0 : c0 + 128],
                        start=True, stop=True),
                        reads=[rkp, rqp], writes=[rpA], tag=f"E4_a{hd}_{ci}")
                    s.step("vector", lambda e: e.tensor_tensor(
                        aT[:], pA[:], self.triu_b[:], ALU.mult),
                        reads=[rpA, self.R("consts")], writes=[raT], tag=f"E4_am{hd}_{ci}")
                    if ci > 0:
                        s.step("vector", lambda e: e.tensor_copy(ssnap[:], pST[0:64, 0:65]),
                               reads=[rpST], writes=[rss], tag=f"E4_sn{hd}_{ci}")
                    s.step("tensor", lambda e, ci=ci, ob_=ob_: e.matmul(
                        pO[ob_][0:65, :], vext[:, ci * 65 : (ci + 1) * 65], aT[:],
                        start=True, stop=(ci == 0)),
                        reads=[rvx, raT], writes=[rpO[ob_]], tag=f"E4_o1{hd}_{ci}")
                    if ci > 0:
                        s.step("tensor", lambda e, c0=c0, ob_=ob_: e.matmul(
                            pO[ob_][0:65, :], ssnap[:], qp[:, c0 : c0 + 128],
                            start=False, stop=True),
                            reads=[rss, rqp], writes=[rpO[ob_]], tag=f"E4_o2{hd}_{ci}")
                    s.step("tensor", lambda e, ci=ci: e.matmul(
                        pST[0:64, 0:65], kT[:, ci * 64 : (ci + 1) * 64],
                        vext[:, ci * 65 : (ci + 1) * 65],
                        start=(ci == 0), stop=(ci == 31)),
                        reads=[rkT, rvx], writes=[rpST], tag=f"E4_su{hd}_{ci}")
                    s.step("vector", lambda e, ob_=ob_: e.tensor_copy(
                        drow[:, 0:128], pO[ob_][64:65, :]),
                        reads=[rpO[ob_]], writes=[rdr], tag=f"E4_d0{hd}_{ci}")
                    s.step("vector", lambda e: e.tensor_scalar_add(
                        drow[:, 0:128], drow[:, 0:128], 1e-6),
                        reads=[rdr], writes=[rdr], tag=f"E4_d1{hd}_{ci}")
                    s.step("vector", lambda e: e.reciprocal(drow[:, 128:256], drow[:, 0:128]),
                           reads=[rdr], writes=[rdr], tag=f"E4_d2{hd}_{ci}")
                    s.step("scalar", lambda e, ob_=ob_: e.copy(onum[:], pO[ob_][0:64, :]),
                           reads=[rpO[ob_]], writes=[ron], tag=f"E4_d3{hd}_{ci}")
                    s.step("tensor", lambda e: e.matmul(
                        pdb[0:64, :], self.ones_r[0:1, 0:64], drow[:, 128:256],
                        start=True, stop=True),
                        reads=[rdr, self.R("consts")], writes=[rpdb], tag=f"E4_d4{hd}_{ci}")
                    s.step("vector", lambda e, c0=c0: e.tensor_tensor(
                        obuf[:, c0 : c0 + 128], onum[:], pdb[0:64, :], ALU.mult),
                        reads=[ron, rpdb], writes=[rob], tag=f"E4_d5{hd}_{ci}")
                s.step("sync", lambda e, r0=r0: e.dma_start(
                    out=self.sc_hn2[r0 : r0 + 64, :], in_=obuf[:]),
                    reads=[rob], writes=[self.R("sc_hn2")], dma=True, tag=f"E4_st{hd}")
            s.barrier(mark="E4")
    def _e5(self):
        s = self.sc
        with ExitStack() as ps:
            dxb = self.sbuf("E5_dx", [128, NCH, T], BF16, ps)
            rdx = self.R("E5_dx")
            ps_in = ExitStack()
            xc = [self.sbuf(f"E5_x{i}", [128, T], F32, ps_in) for i in range(2)]
            rxc = [self.R(f"E5_x{i}") for i in range(2)]
            dt_ = self.sbuf("E5_d", [128, T], F32, ps_in)
            rdt = self.R("E5_d")
            for ec in range(NCH):
                b = ec % 2
                s.step("sync", lambda e, ec=ec, b=b: e.dma_start(
                    out=xc[b][:], in_=self.sc_xa[ec * 128 : (ec + 1) * 128, :]),
                    reads=[self.R("sc_xa")], writes=[rxc[b]], dma=True, tag=f"E5_lx{ec}")
                s.step("vector", lambda e, b=b: e.tensor_tensor(
                    dt_[:, 1:T], xc[b][:, 1:T], xc[b][:, 0 : T - 1], ALU.subtract),
                    reads=[rxc[b]], writes=[rdt], tag=f"E5_df{ec}")
                s.step("vector", lambda e: e.memset(dt_[:, 0:1], 0.0),
                       writes=[rdt], tag=f"E5_d0{ec}")
                s.step("scalar", lambda e, ec=ec: e.copy(dxb[:, ec, :], dt_[:]),
                       reads=[rdt], writes=[rdx], tag=f"E5_cb{ec}")
            ps_in.close()
            s.barrier()
            wo1 = self.sbuf("E5_w1", [128, NCH, D], BF16, ps)
            wg = self.sbuf("E5_w2", [128, NCH, D], BF16, ps)
            rwo1, rwg = self.R("E5_w1"), self.R("E5_w2")
            o8 = [self.sbuf(f"E5_o{i}", [128, NCH, 512], BF16, ps) for i in range(2)]
            ro8 = [self.R(f"E5_o{i}") for i in range(2)]
            xa8 = [self.sbuf(f"E5_a{i}", [128, NCH, 512], F32, ps) for i in range(2)]
            rxa8 = [self.R(f"E5_a{i}") for i in range(2)]
            gsb = [self.sbuf(f"E5_g{i}", [128, 512], F32, ps) for i in range(2)]
            rgsb = [self.R(f"E5_g{i}") for i in range(2)]
            tb = [self.sbuf(f"E5_t{i}", [128, 512], F32, ps) for i in range(2)]
            rtb = [self.R(f"E5_t{i}") for i in range(2)]
            xg8 = [self.sbuf(f"E5_xg{i}", [128, NCH, 512], F32, ps) for i in range(2)]
            rxg8 = [self.R(f"E5_xg{i}") for i in range(2)]
            pa = [self.psum(f"E5_pa{i}", [128, 512], F32, ps) for i in range(2)]
            rpa = [self.R(f"E5_pa{i}") for i in range(2)]
            pg = [self.psum(f"E5_pg{i}", [128, 512], F32, ps) for i in range(2)]
            rpg = [self.R(f"E5_pg{i}") for i in range(2)]
            s.step("sync", lambda e: e.dma_start(
                out=wo1[:], in_=self.pcast["woT1"].rearrange("(c p) e -> p c e", p=128)),
                reads=[self.R("pc_woT1")], writes=[rwo1], dma=True, tag="E5_lw1")
            s.step("sync", lambda e: e.dma_start(
                out=wg[:], in_=self.pcast["wgT"].rearrange("(c p) e -> p c e", p=128)),
                reads=[self.R("pc_wgT")], writes=[rwg], dma=True, tag="E5_lw2")
            for tt in range(NT5):
                t0 = tt * 512
                b = tt % 2
                self.fm_load(o8[b][:], self.sc_hn2, t0, 512, [self.R("sc_hn2")],
                             [ro8[b]], f"E5_lo{tt}")
                self.fm_load(xa8[b][:], self.sc_xa, t0, 512, [self.R("sc_xa")],
                             [rxa8[b]], f"E5_la{tt}")
                for jc in range(NCH):
                    pb = jc % 2
                    def mma(t, jc=jc, t0=t0, pb=pb, b=b):
                        ins = None
                        for kc in range(NCH):
                            ins = t.matmul(pa[pb][:], wo1[:, kc, jc * 128 : (jc + 1) * 128],
                                           o8[b][:, kc, :],
                                           start=(kc == 0), stop=(kc == NCH - 1))
                        return ins
                    s.step("tensor", mma, reads=[rwo1, ro8[b]], writes=[rpa[pb]],
                           tag=f"E5_ma{tt}_{jc}")
                    def mmg(t, jc=jc, t0=t0, pb=pb):
                        ins = None
                        for kc in range(NCH):
                            ins = t.matmul(pg[pb][:], wg[:, kc, jc * 128 : (jc + 1) * 128],
                                           dxb[:, kc, t0 : t0 + 512],
                                           start=(kc == 0), stop=(kc == NCH - 1))
                        return ins
                    s.step("tensor", mmg, reads=[rwg, rdx], writes=[rpg[pb]],
                           tag=f"E5_mg{tt}_{jc}")
                    s.step("scalar", lambda e, pb=pb, jc=jc: e.activation(
                        gsb[pb][:], pg[pb][:], AF.Sigmoid, bias=self.col("l1_b_gate", jc)),
                        reads=[rpg[pb], self.R("smallv")], writes=[rgsb[pb]],
                        tag=f"E5_sg{tt}_{jc}")
                    s.step("vector", lambda e, pb=pb: e.tensor_tensor(
                        tb[pb][:], gsb[pb][:], pa[pb][:], ALU.mult),
                        reads=[rgsb[pb], rpa[pb]], writes=[rtb[pb]], tag=f"E5_gm{tt}_{jc}")
                    s.step("vector", lambda e, b=b, jc=jc, pb=pb: e.tensor_tensor(
                        xg8[b][:, jc, :], xa8[b][:, jc, :], tb[pb][:], ALU.add),
                        reads=[rxa8[b], rtb[pb]], writes=[rxg8[b]], tag=f"E5_xg{tt}_{jc}")
                self.fm_store(xg8[b][:], self.sc_xg, t0, 512, [rxg8[b]],
                              [self.R("sc_xg")], f"E5_s1{tt}")
                self.fm_store(xg8[b][:], self.sc_s1, t0, 512, [rxg8[b]],
                              [self.R("sc_s1")], f"E5_s2{tt}")
            s.barrier(mark="E5")
    def _e6(self):
        s = self.sc
        with ExitStack() as ps2:
            mg = self.sbuf("E6_mg", [128, NCH, 4], F32, ps2)
            rmg = self.R("E6_mg")
            glog = self.sbuf("E6_gl", [4, T], F32, ps2)
            rglog = self.R("E6_gl")
            psg = self.psum("E6_pg", [4, 512], F32, ps2)
            rpsg = self.R("E6_pg")
            s.step("sync", lambda e: e.dma_start(
                out=mg[:], in_=self.w_mgT.rearrange("(c p) e -> p c e", p=128)),
                writes=[rmg], dma=True, tag="E6_lg")

            def hook(kc, tt, t0, ap, res):
                s.step("tensor", lambda e, kc=kc, ap=ap: e.matmul(
                    psg[:], mg[:, kc, :], ap[:],
                    start=(kc == 0), stop=(kc == NCH - 1)),
                    reads=[res, rmg], writes=[rpsg], tag=f"E6_gm{tt}_{kc}")
                if kc == NCH - 1:
                    s.step("vector", lambda e, t0=t0: e.tensor_copy(
                        glog[:, t0 : t0 + 512], psg[:]),
                        reads=[rpsg], writes=[rglog], tag=f"E6_ge{tt}")

            self.ln_fm("E6", self.sc_xg, ("l1_ln_g", "l1_ln_b"), None,
                       store_b16=self.sc_o, f32_hook=hook)
            ltm = self.sbuf("E6_ltm", [128, 8], F32, ps2)
            rltm = self.R("E6_ltm")
            cols = self.sbuf("E6_c", [128, 4], F32, ps2)
            rc = self.R("E6_c")
            gw = self.sbuf("E6_w", [128, 4], F32, ps2)
            rgw = self.R("E6_w")
            pst = self.psum("E6_pt", [128, 8], F32, ps2)
            rpt = self.R("E6_pt")
            for g in range(NT1):
                g0 = g * 128
                s.step("tensor", lambda e, g0=g0: e.transpose(
                    pst[:, 0:4], glog[:, g0 : g0 + 128], self.ident_f[0:4, 0:4]),
                    reads=[rglog, self.R("consts")], writes=[rpt], tag=f"E6_t{g}")
                s.step("vector", lambda e: e.tensor_copy(ltm[:, 0:4], pst[:, 0:4]),
                       reads=[rpt], writes=[rltm], tag=f"E6_c0{g}")
                s.step("vector", lambda e: e.reduce_max(cols[:, 0:1], ltm[:, 0:4], AX.X, negate=True),
                       reads=[rltm], writes=[rc], tag=f"E6_m{g}")
                s.step("scalar", lambda e: e.activation(
                    gw[:], ltm[:, 0:4], AF.Exp, bias=cols[:, 0:1]),
                    reads=[rltm, rc], writes=[rgw], tag=f"E6_e{g}")
                s.step("vector", lambda e: e.reduce_sum(cols[:, 1:2], gw[:], AX.X),
                       reads=[rgw], writes=[rc], tag=f"E6_s{g}")
                s.step("vector", lambda e: e.reciprocal(cols[:, 2:3], cols[:, 1:2]),
                       reads=[rc], writes=[rc], tag=f"E6_r{g}")
                s.step("vector", lambda e: e.tensor_scalar(
                    gw[:], gw[:], cols[:, 2:3], None, ALU.mult),
                    reads=[rgw, rc], writes=[rgw], tag=f"E6_f{g}")
                s.step("gpsimd", lambda e, g0=g0: e.dma_start(
                    out=self.sc_row2.rearrange("o (e t) -> (o t) e", e=8)[g0 : g0 + 128, 0:4],
                    in_=gw[:], allow_slow_non_contiguous=True),
                    reads=[rgw], writes=[self.R("sc_row2")], dma=True, tag=f"E6_d{g}")
            s.barrier(mark="E6")
    def _e7(self):
        self._moe_generic("E7", 4, lambda e: (self.pcast["l1w1T"][e], f"pc_l1w1T_{e}"),
                          lambda e: (self.pcast["l1w2T"][e], f"pc_l1w2T_{e}"),
                          AF.Silu, lambda e: f"l1_b1_{e}", lambda e: f"l1_b2_{e}",
                          self.sc_o, self.sc_s1, cw="row", init_src=None)

    def _moe_generic(self, tag, NE, w1src, w2src, act, b1name, b2name, rhs_src, dst,
                     cw="row", init_src=None):
        s = self.sc
        if init_src is not None:
            s.step("sync", lambda e: e.dma_start(out=dst, in_=init_src),
                   reads=[self.R(init_src.tensor.name)], writes=[self.R(dst.tensor.name)],
                   dma=True, tag=f"{tag}_init")
        with ExitStack() as ps:
            w1 = self.sbuf(f"{tag}_w1", [128, NCH, DFF], BF16, ps)
            w2 = self.sbuf(f"{tag}_w2", [128, 32, D], BF16, ps)
            rw1, rw2 = self.R(f"{tag}_w1"), self.R(f"{tag}_w2")
            rhs = [self.sbuf(f"{tag}_r{i}", [128, NCH, 512], BF16, ps) for i in range(2)]
            rrhs = [self.R(f"{tag}_r{i}") for i in range(2)]
            hid = self.sbuf(f"{tag}_h", [128, 32, 512], BF16, ps)
            rhid = self.R(f"{tag}_h")
            wout = self.sbuf(f"{tag}_o", [128, NCH, 512], F32, ps)
            rwout = self.R(f"{tag}_o")
            cwB = [self.sbuf(f"{tag}_c{i}", [128, 512], F32, ps) for i in range(2)]
            rcwB = [self.R(f"{tag}_c{i}") for i in range(2)]
            cwt = [self.sbuf(f"{tag}_t{i}", [1, 512], F32, ps) for i in range(2)]
            rcwt = [self.R(f"{tag}_t{i}") for i in range(2)]
            pp = [self.psum(f"{tag}_p{i}", [128, 512], F32, ps) for i in range(3)]
            rpp = [self.R(f"{tag}_p{i}") for i in range(3)]
            pq = [self.psum(f"{tag}_q{i}", [128, 512], F32, ps) for i in range(2)]
            rpq = [self.R(f"{tag}_q{i}") for i in range(2)]
            pcb = self.psum(f"{tag}_cb", [128, 512], F32, ps)
            rpcb = self.R(f"{tag}_cb")
            rdst = self.R(dst.tensor.name)
            rsrc = self.R(rhs_src.tensor.name)

            for e in range(NE):
                w1ap, k1 = w1src(e)
                w2ap, k2 = w2src(e)
                s.step("sync", lambda g, w1ap=w1ap: g.dma_start(
                    out=w1[:], in_=w1ap.rearrange("(c p) f -> p c f", p=128)),
                    reads=[self.R(k1)], writes=[rw1], dma=True, tag=f"{tag}_lw1_{e}")
                s.step("sync", lambda g, w2ap=w2ap: g.dma_start(
                    out=w2[:], in_=w2ap.rearrange("(c p) d -> p c d", p=128)),
                    reads=[self.R(k2)], writes=[rw2], dma=True, tag=f"{tag}_lw2_{e}")
                for tt in range(NT5):
                    t0 = tt * 512
                    b = tt % 2
                    self.fm_load(rhs[b][:], rhs_src, t0, 512, [rsrc], [rrhs[b]],
                                 f"{tag}_lr{e}_{tt}")
                    if cw == "row":
                        s.step("sync", lambda en, e=e, t0=t0, b=b: en.dma_start(
                            out=cwt[b][:], in_=self.sc_row2[0:1, e * T + t0 : e * T + t0 + 512]),
                            reads=[self.R("sc_row2")], writes=[rcwt[b]], dma=True,
                            tag=f"{tag}_ct{e}_{tt}")
                        s.step("tensor", lambda en, b=b: en.matmul(
                            pcb[:], self.ones_r[:], cwt[b][:], start=True, stop=True),
                            reads=[rcwt[b], self.R("consts")], writes=[rpcb],
                            tag=f"{tag}_cb{e}_{tt}")
                        s.step("vector", lambda en, b=b: en.tensor_copy(cwB[b][:], pcb[:]),
                               reads=[rpcb], writes=[rcwB[b]], tag=f"{tag}_cc{e}_{tt}")
                    for fc in range(32):
                        pb = fc % 3
                        def mm1(t, fc=fc, b=b, pb=pb):
                            ins = None
                            for kc in range(NCH):
                                ins = t.matmul(pp[pb][:],
                                               w1[:, kc, fc * 128 : (fc + 1) * 128],
                                               rhs[b][:, kc, :],
                                               start=(kc == 0), stop=(kc == NCH - 1))
                            return ins
                        s.step("tensor", mm1, reads=[rw1, rrhs[b]], writes=[rpp[pb]],
                               tag=f"{tag}_m1{e}_{tt}_{fc}")
                        s.step("scalar", lambda en, pb=pb, fc=fc, e=e: en.activation(
                            hid[:, fc, :], pp[pb][:], act,
                            bias=self.col(b1name(e), fc)),
                            reads=[rpp[pb], self.R("smallv")], writes=[rhid],
                            tag=f"{tag}_g{e}_{tt}_{fc}")
                    for dc in range(NCH):
                        qb = dc % 2
                        def mm2(t, dc=dc, qb=qb):
                            ins = None
                            for fc in range(32):
                                ins = t.matmul(pq[qb][:],
                                               w2[:, fc, dc * 128 : (dc + 1) * 128],
                                               hid[:, fc, :],
                                               start=(fc == 0), stop=(fc == 31))
                            return ins
                        s.step("tensor", mm2, reads=[rw2, rhid], writes=[rpq[qb]],
                               tag=f"{tag}_m2{e}_{tt}_{dc}")
                        if cw == "row":
                            s.step("vector", lambda en, qb=qb, dc=dc, b=b, e=e: en.scalar_tensor_tensor(
                                wout[:, dc, :], pq[qb][:], self.col(b2name(e), dc),
                                cwB[b][:], ALU.add, ALU.mult),
                                reads=[rpq[qb], rcwB[b], self.R("smallv")], writes=[rwout],
                                tag=f"{tag}_w{e}_{tt}_{dc}")
                        else:
                            s.step("vector", lambda en, qb=qb, dc=dc, e=e: en.tensor_scalar(
                                wout[:, dc, :], pq[qb][:], self.col(b2name(e), dc),
                                None, ALU.add),
                                reads=[rpq[qb], self.R("smallv")], writes=[rwout],
                                tag=f"{tag}_w{e}_{tt}_{dc}")
                    self.fm_store(wout[:], dst, t0, 512, [rwout], [rdst],
                                  f"{tag}_st{e}_{tt}", accum=(cw != "none"))
            s.barrier(mark=f"MG_{tag}")

    def phase_F(self):
        s = self.sc
        self._moe_generic("F1", 1, lambda e: (self.pcast["fW1"], "pc_fW1"),
                          lambda e: (self.pcast["fW2"], "pc_fW2"),
                          AF.Gelu, lambda e: "f_b1", lambda e: "f_b2",
                          self.sc_xn1b, self.sc_blk, cw="none")
        with ExitStack() as ps:
            x8 = [self.sbuf(f"F2_x{i}", [128, NCH, 512], F32, ps) for i in range(2)]
            rx8 = [self.R(f"F2_x{i}") for i in range(2)]
            wrow = self.sbuf("F2_w", [1, T], F32, ps)
            rwrow = self.R("F2_w")
            psr = self.psum("F2_p", [1, 512], F32, ps)
            rpsr = self.R("F2_p")
            for tt in range(NT5):
                t0 = tt * 512
                b = tt % 2
                self.fm_load(x8[b][:], self.sc_xn1, t0, 512, [self.R("sc_xn1")],
                             [rx8[b]], f"F2_l{tt}")
                def mm(t, b=b):
                    ins = None
                    for kc in range(NCH):
                        ins = t.matmul(psr[:], self.col("f_wr", kc), x8[b][:, kc, :],
                                       start=(kc == 0), stop=(kc == NCH - 1))
                    return ins
                s.step("tensor", mm, reads=[rx8[b], self.R("smallv")], writes=[rpsr],
                       tag=f"F2_m{tt}")
                s.step("vector", lambda e, t0=t0: e.tensor_scalar(
                    wrow[:, t0 : t0 + 512], psr[:], self.col("f_b_r")[0:1], None, ALU.add),
                    reads=[rpsr, self.R("smallv")], writes=[rwrow], tag=f"F2_e{tt}")
            wB = self.sbuf("F3_wB", [128, T], F32, ps)
            rwB = self.R("F3_wB")
            wpart = self.sbuf("F3_wp", [128, 32], F32, ps)
            rwp = self.R("F3_wp")
            cnt = self.sbuf("F3_c", [128, 32], F32, ps)
            rcnt = self.R("F3_c")
            dmy = self.sbuf("F3_d", [128, T], F32, ps)
            rdmy = self.R("F3_d")
            psb = self.psum("F3_p", [128, 512], F32, ps)
            rpsb = self.R("F3_p")
            for tt in range(NT5):
                t0 = tt * 512
                s.step("tensor", lambda e, t0=t0: e.matmul(
                    psb[:], self.ones_r[:], wrow[:, t0 : t0 + 512], start=True, stop=True),
                    reads=[rwrow, self.R("consts")], writes=[rpsb], tag=f"F3_b{tt}")
                s.step("vector", lambda e, t0=t0: e.tensor_copy(wB[:, t0 : t0 + 512], psb[:]),
                       reads=[rpsb], writes=[rwB], tag=f"F3_c{tt}")
            s.step("sync", lambda e: e.dma_start(out=self.sc_row[0:1, :], in_=wrow[:]),
                   reads=[rwrow], writes=[self.R("sc_row")], dma=True, tag="F3_s0")
            s.step("sync", lambda e: e.dma_start(
                out=wpart[:], in_=self.sc_row[0:1, :].rearrange("o (p c) -> (o p) c", p=128)),
                reads=[self.R("sc_row")], writes=[rwp], dma=True, tag="F3_l0")
            for c in range(32):
                s.step("vector", lambda e, c=c: e.tensor_scalar(
                    dmy[:], wB[:], wpart[:, c : c + 1], None, ALU.is_gt),
                    reads=[rwB, rwp], writes=[rdmy], tag=f"F3_g{c}")
                s.step("vector", lambda e, c=c: e.reduce_sum(cnt[:, c : c + 1], dmy[:], AX.X),
                       reads=[rdmy], writes=[rcnt], tag=f"F3_r{c}")
            s.step("vector", lambda e: e.tensor_scalar(
                cnt[:], cnt[:], 2046.0, None, ALU.is_le),
                reads=[rcnt], writes=[rcnt], tag="F3_sel")
            s.step("sync", lambda e: e.dma_start(
                out=self.sc_row[1:2, :].rearrange("o (p c) -> (o p) c", p=128), in_=cnt[:]),
                reads=[rcnt], writes=[self.R("sc_row")], dma=True, tag="F3_s1")
            s.barrier(mark="F3")
        with ExitStack() as ps:
            b8 = [self.sbuf(f"F4_b{i}", [128, NCH, 512], F32, ps) for i in range(2)]
            n8 = [self.sbuf(f"F4_n{i}", [128, NCH, 512], F32, ps) for i in range(2)]
            s8 = [self.sbuf(f"F4_s{i}", [128, NCH, 512], F32, ps) for i in range(2)]
            xx8 = [self.sbuf(f"F4_x{i}", [128, NCH, 512], F32, ps) for i in range(2)]
            rb8 = [self.R(f"F4_b{i}") for i in range(2)]
            rn8 = [self.R(f"F4_n{i}") for i in range(2)]
            rs8 = [self.R(f"F4_s{i}") for i in range(2)]
            rxx8 = [self.R(f"F4_x{i}") for i in range(2)]
            mrow = [self.sbuf(f"F4_m{i}", [1, 512], F32, ps) for i in range(2)]
            rmrow = [self.R(f"F4_m{i}") for i in range(2)]
            t1 = [self.sbuf(f"F4_t1{i}", [128, 512], F32, ps) for i in range(2)]
            rt1 = [self.R(f"F4_t1{i}") for i in range(2)]
            t2 = [self.sbuf(f"F4_t2{i}", [128, 512], F32, ps) for i in range(2)]
            rt2 = [self.R(f"F4_t2{i}") for i in range(2)]
            of8 = [self.sbuf(f"F4_o{i}", [128, NCH, 512], F32, ps) for i in range(2)]
            rof8 = [self.R(f"F4_o{i}") for i in range(2)]
            outT = [self.sbuf(f"F4_T{i}", [128, D], F32, ps) for i in range(2)]
            routT = [self.R(f"F4_T{i}") for i in range(2)]
            pmb = self.psum("F4_pm", [128, 512], F32, ps)
            rpmb = self.R("F4_pm")
            ptp = [self.psum(f"F4_pt{i}", [128, 128], F32, ps) for i in range(2)]
            rptp = [self.R(f"F4_pt{i}") for i in range(2)]
            for tt in range(NT5):
                t0 = tt * 512
                b = tt % 2
                self.fm_load(b8[b][:], self.sc_blk, t0, 512, [self.R("sc_blk")], [rb8[b]], f"F4_lb{tt}")
                self.fm_load(n8[b][:], self.sc_xn1, t0, 512, [self.R("sc_xn1")], [rn8[b]], f"F4_ln{tt}")
                self.fm_load(s8[b][:], self.sc_s1, t0, 512, [self.R("sc_s1")], [rs8[b]], f"F4_ls{tt}")
                self.fm_load(xx8[b][:], self.sc_x1, t0, 512, [self.R("sc_x1")], [rxx8[b]], f"F4_lx{tt}")
                s.step("sync", lambda e, t0=t0, b=b: e.dma_start(
                    out=mrow[b][:], in_=self.sc_row[1:2, t0 : t0 + 512]),
                    reads=[self.R("sc_row")], writes=[rmrow[b]], dma=True, tag=f"F4_lm{tt}")
                s.step("tensor", lambda e, b=b: e.matmul(
                    pmb[:], self.ones_r[:], mrow[b][:], start=True, stop=True),
                    reads=[rmrow[b], self.R("consts")], writes=[rpmb], tag=f"F4_bm{tt}")
                for kc in range(NCH):
                    qb = kc % 2
                    s.step("vector", lambda e, b=b, kc=kc, qb=qb: e.tensor_tensor(
                        t1[qb][:], b8[b][:, kc, :], n8[b][:, kc, :], ALU.subtract),
                        reads=[rb8[b], rn8[b]], writes=[rt1[qb]], tag=f"F4_d{tt}_{kc}")
                    s.step("vector", lambda e, qb=qb: e.tensor_tensor(
                        t1[qb][:], t1[qb][:], pmb[:], ALU.mult),
                        reads=[rt1[qb], rpmb], writes=[rt1[qb]], tag=f"F4_m{tt}_{kc}")
                    s.step("vector", lambda e, b=b, kc=kc, qb=qb: e.tensor_tensor(
                        t1[qb][:], t1[qb][:], n8[b][:, kc, :], ALU.add),
                        reads=[rt1[qb], rn8[b]], writes=[rt1[qb]], tag=f"F4_f{tt}_{kc}")
                    s.step("scalar", lambda e, qb=qb: e.activation(
                        t2[qb][:], t1[qb][:], AF.Silu),
                        reads=[rt1[qb]], writes=[rt2[qb]], tag=f"F4_si{tt}_{kc}")
                    s.step("vector", lambda e, b=b, kc=kc, qb=qb: e.tensor_tensor(
                        t2[qb][:], t2[qb][:], s8[b][:, kc, :], ALU.mult),
                        reads=[rt2[qb], rs8[b]], writes=[rt2[qb]], tag=f"F4_sm{tt}_{kc}")
                    s.step("vector", lambda e, b=b, kc=kc, qb=qb: e.tensor_tensor(
                        of8[b][:, kc, :], xx8[b][:, kc, :], t2[qb][:], ALU.add),
                        reads=[rxx8[b], rt2[qb]], writes=[rof8[b]], tag=f"F4_o{tt}_{kc}")
                for g in range(4):
                    gb = g % 2
                    for kc in range(NCH):
                        pb = kc % 2
                        s.step("tensor", lambda e, b=b, kc=kc, g=g, pb=pb: e.transpose(
                            ptp[pb][:], of8[b][:, kc, g * 128 : (g + 1) * 128], self.ident_f[:]),
                            reads=[rof8[b], self.R("consts")], writes=[rptp[pb]],
                            tag=f"F4_tp{tt}_{g}_{kc}")
                        s.step("vector", lambda e, kc=kc, gb=gb, pb=pb: e.tensor_copy(
                            outT[gb][:, kc * 128 : (kc + 1) * 128], ptp[pb][:]),
                            reads=[rptp[pb]], writes=[routT[gb]], tag=f"F4_tc{tt}_{g}_{kc}")
                    s.step("sync", lambda e, t0=t0, g=g, gb=gb: e.dma_start(
                        out=self.out[t0 + g * 128 : t0 + (g + 1) * 128, :], in_=outT[gb][:]),
                        reads=[routT[gb]], writes=[self.R("out")], dma=True,
                        tag=f"F4_so{tt}_{g}")
            s.barrier(mark="F4")

    def finalize(self):
        s = self.sc
        s.barrier()
        s.step("gpsimd", lambda e: e.engine_nop(), tag="fin")
        blk = self.es.enter_context(self.nc.Block())
        s.finalize(blk)
        self.es.close()
        return self.nc

    def build(self):
        self.phase_A()
        if self.phases >= 2:
            self.phase_B()
        if self.phases >= 3:
            self.phase_C(0, self.sc_h, self.sc_moe)
        if self.phases >= 4:
            self.phase_C(1, self.sc_moe, self.sc_h)
        if self.phases >= 5:
            self.phase_D(self.sc_h)
        if self.phases >= 6:
            self.phase_E()
        if self.phases >= 7:
            self.phase_F()
        return self.finalize()



def kernel(**inputs):
    x = np.ascontiguousarray(np.asarray(inputs["L0_attn_inputs"], np.float32))
    w = host_prep(inputs)
    kb = KB()
    nc = kb.build()
    in_maps = []
    for core in range(8):
        m = {"in_" + k: v for k, v in w.items()}
        m["in_x"] = np.ascontiguousarray(x[core])
        in_maps.append(m)
    res = run_bass_kernel_spmd(nc, in_maps, core_ids=list(range(8)))
    return np.stack([r["out"] for r in res.results], axis=0)
